# version 17
# baseline (speedup 1.0000x reference)
import contextlib
import numpy as np

import concourse.bass as bass
import concourse.mybir as mybir
from concourse.bass_utils import run_bass_kernel_spmd

F32 = mybir.dt.float32
I32 = mybir.dt.int32
AF = mybir.ActivationFunctionType
ALU = mybir.AluOpType
AX = mybir.AxisListType

B, NTOK, NQ, D, H, HD, E = 4, 4096, 2048, 1024, 16, 64, 8
DC = D // 128
QKVD = 3 * D
SHIFT = 6.0


class Ctr:
    def __init__(self, ctx, nc, name):
        self.h = ctx.enter_context(nc.semaphore(name))
        self.n = 0

    def inc(self, op, k=1):
        op.then_inc(self.h, k)
        self.n += k
        return self.n


def build_nc(debug=None):
    nc = bass.Bass("TRN2", target_bir_lowering=False)

    x_kv = nc.dram_tensor("x_kv", [NTOK, D], F32, kind="ExternalInput").ap()
    x_q = nc.dram_tensor("x_q", [NQ, D], F32, kind="ExternalInput").ap()
    wqkv = nc.dram_tensor("wqkv", [DC, 128, QKVD], F32, kind="ExternalInput").ap()
    bqkv = nc.dram_tensor("bqkv", [QKVD], F32, kind="ExternalInput").ap()
    wproj = nc.dram_tensor("wproj", [DC, 128, D], F32, kind="ExternalInput").ap()
    bproj = nc.dram_tensor("bproj", [D], F32, kind="ExternalInput").ap()
    wr = nc.dram_tensor("wr", [128, DC * E], F32, kind="ExternalInput").ap()
    br = nc.dram_tensor("br", [E], F32, kind="ExternalInput").ap()
    W1 = nc.dram_tensor("W1", [E, D, D], F32, kind="ExternalInput").ap()
    W2 = nc.dram_tensor("W2", [E, D, D], F32, kind="ExternalInput").ap()
    b1 = nc.dram_tensor("b1", [E, D], F32, kind="ExternalInput").ap()
    b2 = nc.dram_tensor("b2", [E, D], F32, kind="ExternalInput").ap()
    out = nc.dram_tensor("out", [NQ, D], F32, kind="ExternalOutput").ap()

    sk = "ExternalOutput" if debug else "Internal"
    ski = "ExternalInput" if debug == "attnonly" else sk
    BFD = mybir.dt.bfloat16
    QH_s = nc.dram_tensor("QH_s", [DC, 128, NQ], BFD, kind=ski).ap()
    QL_s = nc.dram_tensor("QL_s", [DC, 128, NQ], BFD, kind=ski).ap()
    KH_s = nc.dram_tensor("KH_s", [DC, 128, NTOK], BFD, kind=ski).ap()
    KL_s = nc.dram_tensor("KL_s", [DC, 128, NTOK], BFD, kind=ski).ap()
    VH_s = nc.dram_tensor("VH_s", [NTOK, D], BFD, kind=ski).ap()
    VL_s = nc.dram_tensor("VL_s", [NTOK, D], BFD, kind=ski).ap()
    rsel = nc.dram_tensor("rsel", [1, 2], F32, kind="ExternalInput").ap()
    XNT_s = nc.dram_tensor("XNT_s", [DC, 128, NQ], F32, kind=ski).ap()
    XNH_s = nc.dram_tensor("XNH_s", [DC, 128, NTOK], mybir.dt.bfloat16, kind=ski).ap()
    XNL_s = nc.dram_tensor("XNL_s", [DC, 128, NTOK], mybir.dt.bfloat16, kind=ski).ap()
    MB_s = nc.dram_tensor("MB_s", [2, 128, E], F32, kind=sk).ap()
    AP_s = nc.dram_tensor("AP_s", [DC, 128, NQ], F32, kind=sk).ap()

    g = contextlib.ExitStack()
    with g:
        ident = g.enter_context(nc.sbuf_tensor("ident", [128, 128], F32))
        iot = g.enter_context(nc.sbuf_tensor("iot", [128, 128], I32))
        ones = g.enter_context(nc.sbuf_tensor("ones", [128, 128], F32))
        m1b = g.enter_context(nc.sbuf_tensor("m1b", [128, E], F32))
        m2b = g.enter_context(nc.sbuf_tensor("m2b", [128, E], F32))
        qb_sb = g.enter_context(nc.sbuf_tensor("qb_sb", [128, 3 * DC], F32))
        bvb = g.enter_context(nc.sbuf_tensor("bvb", [128, D], F32))
        bias_all = g.enter_context(nc.sbuf_tensor("bias_all", [128, DC], F32))
        b1sel = g.enter_context(nc.sbuf_tensor("b1sel", [128, 2 * DC], F32))

        ld = Ctr(g, nc, "ld")
        st = Ctr(g, nc, "st")
        ld_x = [Ctr(g, nc, f"ld_x{i}") for i in range(2)]
        ld_hd = [Ctr(g, nc, f"ld_hd{i}") for i in range(2)]
        ld_pw = [Ctr(g, nc, f"ld_pw{i}") for i in range(2)]
        ld_bt = Ctr(g, nc, "ld_bt")
        ld_wt = [Ctr(g, nc, f"ld_wt{i}") for i in range(3)]
        ld_xn = [Ctr(g, nc, f"ld_xn{i}") for i in range(2)]
        ld_xq = [Ctr(g, nc, f"ld_xq{i}") for i in range(2)]
        st_o = [Ctr(g, nc, f"st_o{i}") for i in range(2)]
        ld_mv = Ctr(g, nc, "ld_mv")
        st_xn = [Ctr(g, nc, f"st_xn{i}") for i in range(2)]
        ld_x2 = [Ctr(g, nc, f"ld_x2{i}") for i in range(2)]
        cv = Ctr(g, nc, "cv")
        cs = Ctr(g, nc, "cs")
        ct = Ctr(g, nc, "ct")
        cg = Ctr(g, nc, "cg")

        def phase_p0():
            with contextlib.ExitStack() as p:
                BFP = mybir.dt.bfloat16
                wq_st = p.enter_context(nc.sbuf_tensor("wq_st", [128, QKVD], F32))
                wq_hi = p.enter_context(nc.sbuf_tensor("wq_hi", [128, DC * QKVD], BFP))
                wq_lo = p.enter_context(nc.sbuf_tensor("wq_lo", [128, DC * QKVD], BFP))
                xnT_hi = [p.enter_context(nc.sbuf_tensor(f"xnT_hi{i}", [128, DC * 512], BFP))
                          for i in range(2)]
                xnT_lo = [p.enter_context(nc.sbuf_tensor(f"xnT_lo{i}", [128, DC * 512], BFP))
                          for i in range(2)]
                wr_hi = p.enter_context(nc.sbuf_tensor("wr_hi", [128, DC * E], BFP))
                wr_lo = p.enter_context(nc.sbuf_tensor("wr_lo", [128, DC * E], BFP))
                xbuf = [p.enter_context(nc.sbuf_tensor(f"xbuf{i}", [128, D], F32)) for i in range(2)]
                x2buf = p.enter_context(nc.sbuf_tensor("x2buf", [128, D], F32))
                xnbuf = [p.enter_context(nc.sbuf_tensor(f"xnbuf{i}", [128, D], F32)) for i in range(2)]
                ev = [p.enter_context(nc.sbuf_tensor(f"ev{i}", [128, 512], F32)) for i in range(2)]
                ehi = [p.enter_context(nc.sbuf_tensor(f"ehi{i}", [128, 512], BFP)) for i in range(2)]
                elo = [p.enter_context(nc.sbuf_tensor(f"elo{i}", [128, 512], BFP)) for i in range(2)]
                vout = [p.enter_context(nc.sbuf_tensor(f"vout{i}", [128, D], F32)) for i in range(2)]
                vhi = [p.enter_context(nc.sbuf_tensor(f"vhi{i}", [128, D], BFP)) for i in range(2)]
                vlo = [p.enter_context(nc.sbuf_tensor(f"vlo{i}", [128, D], BFP)) for i in range(2)]
                stat = p.enter_context(nc.sbuf_tensor("stat", [128, 16], F32))
                wr_sb = p.enter_context(nc.sbuf_tensor("wr_sb", [128, DC * E], F32))
                sc8 = p.enter_context(nc.sbuf_tensor("sc8", [1, E], F32))
                sab = p.enter_context(nc.sbuf_tensor("sab", [1, 2 * E], F32))
                rs_sb = p.enter_context(nc.sbuf_tensor("rs_sb", [1, 2], F32))
                mx = p.enter_context(nc.sbuf_tensor("mx", [1, 2], F32))
                mk12 = p.enter_context(nc.sbuf_tensor("mk12", [1, 2 * E], F32))
                bvrow = p.enter_context(nc.sbuf_tensor("bvrow", [1, D], F32))
                ps_qk = [p.enter_context(nc.psum_tensor(f"ps_qk{i}", [128, 512], F32)) for i in range(2)]
                ps_v = [p.enter_context(nc.psum_tensor(f"ps_v{i}", [128, 512], F32)) for i in range(2)]
                ps_tr = [p.enter_context(nc.psum_tensor(f"ps_tr{i}", [128, 512], F32)) for i in range(2)]
                ps_bc = p.enter_context(nc.psum_tensor("ps_bc", [128, 512], F32))

                blk = p.enter_context(nc.Block("p0"))

                @blk.sync
                def _(sync):
                    sync.dma_start(out=wq_st[:], in_=wqkv[0]).then_inc(ld.h, 16)
                    sync.dma_start(out=wr_sb[:], in_=wr).then_inc(ld.h, 16)
                    sync.dma_start(out=qb_sb[:], in_=bqkv.rearrange("(j p) -> p j", p=128),
                                   allow_slow_non_contiguous=True).then_inc(ld.h, 16)
                    sync.dma_start(out=bvrow[:],
                                   in_=bqkv[2 * D:3 * D].rearrange("(o d) -> o d", o=1)).then_inc(ld.h, 16)
                    sync.dma_start(out=sc8[:], in_=br.rearrange("(o e) -> o e", o=1)).then_inc(ld.h, 16)
                    sync.dma_start(out=rs_sb[:], in_=rsel).then_inc(ld.h, 16)
                ld.n += 16 * 6
                pre_ld = ld.n

                wsplit_marks = []
                for ci in range(DC):
                    if ci > 0:
                        @blk.sync
                        def _(sync, ci=ci):
                            sync.wait_ge(cv.h, wsplit_marks[-1])
                            sync.dma_start(out=wq_st[:], in_=wqkv[ci]).then_inc(ld.h, 16)
                        ld.n += 16
                    wld2 = ld.n

                    @blk.vector
                    def _(vector, ci=ci, wld2=wld2):
                        vector.wait_ge(ld.h, wld2)
                        vector.tensor_copy(wq_hi[:, ci * QKVD:(ci + 1) * QKVD], wq_st[:])
                        vector.drain()
                        cv.inc(vector.tensor_sub(wq_lo[:, ci * QKVD:(ci + 1) * QKVD],
                                                 wq_st[:], wq_hi[:, ci * QKVD:(ci + 1) * QKVD]))
                    wsplit_marks.append(cv.n)

                @blk.vector
                def _(vector):
                    vector.wait_ge(ld.h, pre_ld)
                    vector.tensor_copy(wr_hi[:], wr_sb[:])
                    vector.drain()
                    cv.inc(vector.tensor_sub(wr_lo[:], wr_sb[:], wr_hi[:]))
                wsplit_done = cv.n

                @blk.gpsimd
                def _(gpsimd):
                    gpsimd.iota(iot[:], [[1, 128]], base=0, channel_multiplier=-1)
                    cg.inc(gpsimd.memset(ones[:], 1.0))
                g_pre = cg.n

                @blk.vector
                def _(vector):
                    vector.wait_ge(cg.h, g_pre)
                    cv.inc(vector.tensor_scalar(out=ident[:], in0=iot[:], scalar1=0, scalar2=None,
                                                op0=ALU.is_equal))
                ident_ok = cv.n

                @blk.tensor
                def _(tensor):
                    tensor.wait_ge(ld.h, pre_ld)
                    tensor.wait_ge(cv.h, ident_ok)
                    tensor.matmul(ps_v[0][:], ones[0:1, :], bvrow[0:1, 0:512], start=True, stop=True)
                    ct.inc(tensor.matmul(ps_v[1][:], ones[0:1, :], bvrow[0:1, 512:1024],
                                         start=True, stop=True))
                bv_mm = ct.n

                @blk.vector
                def _(vector):
                    vector.wait_ge(ct.h, bv_mm)
                    vector.tensor_copy(bvb[:, 0:512], ps_v[0][:])
                    cv.inc(vector.tensor_copy(bvb[:, 512:1024], ps_v[1][:]))

                state = {"n_ln": 0, "xbuf_free": [None, None],
                         "tr_evac": [(cv.h, wsplit_done)],
                         "qk_evac": [], "v_evacs": [], "qk_sub": [], "v_sub": []}

                def ln_chunk(src_ap, j, par, xn_gate):
                    i = state["n_ln"] % 2
                    state["n_ln"] += 1

                    @blk.sync
                    def _(sync, i=i, src_ap=src_ap):
                        if state["xbuf_free"][i] is not None:
                            sem, val = state["xbuf_free"][i]
                            sync.wait_ge(sem, val)
                        sync.dma_start(out=xbuf[i][:], in_=src_ap).then_inc(ld_x[i].h, 16)
                    ld_x[i].n += 16
                    my_ld = ld_x[i].n

                    @blk.scalar
                    def _(scalar, i=i, my_ld=my_ld):
                        scalar.wait_ge(ld_x[i].h, my_ld)
                        cs.inc(scalar.activation(out=x2buf[:], in_=xbuf[i][:], func=AF.Square,
                                                 accum_out=stat[:, 4:5]))
                    sq_done = cs.n

                    @blk.vector
                    def _(vector, i=i, my_ld=my_ld, sq_done=sq_done):
                        vector.wait_ge(ld_x[i].h, my_ld)
                        vector.reduce_sum(stat[:, 0:1], xbuf[i][:], axis=AX.X)
                        vector.wait_ge(cs.h, sq_done)
                        vector.drain()
                        vector.tensor_scalar(out=stat[:, 1:2], in0=stat[:, 0:1], scalar1=1.0 / D,
                                             scalar2=None, op0=ALU.mult)
                        vector.tensor_scalar(out=stat[:, 5:6], in0=stat[:, 4:5], scalar1=1.0 / D,
                                             scalar2=None, op0=ALU.mult)
                        vector.drain()
                        vector.scalar_tensor_tensor(out=stat[:, 2:3], in0=stat[:, 1:2],
                                                    scalar=stat[:, 1:2], in1=stat[:, 5:6],
                                                    op0=ALU.mult, op1=ALU.subtract)
                        vector.drain()
                        cv.inc(vector.tensor_scalar(out=stat[:, 2:3], in0=stat[:, 2:3], scalar1=-1.0,
                                                    scalar2=1e-5, op0=ALU.mult, op1=ALU.add))
                    var_done = cv.n

                    @blk.scalar
                    def _(scalar, var_done=var_done):
                        scalar.wait_ge(cv.h, var_done)
                        cs.inc(scalar.activation(out=stat[:, 3:4], in_=stat[:, 2:3], func=AF.Sqrt))
                    sqrt_done = cs.n

                    @blk.vector
                    def _(vector, sqrt_done=sqrt_done):
                        vector.wait_ge(cs.h, sqrt_done)
                        vector.reciprocal(out=stat[:, 6:7], in_=stat[:, 3:4])
                        vector.drain()
                        cv.inc(vector.scalar_tensor_tensor(out=stat[:, 7:8], in0=stat[:, 1:2],
                                                           scalar=-1.0, in1=stat[:, 6:7],
                                                           op0=ALU.mult, op1=ALU.mult))
                    rb_done = cv.n

                    @blk.scalar
                    def _(scalar, i=i, rb_done=rb_done, te2=(state["tr_evac"][-2]
                                                            if len(state["tr_evac"]) >= 2 else None)):
                        scalar.wait_ge(cv.h, rb_done)
                        if te2 is not None:
                            scalar.wait_ge(te2[0], te2[1])
                        cs.inc(scalar.activation(out=xnbuf[i][:], in_=xbuf[i][:], func=AF.Identity,
                                                 scale=stat[:, 6:7], bias=stat[:, 7:8]))
                    ln_done = cs.n
                    state["xbuf_free"][i] = (cs.h, ln_done)

                    @blk.tensor
                    def _(tensor, i=i, ln_done=ln_done, te1=state["tr_evac"][-1]):
                        tensor.wait_ge(cs.h, ln_done)
                        tensor.wait_ge(te1[0], te1[1])
                        for half in range(2):
                            for q in range(4):
                                ci = half * 4 + q
                                op = tensor.transpose(ps_tr[half][:, q * 128:(q + 1) * 128],
                                                      xnbuf[i][:, ci * 128:(ci + 1) * 128], ident[:])
                        ct.inc(op)
                    tp_done = ct.n

                    @blk.scalar
                    def _(scalar, j=j, tp_done=tp_done, par=par, xn_gate=xn_gate):
                        scalar.wait_ge(ct.h, tp_done)
                        if j == 0 and xn_gate is not None:
                            scalar.wait_ge(xn_gate[0], xn_gate[1])
                        for half in range(2):
                            dsth = xnT_hi[par][:].rearrange("p (c t) -> p c t", c=DC)[:, half * 4:half * 4 + 4,
                                                                                     j * 128:(j + 1) * 128]
                            op = scalar.activation(out=dsth,
                                                   in_=ps_tr[half][:].rearrange("p (q t) -> p q t", q=4),
                                                   func=AF.Identity)
                        cs.inc(op)
                    hi_m = cs.n

                    @blk.vector
                    def _(vector, j=j, hi_m=hi_m, tp_done=tp_done, par=par):
                        vector.wait_ge(ct.h, tp_done)
                        vector.wait_ge(cs.h, hi_m)
                        for half in range(2):
                            dsth = xnT_hi[par][:].rearrange("p (c t) -> p c t", c=DC)[:, half * 4:half * 4 + 4,
                                                                                     j * 128:(j + 1) * 128]
                            dstl = xnT_lo[par][:].rearrange("p (c t) -> p c t", c=DC)[:, half * 4:half * 4 + 4,
                                                                                     j * 128:(j + 1) * 128]
                            op = vector.tensor_sub(dstl,
                                                   ps_tr[half][:].rearrange("p (q t) -> p q t", q=4),
                                                   dsth)
                        cv.inc(op)
                    state["tr_evac"].append((cv.h, cv.n))

                def qk_matmuls(is_q, base_tok, xnT_ready, par):
                    woff = 0 if is_q else D
                    boff = 0 if is_q else DC
                    dsth = QH_s if is_q else KH_s
                    dstl = QL_s if is_q else KL_s
                    for dout in range(DC):
                        bank = dout % 2

                        @blk.tensor
                        def _(tensor, dout=dout, bank=bank, xnT_ready=xnT_ready, woff=woff,
                              par=par):
                            tensor.wait_ge(xnT_ready[0], xnT_ready[1])
                            if len(state["qk_evac"]) >= 2:
                                tensor.wait_ge(cv.h, state["qk_evac"][-2])
                            base = woff + dout * 128
                            for ci in range(DC):
                                wsl = slice(ci * QKVD + base, ci * QKVD + base + 128)
                                xsl = slice(ci * 512, (ci + 1) * 512)
                                tensor.matmul(ps_qk[bank][:], wq_hi[:, wsl], xnT_hi[par][:, xsl],
                                              start=(ci == 0), stop=False)
                                tensor.matmul(ps_qk[bank][:], wq_hi[:, wsl], xnT_lo[par][:, xsl],
                                              start=False, stop=False)
                                op = tensor.matmul(ps_qk[bank][:], wq_lo[:, wsl], xnT_hi[par][:, xsl],
                                                   start=False, stop=(ci == DC - 1))
                            ct.inc(op)
                        mm_done = ct.n

                        @blk.vector
                        def _(vector, bank=bank, mm_done=mm_done, boff=boff, dout=dout,
                              psub=(state["qk_sub"][-2] if len(state["qk_sub"]) >= 2 else None)):
                            vector.wait_ge(ct.h, mm_done)
                            vector.wait_ge(st.h, st.n)
                            if psub is not None:
                                vector.wait_ge(cg.h, psub)
                            vector.tensor_scalar(out=ev[bank][:], in0=ps_qk[bank][:],
                                                 scalar1=qb_sb[:, boff + dout:boff + dout + 1],
                                                 scalar2=None, op0=ALU.add)
                            vector.drain()
                            cv.inc(vector.tensor_copy(ehi[bank][:], ev[bank][:]))
                        state["qk_evac"].append(cv.n)
                        evd = cv.n

                        @blk.gpsimd
                        def _(gpsimd, bank=bank, evd=evd):
                            gpsimd.wait_ge(cv.h, evd)
                            cg.inc(gpsimd.tensor_sub(elo[bank][:], ev[bank][:], ehi[bank][:]))
                        state["qk_sub"].append(cg.n)
                        subd = cg.n

                        @blk.scalar
                        def _(scalar, dout=dout, bank=bank, subd=subd, base_tok=base_tok,
                              dsth=dsth, dstl=dstl):
                            scalar.wait_ge(cg.h, subd)
                            scalar.dma_start(out=dsth[dout, :, base_tok:base_tok + 512],
                                             in_=ehi[bank][:]).then_inc(st.h, 16)
                            scalar.dma_start(out=dstl[dout, :, base_tok:base_tok + 512],
                                             in_=elo[bank][:]).then_inc(st.h, 16)
                        st.n += 32

                def v_matmuls(base_tok, xnT_ready, par):
                    for t in range(4):
                        vb = t % 2

                        @blk.tensor
                        def _(tensor, t=t, xnT_ready=xnT_ready, par=par):
                            tensor.wait_ge(xnT_ready[0], xnT_ready[1])
                            if state["v_evacs"]:
                                tensor.wait_ge(cv.h, state["v_evacs"][-1])
                            for half in range(2):
                                for ci in range(DC):
                                    xsl = slice(ci * 512 + t * 128, ci * 512 + (t + 1) * 128)
                                    wsl = slice(ci * QKVD + 2 * D + half * 512,
                                                ci * QKVD + 2 * D + (half + 1) * 512)
                                    tensor.matmul(ps_v[half][:], xnT_hi[par][:, xsl], wq_hi[:, wsl],
                                                  start=(ci == 0), stop=False)
                                    tensor.matmul(ps_v[half][:], xnT_hi[par][:, xsl], wq_lo[:, wsl],
                                                  start=False, stop=False)
                                    op = tensor.matmul(ps_v[half][:], xnT_lo[par][:, xsl], wq_hi[:, wsl],
                                                       start=False, stop=(ci == DC - 1))
                            ct.inc(op)
                        vmm = ct.n

                        @blk.vector
                        def _(vector, vb=vb, vmm=vmm,
                              psub=(state["v_sub"][-2] if len(state["v_sub"]) >= 2 else None)):
                            vector.wait_ge(ct.h, vmm)
                            vector.wait_ge(st.h, st.n)
                            if psub is not None:
                                vector.wait_ge(cg.h, psub)
                            vector.tensor_tensor(out=vout[vb][:, 0:512], in0=ps_v[0][:],
                                                 in1=bvb[:, 0:512], op=ALU.add)
                            vector.tensor_tensor(out=vout[vb][:, 512:1024], in0=ps_v[1][:],
                                                 in1=bvb[:, 512:1024], op=ALU.add)
                            vector.drain()
                            cv.inc(vector.tensor_copy(vhi[vb][:], vout[vb][:]))
                        state["v_evacs"].append(cv.n)
                        evd = cv.n

                        @blk.gpsimd
                        def _(gpsimd, vb=vb, evd=evd):
                            gpsimd.wait_ge(cv.h, evd)
                            cg.inc(gpsimd.tensor_sub(vlo[vb][:], vout[vb][:], vhi[vb][:]))
                        state["v_sub"].append(cg.n)
                        vsub = cg.n

                        @blk.sync
                        def _(sync, t=t, vb=vb, vsub=vsub, base_tok=base_tok):
                            sync.wait_ge(cg.h, vsub)
                            sync.dma_start(
                                out=VH_s[base_tok + t * 128: base_tok + (t + 1) * 128, :],
                                in_=vhi[vb][:]).then_inc(st.h, 16)
                            sync.dma_start(
                                out=VL_s[base_tok + t * 128: base_tok + (t + 1) * 128, :],
                                in_=vlo[vb][:]).then_inc(st.h, 16)
                        st.n += 32

                def router_mm(xnT_ready, part, par):
                    off = part * E

                    @blk.tensor
                    def _(tensor, xnT_ready=xnT_ready, off=off, par=par):
                        tensor.wait_ge(xnT_ready[0], xnT_ready[1])
                        for ci in range(DC):
                            xc = slice(ci * 512, ci * 512 + 1)
                            wc = slice(ci * E, (ci + 1) * E)
                            tensor.matmul(ps_bc[0:1, off:off + E], xnT_hi[par][:, xc], wr_hi[:, wc],
                                          start=(ci == 0), stop=False)
                            tensor.matmul(ps_bc[0:1, off:off + E], xnT_hi[par][:, xc], wr_lo[:, wc],
                                          start=False, stop=False)
                            op = tensor.matmul(ps_bc[0:1, off:off + E], xnT_lo[par][:, xc],
                                               wr_hi[:, wc], start=False, stop=(ci == DC - 1))
                        ct.inc(op)
                    rt = ct.n

                    @blk.vector
                    def _(vector, rt=rt, off=off):
                        vector.wait_ge(ct.h, rt)
                        cv.inc(vector.tensor_copy(sab[0:1, off:off + E], ps_bc[0:1, off:off + E]))
                    return cv.n

                def router_combine(r2_done):
                    @blk.vector
                    def _(vector, r2_done=r2_done):
                        vector.wait_ge(cv.h, r2_done)
                        vector.wait_ge(ld.h, pre_ld)
                        vector.drain()
                        vector.scalar_tensor_tensor(out=sc8[:], in0=sab[0:1, 0:E],
                                                    scalar=rs_sb[0:1, 0:1], in1=sc8[:],
                                                    op0=ALU.mult, op1=ALU.add)
                        vector.drain()
                        vector.scalar_tensor_tensor(out=sc8[:], in0=sab[0:1, E:2 * E],
                                                    scalar=rs_sb[0:1, 1:2], in1=sc8[:],
                                                    op0=ALU.mult, op1=ALU.add)
                        vector.drain()
                        vector.reduce_max(mx[0:1, 0:1], sc8[:], axis=AX.X)
                        vector.drain()
                        vector.tensor_scalar(out=mk12[0:1, 0:E], in0=sc8[:], scalar1=mx[0:1, 0:1],
                                             scalar2=None, op0=ALU.is_ge)
                        vector.drain()
                        vector.scalar_tensor_tensor(out=sc8[:], in0=mk12[0:1, 0:E], scalar=-1e30,
                                                    in1=sc8[:], op0=ALU.mult, op1=ALU.add)
                        vector.drain()
                        vector.reduce_max(mx[0:1, 1:2], sc8[:], axis=AX.X)
                        vector.drain()
                        cv.inc(vector.tensor_scalar(out=mk12[0:1, E:2 * E], in0=sc8[:],
                                                    scalar1=mx[0:1, 1:2], scalar2=None, op0=ALU.is_ge))
                    msk = cv.n

                    @blk.tensor
                    def _(tensor, msk=msk):
                        tensor.wait_ge(cv.h, msk)
                        tensor.matmul(ps_bc[:, 256:256 + E], ones[0:1, :], mk12[0:1, 0:E],
                                      start=True, stop=True)
                        ct.inc(tensor.matmul(ps_bc[:, 384:384 + E], ones[0:1, :], mk12[0:1, E:2 * E],
                                             start=True, stop=True))
                    bca = ct.n

                    @blk.vector
                    def _(vector, bca=bca):
                        vector.wait_ge(ct.h, bca)
                        vector.tensor_copy(m1b[:], ps_bc[:, 256:256 + E])
                        cv.inc(vector.tensor_copy(m2b[:], ps_bc[:, 384:384 + E]))
                    mdone = cv.n

                    if debug:
                        @blk.gpsimd
                        def _(gpsimd, mdone=mdone):
                            gpsimd.wait_ge(cv.h, mdone)
                            gpsimd.dma_start(out=MB_s[0], in_=m1b[:]).then_inc(st.h, 16)
                            gpsimd.dma_start(out=MB_s[1], in_=m2b[:]).then_inc(st.h, 16)
                        st.n += 32

                xn_st = {}
                for sc in range(8):
                    base_tok = sc * 512
                    par = sc % 2
                    xn_gate = xn_st.get(sc - 2)
                    for j in range(4):
                        ln_chunk(x_kv[base_tok + j * 128: base_tok + (j + 1) * 128, :], j,
                                 par, xn_gate if j == 0 else None)
                    xnT_ready = state["tr_evac"][-1]

                    @blk.sync
                    def _(sync, base_tok=base_tok, xnT_ready=xnT_ready, par=par):
                        sync.wait_ge(xnT_ready[0], xnT_ready[1])
                        sync.dma_start(
                            out=XNH_s[:, :, base_tok:base_tok + 512].rearrange("c p t -> p c t"),
                            in_=xnT_hi[par][:].rearrange("p (c t) -> p c t", c=DC)).then_inc(st_xn[par].h, 16)
                        sync.dma_start(
                            out=XNL_s[:, :, base_tok:base_tok + 512].rearrange("c p t -> p c t"),
                            in_=xnT_lo[par][:].rearrange("p (c t) -> p c t", c=DC)).then_inc(st_xn[par].h, 16)
                    st_xn[par].n += 32
                    xn_st[sc] = (st_xn[par].h, st_xn[par].n)

                r_parts = {}
                mm_tail = {}
                xn_final = [st_xn[0].n, st_xn[1].n]
                for sc in range(8):
                    base_tok = sc * 512
                    par = sc % 2
                    gate = mm_tail.get(sc - 2)

                    @blk.sync
                    def _(sync, base_tok=base_tok, par=par, gate=gate,
                          xs=(st_xn[par].h, xn_final[par])):
                        sync.wait_ge(xs[0], xs[1])
                        if gate is not None:
                            sync.wait_ge(ct.h, gate)
                        sync.dma_start(
                            out=xnT_hi[par][:].rearrange("p (c t) -> p c t", c=DC),
                            in_=XNH_s[:, :, base_tok:base_tok + 512].rearrange("c p t -> p c t")).then_inc(ld_x2[par].h, 16)
                        sync.dma_start(
                            out=xnT_lo[par][:].rearrange("p (c t) -> p c t", c=DC),
                            in_=XNL_s[:, :, base_tok:base_tok + 512].rearrange("c p t -> p c t")).then_inc(ld_x2[par].h, 16)
                    ld_x2[par].n += 32
                    xnT_ready = (ld_x2[par].h, ld_x2[par].n)

                    qk_matmuls(False, base_tok, xnT_ready, par)
                    v_matmuls(base_tok, xnT_ready, par)
                    if sc < 4:
                        qk_matmuls(True, base_tok, xnT_ready, par)
                    if sc == 0:
                        r_parts[0] = router_mm(xnT_ready, 0, par)
                    if sc == 4:
                        r_parts[1] = router_mm(xnT_ready, 1, par)
                        router_combine(r_parts[1])
                    mm_tail[sc] = ct.n

                @blk.gpsimd
                def _(gpsimd):
                    gpsimd.wait_ge(st.h, st.n)


        if debug == "attnonly":
            with contextlib.ExitStack() as p:
                blk = p.enter_context(nc.Block("pre"))

                @blk.gpsimd
                def _(gpsimd):
                    cg.inc(gpsimd.memset(ones[:], 1.0))
        else:
            phase_p0()

        if debug == "p0":
            return nc

        projacc = g.enter_context(nc.sbuf_tensor("projacc", [128, DC * NQ], F32))

        with contextlib.ExitStack() as p:
            BF = mybir.dt.bfloat16
            kst = [p.enter_context(nc.sbuf_tensor(f"kst{i}", [128, NTOK], BF)) for i in range(2)]
            qab = [p.enter_context(nc.sbuf_tensor(f"qab{i}", [128, NQ], BF)) for i in range(2)]
            qcd = [p.enter_context(nc.sbuf_tensor(f"qcd{i}", [128, NQ], BF)) for i in range(2)]
            vA = [p.enter_context(nc.sbuf_tensor(f"vA{i}", [128, 32 * (HD + 1)], BF)) for i in range(2)]
            vB = [p.enter_context(nc.sbuf_tensor(f"vB{i}", [128, 32 * HD], BF)) for i in range(2)]
            p32 = [p.enter_context(nc.sbuf_tensor(f"p32_{i}", [128, 1024], F32)) for i in range(2)]
            phi = [p.enter_context(nc.sbuf_tensor(f"phi{i}", [128, 1024], BF)) for i in range(4)]
            plo = [p.enter_context(nc.sbuf_tensor(f"plo{i}", [128, 1024], BF)) for i in range(4)]
            rzb = p.enter_context(nc.sbuf_tensor("rzb", [128, 1024], F32))
            t1 = p.enter_context(nc.sbuf_tensor("t1", [64, 1024], F32))
            aout = p.enter_context(nc.sbuf_tensor("aout", [64, 1024], F32))
            apair = p.enter_context(nc.sbuf_tensor("apair", [128, NQ], F32))
            aph = p.enter_context(nc.sbuf_tensor("aph", [128, NQ], BF))
            apl = p.enter_context(nc.sbuf_tensor("apl", [128, NQ], BF))
            pw_sb = [p.enter_context(nc.sbuf_tensor(f"pw_sb{i}", [128, D], F32)) for i in range(2)]
            pwh = [p.enter_context(nc.sbuf_tensor("pwh0", [128, D], BF))] * 2
            pwl = [p.enter_context(nc.sbuf_tensor("pwl0", [128, D], BF))] * 2
            nshift = p.enter_context(nc.sbuf_tensor("nshift", [128, 1], F32))
            ps_s = [p.enter_context(nc.psum_tensor(f"ps_s{i}", [128, 1024], F32)) for i in range(2)]
            ps_ab = [p.enter_context(nc.psum_tensor(f"ps_ab{i}", [65, 512], F32)) for i in range(2)]
            ps_a2 = [p.enter_context(nc.psum_tensor(f"ps_a2{i}", [64, 512], F32)) for i in range(2)]

            blk = p.enter_context(nc.Block("attn"))

            @blk.gpsimd
            def _(gpsimd):
                for i in range(2):
                    gpsimd.memset(
                        vA[i][:].rearrange("p (kc w) -> p kc w", w=HD + 1)[:, :, HD:HD + 1], 1.0)
                cg.inc(gpsimd.memset(nshift[:], -SHIFT))
            vones = cg.n

            head_ld = {}
            pw_split_by_pair = {}
            conv_done = {}
            head_free = {}
            proj_ev_by_pair = {}
            A = {"o_free": None, "proj_done": None, "prev_exp_last": None,
                 "prev_vhl_last": None, "prev_pv_last": None,
                 "apair_free": None, "apair_split": None, "aout_free": None,
                 "ps_s_free": None, "staging_free": None}

            def load_head(h):
                i = h % 2
                b64 = 64 * (h % 2)

                @blk.sync
                def _(sync, h=h, i=i, b64=b64):
                    if h >= 2:
                        sem, val = head_free[h - 2]
                        sync.wait_ge(sem, val)
                    if h == 0:
                        sync.wait_ge(cg.h, vones)
                    sync.dma_start(out=kst[i][0:64, :],
                                   in_=KH_s[h // 2, b64:b64 + 64, :]).then_inc(ld_hd[i].h, 16)
                    sync.dma_start(out=kst[i][64:128, :],
                                   in_=KL_s[h // 2, b64:b64 + 64, :]).then_inc(ld_hd[i].h, 16)
                    for half in range(2):
                        sync.dma_start(out=qab[i][64 * half:64 * half + 64, :],
                                       in_=QH_s[h // 2, b64:b64 + 64, :]).then_inc(ld_hd[i].h, 16)
                        sync.dma_start(out=qcd[i][64 * half:64 * half + 64, :],
                                       in_=QL_s[h // 2, b64:b64 + 64, :]).then_inc(ld_hd[i].h, 16)
                    for kg in range(4):
                        sync.dma_start(
                            out=vA[i][:].rearrange("p (kc w) -> p kc w", w=HD + 1)[:, kg * 8:(kg + 1) * 8, 0:HD],
                            in_=VH_s.rearrange("(kc p) c -> p kc c", p=128)[:, kg * 8:(kg + 1) * 8,
                                                                           h * HD:(h + 1) * HD]).then_inc(ld_hd[i].h, 16)
                        sync.dma_start(
                            out=vB[i][:].rearrange("p (kc w) -> p kc w", w=HD)[:, kg * 8:(kg + 1) * 8, :],
                            in_=VL_s.rearrange("(kc p) c -> p kc c", p=128)[:, kg * 8:(kg + 1) * 8,
                                                                           h * HD:(h + 1) * HD]).then_inc(ld_hd[i].h, 16)
                    if h % 2 == 0:
                        q = h // 2
                        if q >= 2:
                            sync.wait_ge(cv.h, pw_split_by_pair[q - 2])
                        sync.dma_start(out=pw_sb[q % 2][:], in_=wproj[q]).then_inc(ld_pw[q % 2].h, 16)
                ld_hd[i].n += 14 * 16
                if h % 2 == 0:
                    ld_pw[(h // 2) % 2].n += 16
                head_ld[h] = ld_hd[i].n
                conv_done[h] = None


            load_head(0)

            n_heads = 2 if debug == "attn1" else H

            def emit_pva(h, i, s_, phi_done, pva_done):
                qh, kc = divmod(s_, 32)
                pbk = s_ % 4

                @blk.tensor
                def _(tensor, i=i, s_=s_, kc=kc, pbk=pbk):
                    pd = phi_done[s_]
                    tensor.wait_ge(pd[0], pd[1])
                    if kc == 0 and A["o_free"] is not None:
                        tensor.wait_ge(cv.h, A["o_free"])
                    for qq in range(2):
                        tensor.matmul(ps_ab[qq][:],
                                      vA[i][:, kc * (HD + 1):(kc + 1) * (HD + 1)],
                                      phi[pbk][:, qq * 512:(qq + 1) * 512],
                                      start=(kc == 0), stop=False)
                    for qq in range(2):
                        op = tensor.matmul(ps_a2[qq][:],
                                           vB[i][:, kc * HD:(kc + 1) * HD],
                                           phi[pbk][:, qq * 512:(qq + 1) * 512],
                                           start=(kc == 0), stop=(kc == 31))
                    ct.inc(op)
                pva_done.append(ct.n)

            def emit_pvb(h, i, s_, plo_done, pvb_done):
                qh, kc = divmod(s_, 32)
                pbk = s_ % 4

                @blk.tensor
                def _(tensor, i=i, s_=s_, kc=kc, pbk=pbk):
                    pl = plo_done[s_]
                    tensor.wait_ge(pl[0], pl[1])
                    for qq in range(2):
                        op = tensor.matmul(ps_ab[qq][:],
                                           vA[i][:, kc * (HD + 1):(kc + 1) * (HD + 1)],
                                           plo[pbk][:, qq * 512:(qq + 1) * 512],
                                           start=False, stop=(kc == 31))
                    ct.inc(op)
                pvb_done.append(ct.n)

            def emit_finalize(h, qh, pvb_done, exp_done):
                qoff = qh * 1024
                pvl = pvb_done[qh * 32 + 31]

                @blk.vector
                def _(vector, pvl=pvl):
                    vector.wait_ge(ct.h, pvl)
                    for qq in range(2):
                        vector.reciprocal(out=rzb[64:65, qq * 512:(qq + 1) * 512],
                                          in_=ps_ab[qq][64:65, :])
                    for qq in range(2):
                        op = vector.tensor_copy(t1[:, qq * 512:(qq + 1) * 512],
                                                ps_ab[qq][0:64, :])
                    cv.inc(op)
                rz_done = cv.n

                @blk.vector
                def _(vector, rz_done=rz_done):
                    vector.drain()
                    for qq in range(2):
                        op = vector.tensor_tensor(out=t1[:, qq * 512:(qq + 1) * 512],
                                                  in0=ps_a2[qq][0:64, :],
                                                  in1=t1[:, qq * 512:(qq + 1) * 512], op=ALU.add)
                    cv.inc(op)
                t1_done = cv.n

                @blk.tensor
                def _(tensor, t1_done=t1_done):
                    tensor.wait_ge(cv.h, t1_done)
                    for qq in range(2):
                        op = tensor.matmul(ps_a2[qq][0:64, :],
                                           ones[64:65, 0:64],
                                           rzb[64:65, qq * 512:(qq + 1) * 512],
                                           start=True, stop=True)
                    ct.inc(op)
                zb_done = ct.n

                @blk.vector
                def _(vector, zb_done=zb_done, qoff=qoff, h=h, qh=qh):
                    vector.wait_ge(ct.h, zb_done)
                    if h % 2 == 0 and A["apair_split"] is not None and qh == 0:
                        vector.wait_ge(cv.h, A["apair_split"])
                    if h % 2 == 1 and A["aout_free"] is not None:
                        vector.wait_ge(ld_mv.h, A["aout_free"])
                    for qq in range(2):
                        vector.tensor_copy(rzb[0:64, qq * 512:(qq + 1) * 512],
                                           ps_a2[qq][0:64, :])
                    vector.drain()
                    for qq in range(2):
                        if h % 2 == 0:
                            dsl = apair[0:64, qoff + qq * 512: qoff + (qq + 1) * 512]
                        else:
                            dsl = aout[0:64, qq * 512:(qq + 1) * 512]
                        op = vector.tensor_tensor(
                            out=dsl,
                            in0=t1[:, qq * 512:(qq + 1) * 512],
                            in1=rzb[0:64, qq * 512:(qq + 1) * 512], op=ALU.mult)
                    cv.inc(op)
                A["o_free"] = cv.n
                if h % 2 == 1:
                    halfdone = cv.n

                    @blk.gpsimd
                    def _(gpsimd, halfdone=halfdone, qoff=qoff):
                        gpsimd.wait_ge(cv.h, halfdone)
                        ld_mv.inc(gpsimd.dma_start(out=apair[64:128, qoff:qoff + 1024],
                                                   in_=aout[0:64, :]), 16)
                    A["aout_free"] = ld_mv.n
                return cv.n

            for h in range(n_heads):
                i = h % 2
                if h + 1 < n_heads:
                    load_head(h + 1)

                exp_done = []
                phi_done = []
                plo_done = []
                pva_done = []
                pvb_done = []
                pv_done = None
                for qh in range(2):
                    qoff = qh * 1024
                    run = cs.n
                    for kc in range(32):
                        run += 2 if (qh * 32 + kc) % 2 == 0 else 1
                        exp_done.append(run)
                    scalar_recs = []
                    for kc in range(32):
                        s_ = qh * 32 + kc
                        pbk = s_ % 2
                        pb4 = s_ % 4

                        @blk.tensor
                        def _(tensor, kc=kc, s_=s_, pbk=pbk, i=i, qoff=qoff, h=h):
                            if s_ == 0:
                                tensor.wait_ge(ld_hd[h % 2].h, head_ld[h])
                                if A["ps_s_free"] is not None:
                                    tensor.wait_ge(cv.h, A["ps_s_free"])
                            if s_ >= 2:
                                tensor.wait_ge(cs.h, exp_done[s_ - 2])
                            elif A["prev_exp_last"] is not None:
                                tensor.wait_ge(cs.h, A["prev_exp_last"])
                            for qq in range(2):
                                tensor.matmul(ps_s[pbk][:, qq * 512:(qq + 1) * 512],
                                              kst[i][:, kc * 128:(kc + 1) * 128],
                                              qab[i][:, qoff + qq * 512: qoff + (qq + 1) * 512],
                                              start=True, stop=False)
                            for qq in range(2):
                                op = tensor.matmul(ps_s[pbk][:, qq * 512:(qq + 1) * 512],
                                                   kst[i][:, kc * 128:(kc + 1) * 128],
                                                   qcd[i][:, qoff + qq * 512: qoff + (qq + 1) * 512],
                                                   start=False, stop=True)
                            ct.inc(op)
                        p32_wait = None
                        if s_ >= 2:
                            p32_wait = plo_done[s_ - 2]
                        elif A["prev_vhl_last"] is not None:
                            p32_wait = A["prev_vhl_last"]
                        phibuf_wait = None
                        if s_ >= 4:
                            phibuf_wait = pva_done[s_ - 4]
                        elif A["prev_pv_last"] is not None:
                            phibuf_wait = A["prev_pv_last"]
                        even = (s_ % 2 == 0)
                        scalar_recs.append((ct.n, p32_wait, pbk, pb4, even, phibuf_wait))

                        if even:
                            phi_done.append((cs.h, exp_done[s_] - 1))

                            @blk.vector
                            def _(vector, s_=s_, pb4=pb4, pbk=pbk):
                                vector.wait_ge(cs.h, exp_done[s_])
                                if s_ >= 4:
                                    vector.wait_ge(ct.h, pvb_done[s_ - 4])
                                cv.inc(vector.tensor_sub(plo[pb4][:], p32[pbk][:],
                                                         phi[pb4][:]))
                            plo_done.append((cv.h, cv.n))
                        else:
                            @blk.vector
                            def _(vector, s_=s_, pb4=pb4, pbk=pbk):
                                vector.wait_ge(cs.h, exp_done[s_])
                                if s_ >= 4:
                                    vector.wait_ge(ct.h, pva_done[s_ - 4])
                                elif A["prev_pv_last"] is not None:
                                    vector.wait_ge(ct.h, A["prev_pv_last"])
                                cv.inc(vector.tensor_copy(phi[pb4][:], p32[pbk][:]))
                            phi_done.append((cv.h, cv.n))

                            @blk.gpsimd
                            def _(gpsimd, s_=s_, pb4=pb4, pbk=pbk, pd=phi_done[-1]):
                                gpsimd.wait_ge(pd[0], pd[1])
                                if s_ >= 4:
                                    gpsimd.wait_ge(ct.h, pvb_done[s_ - 4])
                                cg.inc(gpsimd.tensor_sub(plo[pb4][:], p32[pbk][:],
                                                         phi[pb4][:]))
                            plo_done.append((cg.h, cg.n))

                        if kc >= 1:
                            emit_pva(h, i, s_ - 1, phi_done, pva_done)
                        if kc >= 3:
                            emit_pvb(h, i, s_ - 3, plo_done, pvb_done)

                    emit_pva(h, i, qh * 32 + 31, phi_done, pva_done)
                    for sx in (29, 30, 31):
                        emit_pvb(h, i, qh * 32 + sx, plo_done, pvb_done)

                    @blk.scalar
                    def _(scalar, recs=tuple(scalar_recs)):
                        for (ctv, p32w, pbk2, pb42, ev, phw) in recs:
                            scalar.wait_ge(ct.h, ctv)
                            if ev:
                                if phw is not None:
                                    scalar.wait_ge(ct.h, phw)
                                op = scalar.activation(out=phi[pb42][:], in_=ps_s[pbk2][:],
                                                       func=AF.Exp, scale=0.125,
                                                       bias=nshift[:])
                                op.then_inc(cs.h, 1)
                                cs.n += 1
                            if p32w is not None:
                                scalar.wait_ge(p32w[0], p32w[1])
                            op = scalar.activation(out=p32[pbk2][:], in_=ps_s[pbk2][:],
                                                   func=AF.Exp, scale=0.125, bias=nshift[:])
                            op.then_inc(cs.h, 1)
                            cs.n += 1
                    assert cs.n == exp_done[qh * 32 + 31], (cs.n, exp_done[qh * 32 + 31])
                    emit_finalize(h, qh, pvb_done, exp_done)

                scaled = A["o_free"]

                head_free[h] = (ct.h, pvb_done[-1])
                A["prev_exp_last"] = exp_done[-1]
                A["prev_vhl_last"] = plo_done[-1]
                A["prev_pv_last"] = pvb_done[-1]

                if h % 2 == 1:
                    mv_done = ld_mv.n
                    pair = h // 2

                    @blk.vector
                    def _(vector, mv_done=mv_done, pair=pair, h=h):
                        vector.wait_ge(ld_mv.h, mv_done)
                        if pair >= 1:
                            vector.wait_ge(ct.h, A["apair_free"])
                        q2 = (h // 2) % 2
                        vector.wait_ge(ld_pw[q2].h, ld_pw[q2].n)
                        vector.tensor_copy(pwh[q2][:], pw_sb[q2][:])
                        vector.tensor_copy(aph[:], apair[:])
                        vector.drain()
                        vector.tensor_sub(pwl[q2][:], pw_sb[q2][:], pwh[q2][:])
                        cv.inc(vector.tensor_sub(apl[:], apair[:], aph[:]))
                    ap_split = cv.n
                    pw_split_by_pair[pair] = cv.n

                    pr_mm = []
                    pr_ev = []
                    for idx in range(32):
                        dout, qq = divmod(idx, 4)
                        bank = ps_s[(idx % 4) // 2]
                        slc = (idx % 2) * 512

                        @blk.tensor
                        def _(tensor, idx=idx, dout=dout, qq=qq, bank=bank, slc=slc,
                              ap_split=ap_split, h=h):
                            if idx == 0:
                                tensor.wait_ge(cv.h, ap_split)
                                tensor.wait_ge(cv.h, pw_split_by_pair[h // 2])
                            if idx >= 4:
                                tensor.wait_ge(cv.h, pr_ev[idx - 4])
                            q2 = (h // 2) % 2
                            wsl = slice(dout * 128, (dout + 1) * 128)
                            asl = slice(qq * 512, (qq + 1) * 512)
                            tensor.matmul(bank[:, slc:slc + 512], pwh[q2][:, wsl], aph[:, asl],
                                          start=True, stop=False)
                            tensor.matmul(bank[:, slc:slc + 512], pwh[q2][:, wsl], apl[:, asl],
                                          start=False, stop=False)
                            ct.inc(tensor.matmul(bank[:, slc:slc + 512], pwl[q2][:, wsl],
                                                 aph[:, asl], start=False, stop=True))
                        pr_mm.append(ct.n)

                        @blk.vector
                        def _(vector, idx=idx, dout=dout, qq=qq, bank=bank, slc=slc, pair=pair):
                            vector.wait_ge(ct.h, pr_mm[idx])
                            dst2 = projacc[:, dout * NQ + qq * 512: dout * NQ + (qq + 1) * 512]
                            if pair == 0:
                                cv.inc(vector.tensor_copy(dst2, bank[:, slc:slc + 512]))
                            else:
                                cv.inc(vector.tensor_tensor(out=dst2, in0=bank[:, slc:slc + 512],
                                                            in1=dst2, op=ALU.add))
                        pr_ev.append(cv.n)

                    proj_ev_by_pair[pair] = pr_ev[-1]
                    A["proj_done"] = pr_ev[-1]
                    A["ps_s_free"] = pr_ev[-1]
                    A["apair_free"] = pr_mm[-1]
                    A["apair_split"] = ap_split

            if debug in ("attn", "attn1", "attnonly"):
                final_proj = A["proj_done"]

                @blk.gpsimd
                def _(gpsimd):
                    gpsimd.wait_ge(cv.h, final_proj)
                    for ci in range(DC):
                        gpsimd.dma_start(out=AP_s[ci],
                                         in_=projacc[:, ci * NQ:(ci + 1) * NQ]).then_inc(st.h, 16)
                    st.n += 16 * DC
                    gpsimd.wait_ge(st.h, st.n)

        if debug in ("attn", "attn1", "attnonly"):
            return nc

        with contextlib.ExitStack() as p:
            BFF = mybir.dt.bfloat16
            wsel = p.enter_context(nc.sbuf_tensor("wsel", [128, D], F32))
            w1h = p.enter_context(nc.sbuf_tensor("w1h", [128, DC * D], BFF))
            w1l = p.enter_context(nc.sbuf_tensor("w1l", [128, DC * D], BFF))
            w2h = p.enter_context(nc.sbuf_tensor("w2h", [128, DC * D], BFF))
            w2l = p.enter_context(nc.sbuf_tensor("w2l", [128, DC * D], BFF))
            hbuf = p.enter_context(nc.sbuf_tensor("hbuf", [128, 4 * 512], F32))
            hh = [p.enter_context(nc.sbuf_tensor(f"hh{i}", [128, DC * 512], BFF)) for i in range(2)]
            hl = [p.enter_context(nc.sbuf_tensor(f"hl{i}", [128, DC * 512], BFF)) for i in range(2)]
            wtile = [p.enter_context(nc.sbuf_tensor(f"wtile{i}", [128, D], F32)) for i in range(2)]
            xnh8 = p.enter_context(nc.sbuf_tensor("xnh8", [128, DC * 512], BFF))
            xnl8 = p.enter_context(nc.sbuf_tensor("xnl8", [128, DC * 512], BFF))
            btile = p.enter_context(nc.sbuf_tensor("btile", [128, E], F32))
            b2s = p.enter_context(nc.sbuf_tensor("b2s", [128, 2 * E], F32))
            pb_t = p.enter_context(nc.sbuf_tensor("pb_t", [128, DC], F32))
            ps_f = [p.enter_context(nc.psum_tensor(f"ps_f{i}", [128, 512], F32)) for i in range(DC)]

            blk = p.enter_context(nc.Block("ffn"))

            @blk.sync
            def _(sync):
                sync.dma_start(out=pb_t[:], in_=bproj.rearrange("(j p) -> p j", p=128),
                               allow_slow_non_contiguous=True).then_inc(ld_bt.h, 16)
            ld_bt.n += 16
            pb_ld = ld_bt.n

            for bi, (bsrc, bdst) in enumerate(((b1, b1sel), (b2, b2s))):
                for e in range(E):
                    @blk.sync
                    def _(sync, bsrc=bsrc, e=e):
                        sync.wait_ge(cv.h, cv.n)
                        sync.dma_start(out=btile[:],
                                       in_=bsrc[e].rearrange("(j p) -> p j", p=128),
                                       allow_slow_non_contiguous=True).then_inc(ld_bt.h, 16)
                    ld_bt.n += 16
                    bld = ld_bt.n

                    @blk.vector
                    def _(vector, bdst=bdst, e=e, bld=bld):
                        vector.wait_ge(ld_bt.h, bld)
                        for k, msk in ((0, m1b), (1, m2b)):
                            dst2 = bdst[:, k * E:(k + 1) * E]
                            if e == 0:
                                op = vector.tensor_scalar(out=dst2, in0=btile[:],
                                                          scalar1=msk[:, e:e + 1], scalar2=None,
                                                          op0=ALU.mult)
                            else:
                                op = vector.scalar_tensor_tensor(out=dst2, in0=btile[:],
                                                                 scalar=msk[:, e:e + 1], in1=dst2,
                                                                 op0=ALU.mult, op1=ALU.add)
                        vector.drain()
                        cv.inc(op)

            @blk.vector
            def _(vector, full=None):
                vector.wait_ge(ld_bt.h, ld_bt.n)
                vector.drain()
                vector.tensor_tensor(out=bias_all[:], in0=b2s[:, 0:E], in1=b2s[:, E:2 * E],
                                     op=ALU.add)
                vector.drain()
                cv.inc(vector.scalar_tensor_tensor(out=bias_all[:], in0=bias_all[:], scalar=0.5,
                                                   in1=pb_t[:], op0=ALU.mult, op1=ALU.add))

            wsel_marks = []
            sel_ci = {}
            wfree = {"w1": None, "w2": None}
            R = {"gelu_last": None,
                 "oev_last": None,
                 "omm_par": [None, None],
                 "hbuf_free": None,
                 "xn_free": None}

            def emit_h(k, t, w, my_xld):
                pg = R["gelu_last"]

                @blk.tensor
                def _(tensor, k=k, t=t, w=w, my_xld=my_xld, pg=pg):
                    tensor.wait_ge(ld_xn[0].h, my_xld)
                    if pg is not None:
                        tensor.wait_ge(cs.h, pg)
                    for ci in range(DC):
                        if t == 0 and w == 0:
                            tensor.wait_ge(cg.h, sel_ci[(k, "w1", ci)])
                        xsl = slice(ci * 512, (ci + 1) * 512)
                        for dout in range(4 * w, 4 * w + 4):
                            wsl = slice(ci * D + dout * 128, ci * D + (dout + 1) * 128)
                            tensor.matmul(ps_f[dout % 4][:], w1h[:, wsl], xnh8[:, xsl],
                                          start=(ci == 0), stop=False)
                            tensor.matmul(ps_f[dout % 4][:], w1h[:, wsl], xnl8[:, xsl],
                                          start=False, stop=False)
                            op = tensor.matmul(ps_f[dout % 4][:], w1l[:, wsl], xnh8[:, xsl],
                                               start=False, stop=(ci == DC - 1))
                    ct.inc(op)
                return ct.n

            def emit_gelu(k, t, w, hmmv):
                hhfree = R["omm_par"][t % 2]
                hbfree = R["hbuf_free"]

                @blk.scalar
                def _(scalar, k=k, t=t, w=w, hmmv=hmmv, hhfree=hhfree, hbfree=hbfree):
                    scalar.wait_ge(ct.h, hmmv)
                    if hhfree is not None:
                        scalar.wait_ge(ct.h, hhfree)
                    if hbfree is not None:
                        scalar.wait_ge(cv.h, hbfree)
                    for dj, dout in enumerate(range(4 * w, 4 * w + 4)):
                        b = b1sel[:, k * E + dout:k * E + dout + 1]
                        scalar.activation(out=hh[t % 2][:, dout * 512:(dout + 1) * 512],
                                          in_=ps_f[dout % 4][:], func=AF.Gelu, bias=b)
                        op = scalar.activation(out=hbuf[:, dj * 512:(dj + 1) * 512],
                                               in_=ps_f[dout % 4][:], func=AF.Gelu, bias=b)
                    cs.inc(op)
                R["gelu_last"] = cs.n
                return cs.n

            def emit_sub(k, t, w, gv):
                hlfree = R["omm_par"][t % 2]

                @blk.vector
                def _(vector, t=t, w=w, gv=gv, hlfree=hlfree):
                    vector.wait_ge(cs.h, gv)
                    if hlfree is not None:
                        vector.wait_ge(ct.h, hlfree)
                    for dj, dout in enumerate(range(4 * w, 4 * w + 4)):
                        op = vector.tensor_sub(hl[t % 2][:, dout * 512:(dout + 1) * 512],
                                               hbuf[:, dj * 512:(dj + 1) * 512],
                                               hh[t % 2][:, dout * 512:(dout + 1) * 512])
                    cv.inc(op)
                R["hbuf_free"] = cv.n
                return cv.n

            def emit_o(k, t, w, subv):
                oevfree = R["oev_last"]

                @blk.tensor
                def _(tensor, k=k, t=t, w=w, subv=subv, oevfree=oevfree):
                    tensor.wait_ge(cv.h, subv)
                    if oevfree is not None:
                        tensor.wait_ge(cv.h, oevfree)
                    for ci2 in range(DC):
                        if t == 0 and w == 0:
                            tensor.wait_ge(cg.h, sel_ci[(k, "w2", ci2)])
                        hsl = slice(ci2 * 512, (ci2 + 1) * 512)
                        for dout in range(4 * w, 4 * w + 4):
                            wsl = slice(ci2 * D + dout * 128, ci2 * D + (dout + 1) * 128)
                            tensor.matmul(ps_f[4 + dout % 4][:], w2h[:, wsl],
                                          hh[t % 2][:, hsl], start=(ci2 == 0), stop=False)
                            tensor.matmul(ps_f[4 + dout % 4][:], w2h[:, wsl],
                                          hl[t % 2][:, hsl], start=False, stop=False)
                            op = tensor.matmul(ps_f[4 + dout % 4][:], w2l[:, wsl],
                                               hh[t % 2][:, hsl], start=False,
                                               stop=(ci2 == DC - 1))
                    ct.inc(op)
                R["omm_par"][t % 2] = ct.n
                return ct.n

            def emit_oev(k, t, w, ommv):
                @blk.vector
                def _(vector, t=t, w=w, ommv=ommv):
                    vector.wait_ge(ct.h, ommv)
                    for dout in range(4 * w, 4 * w + 4):
                        dst2 = projacc[:, dout * NQ + t * 512: dout * NQ + (t + 1) * 512]
                        op = vector.scalar_tensor_tensor(out=dst2, in0=ps_f[4 + dout % 4][:],
                                                         scalar=0.5, in1=dst2,
                                                         op0=ALU.mult, op1=ALU.add)
                    cv.inc(op)
                R["oev_last"] = cv.n
                return cv.n

            for k in range(2):
                msk = m1b if k == 0 else m2b
                for wsrc, wdh, wdl, tag in ((W1, w1h, w1l, "w1"), (W2, w2h, w2l, "w2")):
                    for ci in range(DC):
                        for e in range(E):
                            tidx = len(wsel_marks) % 2

                            @blk.sync
                            def _(sync, wsrc=wsrc, e=e, ci=ci, tidx=tidx):
                                if len(wsel_marks) >= 2:
                                    sync.wait_ge(cv.h, wsel_marks[-2])
                                sync.dma_start(out=wtile[tidx][:],
                                               in_=wsrc[e, ci * 128:(ci + 1) * 128, :]).then_inc(ld_wt[tidx].h, 16)
                            ld_wt[tidx].n += 16
                            wld = ld_wt[tidx].n
                            prev_sub = max(sel_ci.values()) if sel_ci else None

                            @blk.vector
                            def _(vector, wdh=wdh, e=e, ci=ci, tidx=tidx, wld=wld,
                                  msk=msk, tag=tag, prev_sub=prev_sub):
                                vector.wait_ge(ld_wt[tidx].h, wld)
                                if e == 0 and prev_sub is not None:
                                    vector.wait_ge(cg.h, prev_sub)
                                vector.drain()
                                if e == 0:
                                    op = vector.tensor_scalar(out=wsel[:], in0=wtile[tidx][:],
                                                              scalar1=msk[:, e:e + 1],
                                                              scalar2=None, op0=ALU.mult)
                                else:
                                    op = vector.scalar_tensor_tensor(out=wsel[:], in0=wtile[tidx][:],
                                                                     scalar=msk[:, e:e + 1],
                                                                     in1=wsel[:], op0=ALU.mult,
                                                                     op1=ALU.add)
                                if e == E - 1:
                                    vector.drain()
                                    if ci == 0 and wfree[tag] is not None:
                                        vector.wait_ge(ct.h, wfree[tag])
                                    op = vector.tensor_copy(wdh[:, ci * D:(ci + 1) * D],
                                                            wsel[:])
                                cv.inc(op)
                            wsel_marks.append(cv.n)

                            if e == E - 1:
                                hi_done = cv.n

                                @blk.gpsimd
                                def _(gpsimd, ci=ci, hi_done=hi_done, wdh=wdh, wdl=wdl):
                                    gpsimd.wait_ge(cv.h, hi_done)
                                    cg.inc(gpsimd.tensor_sub(wdl[:, ci * D:(ci + 1) * D],
                                                             wsel[:],
                                                             wdh[:, ci * D:(ci + 1) * D]))
                                sel_ci[(k, tag, ci)] = cg.n

                hmmd = {}
                subd = {}
                ommd = {}
                for t in range(4):
                    xf = R["xn_free"]

                    @blk.scalar
                    def _(scalar, t=t, xf=xf):
                        if xf is not None:
                            scalar.wait_ge(ct.h, xf)
                        for ci in range(DC):
                            scalar.dma_start(out=xnh8[:, ci * 512:(ci + 1) * 512],
                                             in_=XNH_s[ci, :, t * 512:(t + 1) * 512]).then_inc(ld_xn[0].h, 16)
                            scalar.dma_start(out=xnl8[:, ci * 512:(ci + 1) * 512],
                                             in_=XNL_s[ci, :, t * 512:(t + 1) * 512]).then_inc(ld_xn[0].h, 16)
                    ld_xn[0].n += 256
                    my_xld = ld_xn[0].n

                    hmmd[(t, 0)] = emit_h(k, t, 0, my_xld)
                    if t >= 1:
                        ommd[(t - 1, 0)] = emit_o(k, t - 1, 0, subd[(t - 1, 1)])
                    gv0 = emit_gelu(k, t, 0, hmmd[(t, 0)])
                    subd[(t, 0)] = emit_sub(k, t, 0, gv0)
                    hmmd[(t, 1)] = emit_h(k, t, 1, my_xld)
                    R["xn_free"] = hmmd[(t, 1)]
                    if t >= 1:
                        emit_oev(k, t - 1, 0, ommd[(t - 1, 0)])
                        ommd[(t - 1, 1)] = emit_o(k, t - 1, 1, subd[(t - 1, 1)])
                        emit_oev(k, t - 1, 1, ommd[(t - 1, 1)])
                    gv1 = emit_gelu(k, t, 1, hmmd[(t, 1)])
                    subd[(t, 1)] = emit_sub(k, t, 1, gv1)
                ommd[(3, 0)] = emit_o(k, 3, 0, subd[(3, 1)])
                emit_oev(k, 3, 0, ommd[(3, 0)])
                ommd[(3, 1)] = emit_o(k, 3, 1, subd[(3, 1)])
                emit_oev(k, 3, 1, ommd[(3, 1)])
                wfree["w1"] = hmmd[(3, 1)]
                wfree["w2"] = ommd[(3, 1)]

        with contextlib.ExitStack() as p:
            xqb = [p.enter_context(nc.sbuf_tensor(f"xqb{i}", [128, D], F32)) for i in range(2)]
            outb = [p.enter_context(nc.sbuf_tensor(f"outb{i}", [128, D], F32)) for i in range(2)]
            ps_c = [p.enter_context(nc.psum_tensor(f"ps_c{i}", [128, 512], F32)) for i in range(2)]

            blk = p.enter_context(nc.Block("fin"))

            @blk.vector
            def _(vector):
                for dout in range(DC):
                    op = vector.tensor_scalar(out=projacc[:, dout * NQ:(dout + 1) * NQ],
                                              in0=projacc[:, dout * NQ:(dout + 1) * NQ],
                                              scalar1=bias_all[:, dout:dout + 1], scalar2=None,
                                              op0=ALU.add)
                cv.inc(op)
            bias_done = cv.n

            ev_marks = []
            st_marks = []
            for tc in range(16):
                i = tc % 2

                @blk.sync
                def _(sync, tc=tc, i=i):
                    if tc >= 2:
                        sync.wait_ge(cv.h, ev_marks[tc - 2])
                    sync.dma_start(out=xqb[i][:],
                                   in_=x_q[tc * 128:(tc + 1) * 128, :]).then_inc(ld_xq[i].h, 16)
                ld_xq[i].n += 16
                xld = ld_xq[i].n

                @blk.tensor
                def _(tensor, tc=tc, bias_done=bias_done):
                    if tc == 0:
                        tensor.wait_ge(cv.h, bias_done)
                    if tc >= 1:
                        tensor.wait_ge(cv.h, ev_marks[tc - 1])
                    for half in range(2):
                        for q in range(4):
                            ci = half * 4 + q
                            op = tensor.transpose(ps_c[half][:, q * 128:(q + 1) * 128],
                                                  projacc[:, ci * NQ + tc * 128: ci * NQ + (tc + 1) * 128],
                                                  ident[:])
                    ct.inc(op)
                tdn = ct.n

                @blk.vector
                def _(vector, tc=tc, i=i, xld=xld, tdn=tdn):
                    vector.wait_ge(ct.h, tdn)
                    vector.wait_ge(ld_xq[i].h, xld)
                    if tc >= 2:
                        vector.wait_ge(st_o[i].h, st_marks[tc - 2])
                    for half in range(2):
                        op = vector.tensor_tensor(out=outb[i][:, half * 512:(half + 1) * 512],
                                                  in0=ps_c[half][:],
                                                  in1=xqb[i][:, half * 512:(half + 1) * 512],
                                                  op=ALU.add)
                    cv.inc(op)
                ev_marks.append(cv.n)

                @blk.gpsimd
                def _(gpsimd, tc=tc, i=i, evm=ev_marks[-1]):
                    gpsimd.wait_ge(cv.h, evm)
                    gpsimd.dma_start(out=out[tc * 128:(tc + 1) * 128, :],
                                     in_=outb[i][:]).then_inc(st_o[i].h, 16)
                st_o[i].n += 16
                st_marks.append(st_o[i].n)

            @blk.gpsimd
            def _(gpsimd):
                gpsimd.wait_ge(st_o[0].h, st_o[0].n)
                gpsimd.wait_ge(st_o[1].h, st_o[1].n)

    return nc


_NC_CACHE = {}


def kernel(L0_attn_x, L0_attn_qkv_w, L0_attn_qkv_b, L0_attn_proj_w, L0_attn_proj_b,
           L0_ffn_Wr, L0_ffn_br, L0_ffn_W1, L0_ffn_b1, L0_ffn_W2, L0_ffn_b2):
    x = np.ascontiguousarray(np.asarray(L0_attn_x, dtype=np.float32))
    qkvT = np.ascontiguousarray(np.asarray(L0_attn_qkv_w, dtype=np.float32).T).reshape(DC, 128, QKVD)
    projT = np.ascontiguousarray(np.asarray(L0_attn_proj_w, dtype=np.float32).T).reshape(DC, 128, D)
    wr_t = np.ascontiguousarray(np.asarray(L0_ffn_Wr, dtype=np.float32).reshape(DC, 128, E).transpose(1, 0, 2).reshape(128, DC * E))
    bqkv = np.ascontiguousarray(np.asarray(L0_attn_qkv_b, dtype=np.float32))
    bproj = np.ascontiguousarray(np.asarray(L0_attn_proj_b, dtype=np.float32))
    br = np.ascontiguousarray(np.asarray(L0_ffn_br, dtype=np.float32))
    W1 = np.ascontiguousarray(np.asarray(L0_ffn_W1, dtype=np.float32))
    W2 = np.ascontiguousarray(np.asarray(L0_ffn_W2, dtype=np.float32))
    b1 = np.ascontiguousarray(np.asarray(L0_ffn_b1, dtype=np.float32))
    b2 = np.ascontiguousarray(np.asarray(L0_ffn_b2, dtype=np.float32))

    if "nc" not in _NC_CACHE:
        _NC_CACHE["nc"] = build_nc()
    nc = _NC_CACHE["nc"]

    in_maps = []
    for c in range(8):
        b, qh = divmod(c, 2)
        xp = np.ascontiguousarray(
            np.concatenate([x[b, qh * NQ:(qh + 1) * NQ],
                            x[b, (1 - qh) * NQ:(2 - qh) * NQ]], axis=0))
        in_maps.append({
            "x_kv": xp,
            "x_q": np.ascontiguousarray(x[b, qh * NQ:(qh + 1) * NQ]),
            "rsel": np.array([[1.0 - qh, float(qh)]], dtype=np.float32),
            "wqkv": qkvT, "bqkv": bqkv, "wproj": projT, "bproj": bproj,
            "wr": wr_t, "br": br, "W1": W1, "W2": W2, "b1": b1, "b2": b2,
        })
    res = run_bass_kernel_spmd(nc, in_maps, core_ids=list(range(8)))
    outp = np.empty((B, NTOK, D), dtype=np.float32)
    for c in range(8):
        b, qh = divmod(c, 2)
        outp[b, qh * NQ:(qh + 1) * NQ] = res.results[c]["out"]
    return outp


# version 18
# speedup vs baseline: 1.0231x; 1.0231x over previous
import contextlib
import numpy as np

import concourse.bass as bass
import concourse.mybir as mybir
from concourse.bass_utils import run_bass_kernel_spmd

F32 = mybir.dt.float32
I32 = mybir.dt.int32
AF = mybir.ActivationFunctionType
ALU = mybir.AluOpType
AX = mybir.AxisListType

B, NTOK, NQ, D, H, HD, E = 4, 4096, 2048, 1024, 16, 64, 8
DC = D // 128
QKVD = 3 * D
SHIFT = 6.0


class Ctr:
    def __init__(self, ctx, nc, name):
        self.h = ctx.enter_context(nc.semaphore(name))
        self.n = 0

    def inc(self, op, k=1):
        op.then_inc(self.h, k)
        self.n += k
        return self.n


def build_nc(debug=None):
    nc = bass.Bass("TRN2", target_bir_lowering=False)

    x_kv = nc.dram_tensor("x_kv", [NTOK, D], F32, kind="ExternalInput").ap()
    x_q = nc.dram_tensor("x_q", [NQ, D], F32, kind="ExternalInput").ap()
    wqkv = nc.dram_tensor("wqkv", [DC, 128, QKVD], F32, kind="ExternalInput").ap()
    bqkv = nc.dram_tensor("bqkv", [QKVD], F32, kind="ExternalInput").ap()
    wproj = nc.dram_tensor("wproj", [DC, 128, D], F32, kind="ExternalInput").ap()
    bproj = nc.dram_tensor("bproj", [D], F32, kind="ExternalInput").ap()
    wr = nc.dram_tensor("wr", [128, DC * E], F32, kind="ExternalInput").ap()
    br = nc.dram_tensor("br", [E], F32, kind="ExternalInput").ap()
    W1 = nc.dram_tensor("W1", [E, D, D], F32, kind="ExternalInput").ap()
    W2 = nc.dram_tensor("W2", [E, D, D], F32, kind="ExternalInput").ap()
    b1 = nc.dram_tensor("b1", [E, D], F32, kind="ExternalInput").ap()
    b2 = nc.dram_tensor("b2", [E, D], F32, kind="ExternalInput").ap()
    out = nc.dram_tensor("out", [NQ, D], F32, kind="ExternalOutput").ap()

    sk = "ExternalOutput" if debug else "Internal"
    ski = "ExternalInput" if debug == "attnonly" else sk
    BFD = mybir.dt.bfloat16
    QH_s = nc.dram_tensor("QH_s", [DC, 128, NQ], BFD, kind=ski).ap()
    QL_s = nc.dram_tensor("QL_s", [DC, 128, NQ], BFD, kind=ski).ap()
    KH_s = nc.dram_tensor("KH_s", [DC, 128, NTOK], BFD, kind=ski).ap()
    KL_s = nc.dram_tensor("KL_s", [DC, 128, NTOK], BFD, kind=ski).ap()
    VH_s = nc.dram_tensor("VH_s", [NTOK, D], BFD, kind=ski).ap()
    VL_s = nc.dram_tensor("VL_s", [NTOK, D], BFD, kind=ski).ap()
    rsel = nc.dram_tensor("rsel", [1, 2], F32, kind="ExternalInput").ap()
    XNT_s = nc.dram_tensor("XNT_s", [DC, 128, NQ], F32, kind=ski).ap()
    XNH_s = nc.dram_tensor("XNH_s", [DC, 128, NQ], mybir.dt.bfloat16, kind=ski).ap()
    XNL_s = nc.dram_tensor("XNL_s", [DC, 128, NQ], mybir.dt.bfloat16, kind=ski).ap()
    MB_s = nc.dram_tensor("MB_s", [2, 128, E], F32, kind=sk).ap()
    AP_s = nc.dram_tensor("AP_s", [DC, 128, NQ], F32, kind=sk).ap()

    g = contextlib.ExitStack()
    with g:
        ident = g.enter_context(nc.sbuf_tensor("ident", [128, 128], F32))
        iot = g.enter_context(nc.sbuf_tensor("iot", [128, 128], I32))
        ones = g.enter_context(nc.sbuf_tensor("ones", [128, 128], F32))
        m1b = g.enter_context(nc.sbuf_tensor("m1b", [128, E], F32))
        m2b = g.enter_context(nc.sbuf_tensor("m2b", [128, E], F32))
        qb_sb = g.enter_context(nc.sbuf_tensor("qb_sb", [128, 3 * DC], F32))
        bvb = g.enter_context(nc.sbuf_tensor("bvb", [128, D], F32))
        bias_all = g.enter_context(nc.sbuf_tensor("bias_all", [128, DC], F32))
        b1sel = g.enter_context(nc.sbuf_tensor("b1sel", [128, 2 * DC], F32))

        ld = Ctr(g, nc, "ld")
        st = Ctr(g, nc, "st")
        ld_x = [Ctr(g, nc, f"ld_x{i}") for i in range(2)]
        ld_hd = [Ctr(g, nc, f"ld_hd{i}") for i in range(2)]
        ld_pw = [Ctr(g, nc, f"ld_pw{i}") for i in range(2)]
        ld_bt = Ctr(g, nc, "ld_bt")
        ld_wt = [Ctr(g, nc, f"ld_wt{i}") for i in range(3)]
        ld_xn = [Ctr(g, nc, f"ld_xn{i}") for i in range(2)]
        ld_xq = [Ctr(g, nc, f"ld_xq{i}") for i in range(2)]
        st_o = [Ctr(g, nc, f"st_o{i}") for i in range(2)]
        ld_mv = Ctr(g, nc, "ld_mv")
        cv = Ctr(g, nc, "cv")
        cs = Ctr(g, nc, "cs")
        ct = Ctr(g, nc, "ct")
        cg = Ctr(g, nc, "cg")

        def phase_p0():
            with contextlib.ExitStack() as p:
                BFP = mybir.dt.bfloat16
                wq_st = p.enter_context(nc.sbuf_tensor("wq_st", [128, QKVD], F32))
                wq_hi = p.enter_context(nc.sbuf_tensor("wq_hi", [128, DC * QKVD], BFP))
                wq_lo = p.enter_context(nc.sbuf_tensor("wq_lo", [128, DC * QKVD], BFP))
                xnT_hi = p.enter_context(nc.sbuf_tensor("xnT_hi", [128, DC * 512], BFP))
                xnT_lo = p.enter_context(nc.sbuf_tensor("xnT_lo", [128, DC * 512], BFP))
                wr_hi = p.enter_context(nc.sbuf_tensor("wr_hi", [128, DC * E], BFP))
                wr_lo = p.enter_context(nc.sbuf_tensor("wr_lo", [128, DC * E], BFP))
                xbuf = [p.enter_context(nc.sbuf_tensor(f"xbuf{i}", [128, D], F32)) for i in range(2)]
                x2buf = p.enter_context(nc.sbuf_tensor("x2buf", [128, D], F32))
                xnbuf = [p.enter_context(nc.sbuf_tensor(f"xnbuf{i}", [128, D], F32)) for i in range(2)]
                ev = [p.enter_context(nc.sbuf_tensor(f"ev{i}", [128, 512], F32)) for i in range(2)]
                ehi = [p.enter_context(nc.sbuf_tensor(f"ehi{i}", [128, 512], BFP)) for i in range(2)]
                elo = [p.enter_context(nc.sbuf_tensor(f"elo{i}", [128, 512], BFP)) for i in range(2)]
                vout = [p.enter_context(nc.sbuf_tensor(f"vout{i}", [128, D], F32)) for i in range(2)]
                vhi = [p.enter_context(nc.sbuf_tensor(f"vhi{i}", [128, D], BFP)) for i in range(2)]
                vlo = [p.enter_context(nc.sbuf_tensor(f"vlo{i}", [128, D], BFP)) for i in range(2)]
                stat = p.enter_context(nc.sbuf_tensor("stat", [128, 16], F32))
                wr_sb = p.enter_context(nc.sbuf_tensor("wr_sb", [128, DC * E], F32))
                sc8 = p.enter_context(nc.sbuf_tensor("sc8", [1, E], F32))
                sab = p.enter_context(nc.sbuf_tensor("sab", [1, 2 * E], F32))
                rs_sb = p.enter_context(nc.sbuf_tensor("rs_sb", [1, 2], F32))
                mx = p.enter_context(nc.sbuf_tensor("mx", [1, 2], F32))
                mk12 = p.enter_context(nc.sbuf_tensor("mk12", [1, 2 * E], F32))
                bvrow = p.enter_context(nc.sbuf_tensor("bvrow", [1, D], F32))
                ps_qk = [p.enter_context(nc.psum_tensor(f"ps_qk{i}", [128, 512], F32)) for i in range(2)]
                ps_v = [p.enter_context(nc.psum_tensor(f"ps_v{i}", [128, 512], F32)) for i in range(2)]
                ps_tr = [p.enter_context(nc.psum_tensor(f"ps_tr{i}", [128, 512], F32)) for i in range(2)]
                ps_bc = p.enter_context(nc.psum_tensor("ps_bc", [128, 512], F32))

                blk = p.enter_context(nc.Block("p0"))

                @blk.sync
                def _(sync):
                    sync.dma_start(out=wq_st[:], in_=wqkv[0]).then_inc(ld.h, 16)
                    sync.dma_start(out=wr_sb[:], in_=wr).then_inc(ld.h, 16)
                    sync.dma_start(out=qb_sb[:], in_=bqkv.rearrange("(j p) -> p j", p=128),
                                   allow_slow_non_contiguous=True).then_inc(ld.h, 16)
                    sync.dma_start(out=bvrow[:],
                                   in_=bqkv[2 * D:3 * D].rearrange("(o d) -> o d", o=1)).then_inc(ld.h, 16)
                    sync.dma_start(out=sc8[:], in_=br.rearrange("(o e) -> o e", o=1)).then_inc(ld.h, 16)
                    sync.dma_start(out=rs_sb[:], in_=rsel).then_inc(ld.h, 16)
                ld.n += 16 * 6
                pre_ld = ld.n

                wsplit_marks = []
                for ci in range(DC):
                    if ci > 0:
                        @blk.sync
                        def _(sync, ci=ci):
                            sync.wait_ge(cv.h, wsplit_marks[-1])
                            sync.dma_start(out=wq_st[:], in_=wqkv[ci]).then_inc(ld.h, 16)
                        ld.n += 16
                    wld2 = ld.n

                    @blk.vector
                    def _(vector, ci=ci, wld2=wld2):
                        vector.wait_ge(ld.h, wld2)
                        vector.tensor_copy(wq_hi[:, ci * QKVD:(ci + 1) * QKVD], wq_st[:])
                        vector.drain()
                        cv.inc(vector.tensor_sub(wq_lo[:, ci * QKVD:(ci + 1) * QKVD],
                                                 wq_st[:], wq_hi[:, ci * QKVD:(ci + 1) * QKVD]))
                    wsplit_marks.append(cv.n)

                @blk.vector
                def _(vector):
                    vector.wait_ge(ld.h, pre_ld)
                    vector.tensor_copy(wr_hi[:], wr_sb[:])
                    vector.drain()
                    cv.inc(vector.tensor_sub(wr_lo[:], wr_sb[:], wr_hi[:]))
                wsplit_done = cv.n

                @blk.gpsimd
                def _(gpsimd):
                    gpsimd.iota(iot[:], [[1, 128]], base=0, channel_multiplier=-1)
                    cg.inc(gpsimd.memset(ones[:], 1.0))
                g_pre = cg.n

                @blk.vector
                def _(vector):
                    vector.wait_ge(cg.h, g_pre)
                    cv.inc(vector.tensor_scalar(out=ident[:], in0=iot[:], scalar1=0, scalar2=None,
                                                op0=ALU.is_equal))
                ident_ok = cv.n

                @blk.tensor
                def _(tensor):
                    tensor.wait_ge(ld.h, pre_ld)
                    tensor.wait_ge(cv.h, ident_ok)
                    tensor.matmul(ps_v[0][:], ones[0:1, :], bvrow[0:1, 0:512], start=True, stop=True)
                    ct.inc(tensor.matmul(ps_v[1][:], ones[0:1, :], bvrow[0:1, 512:1024],
                                         start=True, stop=True))
                bv_mm = ct.n

                @blk.vector
                def _(vector):
                    vector.wait_ge(ct.h, bv_mm)
                    vector.tensor_copy(bvb[:, 0:512], ps_v[0][:])
                    cv.inc(vector.tensor_copy(bvb[:, 512:1024], ps_v[1][:]))

                state = {"n_ln": 0, "xbuf_free": [None, None],
                         "tr_evac": [(cv.h, wsplit_done)],
                         "qk_evac": [], "v_evacs": [], "qk_sub": [], "v_sub": []}

                def ln_chunk(src_ap, j):
                    i = state["n_ln"] % 2
                    state["n_ln"] += 1

                    @blk.sync
                    def _(sync, i=i, src_ap=src_ap):
                        if state["xbuf_free"][i] is not None:
                            sem, val = state["xbuf_free"][i]
                            sync.wait_ge(sem, val)
                        sync.dma_start(out=xbuf[i][:], in_=src_ap).then_inc(ld_x[i].h, 16)
                    ld_x[i].n += 16
                    my_ld = ld_x[i].n

                    @blk.scalar
                    def _(scalar, i=i, my_ld=my_ld):
                        scalar.wait_ge(ld_x[i].h, my_ld)
                        cs.inc(scalar.activation(out=x2buf[:], in_=xbuf[i][:], func=AF.Square,
                                                 accum_out=stat[:, 4:5]))
                    sq_done = cs.n

                    @blk.vector
                    def _(vector, i=i, my_ld=my_ld, sq_done=sq_done):
                        vector.wait_ge(ld_x[i].h, my_ld)
                        vector.reduce_sum(stat[:, 0:1], xbuf[i][:], axis=AX.X)
                        vector.wait_ge(cs.h, sq_done)
                        vector.drain()
                        vector.tensor_scalar(out=stat[:, 1:2], in0=stat[:, 0:1], scalar1=1.0 / D,
                                             scalar2=None, op0=ALU.mult)
                        vector.tensor_scalar(out=stat[:, 5:6], in0=stat[:, 4:5], scalar1=1.0 / D,
                                             scalar2=None, op0=ALU.mult)
                        vector.drain()
                        vector.scalar_tensor_tensor(out=stat[:, 2:3], in0=stat[:, 1:2],
                                                    scalar=stat[:, 1:2], in1=stat[:, 5:6],
                                                    op0=ALU.mult, op1=ALU.subtract)
                        vector.drain()
                        cv.inc(vector.tensor_scalar(out=stat[:, 2:3], in0=stat[:, 2:3], scalar1=-1.0,
                                                    scalar2=1e-5, op0=ALU.mult, op1=ALU.add))
                    var_done = cv.n

                    @blk.scalar
                    def _(scalar, var_done=var_done):
                        scalar.wait_ge(cv.h, var_done)
                        cs.inc(scalar.activation(out=stat[:, 3:4], in_=stat[:, 2:3], func=AF.Sqrt))
                    sqrt_done = cs.n

                    @blk.vector
                    def _(vector, sqrt_done=sqrt_done):
                        vector.wait_ge(cs.h, sqrt_done)
                        vector.reciprocal(out=stat[:, 6:7], in_=stat[:, 3:4])
                        vector.drain()
                        cv.inc(vector.scalar_tensor_tensor(out=stat[:, 7:8], in0=stat[:, 1:2],
                                                           scalar=-1.0, in1=stat[:, 6:7],
                                                           op0=ALU.mult, op1=ALU.mult))
                    rb_done = cv.n

                    @blk.scalar
                    def _(scalar, i=i, rb_done=rb_done, te2=(state["tr_evac"][-2]
                                                            if len(state["tr_evac"]) >= 2 else None)):
                        scalar.wait_ge(cv.h, rb_done)
                        if te2 is not None:
                            scalar.wait_ge(te2[0], te2[1])
                        cs.inc(scalar.activation(out=xnbuf[i][:], in_=xbuf[i][:], func=AF.Identity,
                                                 scale=stat[:, 6:7], bias=stat[:, 7:8]))
                    ln_done = cs.n
                    state["xbuf_free"][i] = (cs.h, ln_done)

                    @blk.tensor
                    def _(tensor, i=i, ln_done=ln_done, te1=state["tr_evac"][-1]):
                        tensor.wait_ge(cs.h, ln_done)
                        tensor.wait_ge(te1[0], te1[1])
                        for half in range(2):
                            for q in range(4):
                                ci = half * 4 + q
                                op = tensor.transpose(ps_tr[half][:, q * 128:(q + 1) * 128],
                                                      xnbuf[i][:, ci * 128:(ci + 1) * 128], ident[:])
                        ct.inc(op)
                    tp_done = ct.n

                    @blk.scalar
                    def _(scalar, j=j, tp_done=tp_done):
                        scalar.wait_ge(ct.h, tp_done)
                        for half in range(2):
                            dsth = xnT_hi[:].rearrange("p (c t) -> p c t", c=DC)[:, half * 4:half * 4 + 4,
                                                                                j * 128:(j + 1) * 128]
                            op = scalar.activation(out=dsth,
                                                   in_=ps_tr[half][:].rearrange("p (q t) -> p q t", q=4),
                                                   func=AF.Identity)
                        cs.inc(op)
                    hi_m = cs.n

                    @blk.vector
                    def _(vector, j=j, hi_m=hi_m, tp_done=tp_done):
                        vector.wait_ge(ct.h, tp_done)
                        vector.wait_ge(cs.h, hi_m)
                        for half in range(2):
                            dsth = xnT_hi[:].rearrange("p (c t) -> p c t", c=DC)[:, half * 4:half * 4 + 4,
                                                                                j * 128:(j + 1) * 128]
                            dstl = xnT_lo[:].rearrange("p (c t) -> p c t", c=DC)[:, half * 4:half * 4 + 4,
                                                                                j * 128:(j + 1) * 128]
                            op = vector.tensor_sub(dstl,
                                                   ps_tr[half][:].rearrange("p (q t) -> p q t", q=4),
                                                   dsth)
                        cv.inc(op)
                    state["tr_evac"].append((cv.h, cv.n))

                def qk_matmuls(is_q, base_tok, xnT_ready):
                    woff = 0 if is_q else D
                    boff = 0 if is_q else DC
                    dsth = QH_s if is_q else KH_s
                    dstl = QL_s if is_q else KL_s
                    for dout in range(DC):
                        bank = dout % 2

                        @blk.tensor
                        def _(tensor, dout=dout, bank=bank, xnT_ready=xnT_ready, woff=woff):
                            tensor.wait_ge(xnT_ready[0], xnT_ready[1])
                            if len(state["qk_evac"]) >= 2:
                                tensor.wait_ge(cv.h, state["qk_evac"][-2])
                            base = woff + dout * 128
                            for ci in range(DC):
                                wsl = slice(ci * QKVD + base, ci * QKVD + base + 128)
                                xsl = slice(ci * 512, (ci + 1) * 512)
                                tensor.matmul(ps_qk[bank][:], wq_hi[:, wsl], xnT_hi[:, xsl],
                                              start=(ci == 0), stop=False)
                                tensor.matmul(ps_qk[bank][:], wq_hi[:, wsl], xnT_lo[:, xsl],
                                              start=False, stop=False)
                                op = tensor.matmul(ps_qk[bank][:], wq_lo[:, wsl], xnT_hi[:, xsl],
                                                   start=False, stop=(ci == DC - 1))
                            ct.inc(op)
                        mm_done = ct.n

                        @blk.vector
                        def _(vector, bank=bank, mm_done=mm_done, boff=boff, dout=dout,
                              psub=(state["qk_sub"][-2] if len(state["qk_sub"]) >= 2 else None)):
                            vector.wait_ge(ct.h, mm_done)
                            vector.wait_ge(st.h, st.n)
                            if psub is not None:
                                vector.wait_ge(cg.h, psub)
                            vector.tensor_scalar(out=ev[bank][:], in0=ps_qk[bank][:],
                                                 scalar1=qb_sb[:, boff + dout:boff + dout + 1],
                                                 scalar2=None, op0=ALU.add)
                            vector.drain()
                            cv.inc(vector.tensor_copy(ehi[bank][:], ev[bank][:]))
                        state["qk_evac"].append(cv.n)
                        evd = cv.n

                        @blk.gpsimd
                        def _(gpsimd, bank=bank, evd=evd):
                            gpsimd.wait_ge(cv.h, evd)
                            cg.inc(gpsimd.tensor_sub(elo[bank][:], ev[bank][:], ehi[bank][:]))
                        state["qk_sub"].append(cg.n)
                        subd = cg.n

                        @blk.scalar
                        def _(scalar, dout=dout, bank=bank, subd=subd, base_tok=base_tok,
                              dsth=dsth, dstl=dstl):
                            scalar.wait_ge(cg.h, subd)
                            scalar.dma_start(out=dsth[dout, :, base_tok:base_tok + 512],
                                             in_=ehi[bank][:]).then_inc(st.h, 16)
                            scalar.dma_start(out=dstl[dout, :, base_tok:base_tok + 512],
                                             in_=elo[bank][:]).then_inc(st.h, 16)
                        st.n += 32

                def v_matmuls(base_tok, xnT_ready):
                    for t in range(4):
                        vb = t % 2

                        @blk.tensor
                        def _(tensor, t=t, xnT_ready=xnT_ready):
                            tensor.wait_ge(xnT_ready[0], xnT_ready[1])
                            if state["v_evacs"]:
                                tensor.wait_ge(cv.h, state["v_evacs"][-1])
                            for half in range(2):
                                for ci in range(DC):
                                    xsl = slice(ci * 512 + t * 128, ci * 512 + (t + 1) * 128)
                                    wsl = slice(ci * QKVD + 2 * D + half * 512,
                                                ci * QKVD + 2 * D + (half + 1) * 512)
                                    tensor.matmul(ps_v[half][:], xnT_hi[:, xsl], wq_hi[:, wsl],
                                                  start=(ci == 0), stop=False)
                                    tensor.matmul(ps_v[half][:], xnT_hi[:, xsl], wq_lo[:, wsl],
                                                  start=False, stop=False)
                                    op = tensor.matmul(ps_v[half][:], xnT_lo[:, xsl], wq_hi[:, wsl],
                                                       start=False, stop=(ci == DC - 1))
                            ct.inc(op)
                        vmm = ct.n

                        @blk.vector
                        def _(vector, vb=vb, vmm=vmm,
                              psub=(state["v_sub"][-2] if len(state["v_sub"]) >= 2 else None)):
                            vector.wait_ge(ct.h, vmm)
                            vector.wait_ge(st.h, st.n)
                            if psub is not None:
                                vector.wait_ge(cg.h, psub)
                            vector.tensor_tensor(out=vout[vb][:, 0:512], in0=ps_v[0][:],
                                                 in1=bvb[:, 0:512], op=ALU.add)
                            vector.tensor_tensor(out=vout[vb][:, 512:1024], in0=ps_v[1][:],
                                                 in1=bvb[:, 512:1024], op=ALU.add)
                            vector.drain()
                            cv.inc(vector.tensor_copy(vhi[vb][:], vout[vb][:]))
                        state["v_evacs"].append(cv.n)
                        evd = cv.n

                        @blk.gpsimd
                        def _(gpsimd, vb=vb, evd=evd):
                            gpsimd.wait_ge(cv.h, evd)
                            cg.inc(gpsimd.tensor_sub(vlo[vb][:], vout[vb][:], vhi[vb][:]))
                        state["v_sub"].append(cg.n)
                        vsub = cg.n

                        @blk.sync
                        def _(sync, t=t, vb=vb, vsub=vsub, base_tok=base_tok):
                            sync.wait_ge(cg.h, vsub)
                            sync.dma_start(
                                out=VH_s[base_tok + t * 128: base_tok + (t + 1) * 128, :],
                                in_=vhi[vb][:]).then_inc(st.h, 16)
                            sync.dma_start(
                                out=VL_s[base_tok + t * 128: base_tok + (t + 1) * 128, :],
                                in_=vlo[vb][:]).then_inc(st.h, 16)
                        st.n += 32

                def router_mm(xnT_ready, part):
                    off = part * E

                    @blk.tensor
                    def _(tensor, xnT_ready=xnT_ready, off=off):
                        tensor.wait_ge(xnT_ready[0], xnT_ready[1])
                        for ci in range(DC):
                            xc = slice(ci * 512, ci * 512 + 1)
                            wc = slice(ci * E, (ci + 1) * E)
                            tensor.matmul(ps_bc[0:1, off:off + E], xnT_hi[:, xc], wr_hi[:, wc],
                                          start=(ci == 0), stop=False)
                            tensor.matmul(ps_bc[0:1, off:off + E], xnT_hi[:, xc], wr_lo[:, wc],
                                          start=False, stop=False)
                            op = tensor.matmul(ps_bc[0:1, off:off + E], xnT_lo[:, xc],
                                               wr_hi[:, wc], start=False, stop=(ci == DC - 1))
                        ct.inc(op)
                    rt = ct.n

                    @blk.vector
                    def _(vector, rt=rt, off=off):
                        vector.wait_ge(ct.h, rt)
                        cv.inc(vector.tensor_copy(sab[0:1, off:off + E], ps_bc[0:1, off:off + E]))
                    return cv.n

                def router_combine(r2_done):
                    @blk.vector
                    def _(vector, r2_done=r2_done):
                        vector.wait_ge(cv.h, r2_done)
                        vector.wait_ge(ld.h, pre_ld)
                        vector.drain()
                        vector.scalar_tensor_tensor(out=sc8[:], in0=sab[0:1, 0:E],
                                                    scalar=rs_sb[0:1, 0:1], in1=sc8[:],
                                                    op0=ALU.mult, op1=ALU.add)
                        vector.drain()
                        vector.scalar_tensor_tensor(out=sc8[:], in0=sab[0:1, E:2 * E],
                                                    scalar=rs_sb[0:1, 1:2], in1=sc8[:],
                                                    op0=ALU.mult, op1=ALU.add)
                        vector.drain()
                        vector.reduce_max(mx[0:1, 0:1], sc8[:], axis=AX.X)
                        vector.drain()
                        vector.tensor_scalar(out=mk12[0:1, 0:E], in0=sc8[:], scalar1=mx[0:1, 0:1],
                                             scalar2=None, op0=ALU.is_ge)
                        vector.drain()
                        vector.scalar_tensor_tensor(out=sc8[:], in0=mk12[0:1, 0:E], scalar=-1e30,
                                                    in1=sc8[:], op0=ALU.mult, op1=ALU.add)
                        vector.drain()
                        vector.reduce_max(mx[0:1, 1:2], sc8[:], axis=AX.X)
                        vector.drain()
                        cv.inc(vector.tensor_scalar(out=mk12[0:1, E:2 * E], in0=sc8[:],
                                                    scalar1=mx[0:1, 1:2], scalar2=None, op0=ALU.is_ge))
                    msk = cv.n

                    @blk.tensor
                    def _(tensor, msk=msk):
                        tensor.wait_ge(cv.h, msk)
                        tensor.matmul(ps_bc[:, 256:256 + E], ones[0:1, :], mk12[0:1, 0:E],
                                      start=True, stop=True)
                        ct.inc(tensor.matmul(ps_bc[:, 384:384 + E], ones[0:1, :], mk12[0:1, E:2 * E],
                                             start=True, stop=True))
                    bca = ct.n

                    @blk.vector
                    def _(vector, bca=bca):
                        vector.wait_ge(ct.h, bca)
                        vector.tensor_copy(m1b[:], ps_bc[:, 256:256 + E])
                        cv.inc(vector.tensor_copy(m2b[:], ps_bc[:, 384:384 + E]))
                    mdone = cv.n

                    if debug:
                        @blk.gpsimd
                        def _(gpsimd, mdone=mdone):
                            gpsimd.wait_ge(cv.h, mdone)
                            gpsimd.dma_start(out=MB_s[0], in_=m1b[:]).then_inc(st.h, 16)
                            gpsimd.dma_start(out=MB_s[1], in_=m2b[:]).then_inc(st.h, 16)
                        st.n += 32

                r_parts = {}
                for sc in range(8):
                    base_tok = sc * 512
                    for j in range(4):
                        ln_chunk(x_kv[base_tok + j * 128: base_tok + (j + 1) * 128, :], j)
                    xnT_ready = state["tr_evac"][-1]
                    qk_matmuls(False, base_tok, xnT_ready)
                    v_matmuls(base_tok, xnT_ready)
                    if sc < 4:
                        @blk.sync
                        def _(sync, base_tok=base_tok, xnT_ready=xnT_ready):
                            sync.wait_ge(xnT_ready[0], xnT_ready[1])
                            sync.dma_start(
                                out=XNH_s[:, :, base_tok:base_tok + 512].rearrange("c p t -> p c t"),
                                in_=xnT_hi[:].rearrange("p (c t) -> p c t", c=DC)).then_inc(st.h, 16)
                            sync.dma_start(
                                out=XNL_s[:, :, base_tok:base_tok + 512].rearrange("c p t -> p c t"),
                                in_=xnT_lo[:].rearrange("p (c t) -> p c t", c=DC)).then_inc(st.h, 16)
                        st.n += 32
                        qk_matmuls(True, base_tok, xnT_ready)
                    if sc == 0:
                        r_parts[0] = router_mm(xnT_ready, 0)
                    if sc == 4:
                        r_parts[1] = router_mm(xnT_ready, 1)
                        router_combine(r_parts[1])
                    tdone = ct.n

                    @blk.gpsimd
                    def _(gpsimd, tdone=tdone):
                        gpsimd.wait_ge(ct.h, tdone)
                        gpsimd.wait_ge(st.h, st.n)
                        cg.inc(gpsimd.engine_nop())
                    state["tr_evac"].append((cg.h, cg.n))

                @blk.gpsimd
                def _(gpsimd):
                    gpsimd.wait_ge(st.h, st.n)


        if debug == "attnonly":
            with contextlib.ExitStack() as p:
                blk = p.enter_context(nc.Block("pre"))

                @blk.gpsimd
                def _(gpsimd):
                    cg.inc(gpsimd.memset(ones[:], 1.0))
        else:
            phase_p0()

        if debug == "p0":
            return nc

        projacc = g.enter_context(nc.sbuf_tensor("projacc", [128, DC * NQ], F32))

        with contextlib.ExitStack() as p:
            BF = mybir.dt.bfloat16
            kst = [p.enter_context(nc.sbuf_tensor(f"kst{i}", [128, NTOK], BF)) for i in range(2)]
            qab = [p.enter_context(nc.sbuf_tensor(f"qab{i}", [128, NQ], BF)) for i in range(2)]
            qcd = [p.enter_context(nc.sbuf_tensor(f"qcd{i}", [128, NQ], BF)) for i in range(2)]
            vA = [p.enter_context(nc.sbuf_tensor(f"vA{i}", [128, 32 * (HD + 1)], BF)) for i in range(2)]
            vB = [p.enter_context(nc.sbuf_tensor(f"vB{i}", [128, 32 * HD], BF)) for i in range(2)]
            p32 = [p.enter_context(nc.sbuf_tensor(f"p32_{i}", [128, 1024], F32)) for i in range(2)]
            phi = [p.enter_context(nc.sbuf_tensor(f"phi{i}", [128, 1024], BF)) for i in range(4)]
            plo = [p.enter_context(nc.sbuf_tensor(f"plo{i}", [128, 1024], BF)) for i in range(4)]
            rzb = p.enter_context(nc.sbuf_tensor("rzb", [128, 1024], F32))
            t1 = p.enter_context(nc.sbuf_tensor("t1", [64, 1024], F32))
            aout = p.enter_context(nc.sbuf_tensor("aout", [64, 1024], F32))
            apair = p.enter_context(nc.sbuf_tensor("apair", [128, NQ], F32))
            aph = p.enter_context(nc.sbuf_tensor("aph", [128, NQ], BF))
            apl = p.enter_context(nc.sbuf_tensor("apl", [128, NQ], BF))
            pw_sb = [p.enter_context(nc.sbuf_tensor(f"pw_sb{i}", [128, D], F32)) for i in range(2)]
            pwh = [p.enter_context(nc.sbuf_tensor("pwh0", [128, D], BF))] * 2
            pwl = [p.enter_context(nc.sbuf_tensor("pwl0", [128, D], BF))] * 2
            nshift = p.enter_context(nc.sbuf_tensor("nshift", [128, 1], F32))
            ps_s = [p.enter_context(nc.psum_tensor(f"ps_s{i}", [128, 1024], F32)) for i in range(2)]
            ps_ab = [p.enter_context(nc.psum_tensor(f"ps_ab{i}", [65, 512], F32)) for i in range(2)]
            ps_a2 = [p.enter_context(nc.psum_tensor(f"ps_a2{i}", [128, 512], F32)) for i in range(2)]

            blk = p.enter_context(nc.Block("attn"))

            @blk.gpsimd
            def _(gpsimd):
                for i in range(2):
                    gpsimd.memset(
                        vA[i][:].rearrange("p (kc w) -> p kc w", w=HD + 1)[:, :, HD:HD + 1], 1.0)
                cg.inc(gpsimd.memset(nshift[:], -SHIFT))
            vones = cg.n

            head_ld = {}
            pw_split_by_pair = {}
            conv_done = {}
            head_free = {}
            proj_ev_by_pair = {}
            A = {"o_free": None, "proj_done": None, "prev_exp_last": None,
                 "prev_vhl_last": None, "prev_pv_last": None,
                 "apair_free": None, "apair_split": None, "aout_free": None,
                 "ps_s_free": None, "staging_free": None}

            def load_head(h):
                i = h % 2
                b64 = 64 * (h % 2)

                @blk.sync
                def _(sync, h=h, i=i, b64=b64):
                    if h >= 2:
                        sem, val = head_free[h - 2]
                        sync.wait_ge(sem, val)
                    if h == 0:
                        sync.wait_ge(cg.h, vones)
                    sync.dma_start(out=kst[i][0:64, :],
                                   in_=KH_s[h // 2, b64:b64 + 64, :]).then_inc(ld_hd[i].h, 16)
                    sync.dma_start(out=kst[i][64:128, :],
                                   in_=KL_s[h // 2, b64:b64 + 64, :]).then_inc(ld_hd[i].h, 16)
                    for half in range(2):
                        sync.dma_start(out=qab[i][64 * half:64 * half + 64, :],
                                       in_=QH_s[h // 2, b64:b64 + 64, :]).then_inc(ld_hd[i].h, 16)
                        sync.dma_start(out=qcd[i][64 * half:64 * half + 64, :],
                                       in_=QL_s[h // 2, b64:b64 + 64, :]).then_inc(ld_hd[i].h, 16)
                    for kg in range(4):
                        sync.dma_start(
                            out=vA[i][:].rearrange("p (kc w) -> p kc w", w=HD + 1)[:, kg * 8:(kg + 1) * 8, 0:HD],
                            in_=VH_s.rearrange("(kc p) c -> p kc c", p=128)[:, kg * 8:(kg + 1) * 8,
                                                                           h * HD:(h + 1) * HD]).then_inc(ld_hd[i].h, 16)
                        sync.dma_start(
                            out=vB[i][:].rearrange("p (kc w) -> p kc w", w=HD)[:, kg * 8:(kg + 1) * 8, :],
                            in_=VL_s.rearrange("(kc p) c -> p kc c", p=128)[:, kg * 8:(kg + 1) * 8,
                                                                           h * HD:(h + 1) * HD]).then_inc(ld_hd[i].h, 16)
                    if h % 2 == 0:
                        q = h // 2
                        if q >= 2:
                            sync.wait_ge(cv.h, pw_split_by_pair[q - 2])
                        sync.dma_start(out=pw_sb[q % 2][:], in_=wproj[q]).then_inc(ld_pw[q % 2].h, 16)
                ld_hd[i].n += 14 * 16
                if h % 2 == 0:
                    ld_pw[(h // 2) % 2].n += 16
                head_ld[h] = ld_hd[i].n
                conv_done[h] = None


            load_head(0)

            n_heads = 2 if debug == "attn1" else H

            def emit_pva(h, i, s_, phi_done, pva_done):
                qh, kc = divmod(s_, 32)
                pbk = s_ % 4

                @blk.tensor
                def _(tensor, i=i, s_=s_, kc=kc, pbk=pbk):
                    pd = phi_done[s_]
                    tensor.wait_ge(pd[0], pd[1])
                    if kc == 0 and A["o_free"] is not None:
                        tensor.wait_ge(cv.h, A["o_free"])
                    for qq in range(2):
                        tensor.matmul(ps_ab[qq][:],
                                      vA[i][:, kc * (HD + 1):(kc + 1) * (HD + 1)],
                                      phi[pbk][:, qq * 512:(qq + 1) * 512],
                                      start=(kc == 0), stop=False)
                    for qq in range(2):
                        op = tensor.matmul(ps_a2[qq][0:64, :],
                                           vB[i][:, kc * HD:(kc + 1) * HD],
                                           phi[pbk][:, qq * 512:(qq + 1) * 512],
                                           start=(kc == 0), stop=(kc == 31))
                    ct.inc(op)
                pva_done.append(ct.n)

            def emit_pvb(h, i, s_, plo_done, pvb_done):
                qh, kc = divmod(s_, 32)
                pbk = s_ % 4

                @blk.tensor
                def _(tensor, i=i, s_=s_, kc=kc, pbk=pbk):
                    pl = plo_done[s_]
                    tensor.wait_ge(pl[0], pl[1])
                    for qq in range(2):
                        op = tensor.matmul(ps_ab[qq][:],
                                           vA[i][:, kc * (HD + 1):(kc + 1) * (HD + 1)],
                                           plo[pbk][:, qq * 512:(qq + 1) * 512],
                                           start=False, stop=(kc == 31))
                    ct.inc(op)
                pvb_done.append(ct.n)

            def finalize_a(h, qh, pvb_done):
                pvl = pvb_done[qh * 32 + 31]

                @blk.vector
                def _(vector, pvl=pvl):
                    vector.wait_ge(ct.h, pvl)
                    for qq in range(2):
                        vector.reciprocal(out=rzb[64:65, qq * 512:(qq + 1) * 512],
                                          in_=ps_ab[qq][64:65, :])
                    for qq in range(2):
                        op = vector.tensor_copy(t1[:, qq * 512:(qq + 1) * 512],
                                                ps_ab[qq][0:64, :])
                    cv.inc(op)

                @blk.vector
                def _(vector):
                    vector.drain()
                    for qq in range(2):
                        op = vector.tensor_tensor(out=t1[:, qq * 512:(qq + 1) * 512],
                                                  in0=ps_a2[qq][0:64, :],
                                                  in1=t1[:, qq * 512:(qq + 1) * 512], op=ALU.add)
                    cv.inc(op)
                A["o_free"] = cv.n
                return cv.n

            def finalize_b(h, qh, t1_done):
                qoff = qh * 1024

                @blk.tensor
                def _(tensor, t1_done=t1_done):
                    tensor.wait_ge(cv.h, t1_done)
                    for qq in range(2):
                        op = tensor.matmul(ps_a2[qq][64:128, :],
                                           ones[64:65, 0:64],
                                           rzb[64:65, qq * 512:(qq + 1) * 512],
                                           start=True, stop=True, tile_position=(64, 64),
                                           skip_group_check=True)
                    ct.inc(op)
                zb_done = ct.n

                @blk.vector
                def _(vector, zb_done=zb_done, qoff=qoff, h=h, qh=qh):
                    vector.wait_ge(ct.h, zb_done)
                    if h % 2 == 0 and A["apair_split"] is not None and qh == 0:
                        vector.wait_ge(cv.h, A["apair_split"])
                    if h % 2 == 1 and A["aout_free"] is not None:
                        vector.wait_ge(ld_mv.h, A["aout_free"])
                    for qq in range(2):
                        vector.tensor_copy(rzb[0:64, qq * 512:(qq + 1) * 512],
                                           ps_a2[qq][64:128, :])
                    vector.drain()
                    for qq in range(2):
                        if h % 2 == 0:
                            dsl = apair[0:64, qoff + qq * 512: qoff + (qq + 1) * 512]
                        else:
                            dsl = aout[0:64, qq * 512:(qq + 1) * 512]
                        op = vector.tensor_tensor(
                            out=dsl,
                            in0=t1[:, qq * 512:(qq + 1) * 512],
                            in1=rzb[0:64, qq * 512:(qq + 1) * 512], op=ALU.mult)
                    cv.inc(op)
                if h % 2 == 1:
                    halfdone = cv.n

                    @blk.gpsimd
                    def _(gpsimd, halfdone=halfdone, qoff=qoff):
                        gpsimd.wait_ge(cv.h, halfdone)
                        ld_mv.inc(gpsimd.dma_start(out=apair[64:128, qoff:qoff + 1024],
                                                   in_=aout[0:64, :]), 16)
                    A["aout_free"] = ld_mv.n
                return cv.n

            for h in range(n_heads):
                i = h % 2
                if h + 1 < n_heads:
                    load_head(h + 1)

                exp_done = []
                phi_done = []
                plo_done = []
                pva_done = []
                pvb_done = []
                pv_done = None
                for qh in range(2):
                    qoff = qh * 1024
                    run = cs.n
                    for kc in range(32):
                        run += 2 if (qh * 32 + kc) % 2 == 0 else 1
                        exp_done.append(run)
                    scalar_recs = []
                    for kc in range(32):
                        s_ = qh * 32 + kc
                        pbk = s_ % 2
                        pb4 = s_ % 4

                        @blk.tensor
                        def _(tensor, kc=kc, s_=s_, pbk=pbk, i=i, qoff=qoff, h=h):
                            if s_ == 0:
                                tensor.wait_ge(ld_hd[h % 2].h, head_ld[h])
                                if A["ps_s_free"] is not None:
                                    tensor.wait_ge(cv.h, A["ps_s_free"])
                            if s_ >= 2:
                                tensor.wait_ge(cs.h, exp_done[s_ - 2])
                            elif A["prev_exp_last"] is not None:
                                tensor.wait_ge(cs.h, A["prev_exp_last"])
                            for qq in range(2):
                                tensor.matmul(ps_s[pbk][:, qq * 512:(qq + 1) * 512],
                                              kst[i][:, kc * 128:(kc + 1) * 128],
                                              qab[i][:, qoff + qq * 512: qoff + (qq + 1) * 512],
                                              start=True, stop=False)
                            for qq in range(2):
                                op = tensor.matmul(ps_s[pbk][:, qq * 512:(qq + 1) * 512],
                                                   kst[i][:, kc * 128:(kc + 1) * 128],
                                                   qcd[i][:, qoff + qq * 512: qoff + (qq + 1) * 512],
                                                   start=False, stop=True)
                            ct.inc(op)
                        p32_wait = None
                        if s_ >= 2:
                            p32_wait = plo_done[s_ - 2]
                        elif A["prev_vhl_last"] is not None:
                            p32_wait = A["prev_vhl_last"]
                        phibuf_wait = None
                        if s_ >= 4:
                            phibuf_wait = pva_done[s_ - 4]
                        elif A["prev_pv_last"] is not None:
                            phibuf_wait = A["prev_pv_last"]
                        even = (s_ % 2 == 0)
                        scalar_recs.append((ct.n, p32_wait, pbk, pb4, even, phibuf_wait))

                        if even:
                            phi_done.append((cs.h, exp_done[s_] - 1))

                            @blk.vector
                            def _(vector, s_=s_, pb4=pb4, pbk=pbk):
                                vector.wait_ge(cs.h, exp_done[s_])
                                if s_ >= 4:
                                    vector.wait_ge(ct.h, pvb_done[s_ - 4])
                                cv.inc(vector.tensor_sub(plo[pb4][:], p32[pbk][:],
                                                         phi[pb4][:]))
                            plo_done.append((cv.h, cv.n))
                        else:
                            @blk.vector
                            def _(vector, s_=s_, pb4=pb4, pbk=pbk):
                                vector.wait_ge(cs.h, exp_done[s_])
                                if s_ >= 4:
                                    vector.wait_ge(ct.h, pva_done[s_ - 4])
                                elif A["prev_pv_last"] is not None:
                                    vector.wait_ge(ct.h, A["prev_pv_last"])
                                cv.inc(vector.tensor_copy(phi[pb4][:], p32[pbk][:]))
                            phi_done.append((cv.h, cv.n))

                            @blk.gpsimd
                            def _(gpsimd, s_=s_, pb4=pb4, pbk=pbk, pd=phi_done[-1]):
                                gpsimd.wait_ge(pd[0], pd[1])
                                if s_ >= 4:
                                    gpsimd.wait_ge(ct.h, pvb_done[s_ - 4])
                                cg.inc(gpsimd.tensor_sub(plo[pb4][:], p32[pbk][:],
                                                         phi[pb4][:]))
                            plo_done.append((cg.h, cg.n))

                        if kc >= 1:
                            emit_pva(h, i, s_ - 1, phi_done, pva_done)
                        if kc >= 3:
                            emit_pvb(h, i, s_ - 3, plo_done, pvb_done)
                        if kc == 4 and A.get("pend_fb"):
                            ph, pq, pt1 = A["pend_fb"]
                            A["pend_fb"] = None
                            finalize_b(ph, pq, pt1)

                    emit_pva(h, i, qh * 32 + 31, phi_done, pva_done)
                    for sx in (29, 30, 31):
                        emit_pvb(h, i, qh * 32 + sx, plo_done, pvb_done)

                    @blk.scalar
                    def _(scalar, recs=tuple(scalar_recs)):
                        for (ctv, p32w, pbk2, pb42, ev, phw) in recs:
                            scalar.wait_ge(ct.h, ctv)
                            if ev:
                                if phw is not None:
                                    scalar.wait_ge(ct.h, phw)
                                op = scalar.activation(out=phi[pb42][:], in_=ps_s[pbk2][:],
                                                       func=AF.Exp, scale=0.125,
                                                       bias=nshift[:])
                                op.then_inc(cs.h, 1)
                                cs.n += 1
                            if p32w is not None:
                                scalar.wait_ge(p32w[0], p32w[1])
                            op = scalar.activation(out=p32[pbk2][:], in_=ps_s[pbk2][:],
                                                   func=AF.Exp, scale=0.125, bias=nshift[:])
                            op.then_inc(cs.h, 1)
                            cs.n += 1
                    assert cs.n == exp_done[qh * 32 + 31], (cs.n, exp_done[qh * 32 + 31])
                    t1v = finalize_a(h, qh, pvb_done)
                    if h % 2 == 1 and qh == 1:
                        finalize_b(h, qh, t1v)
                    else:
                        A["pend_fb"] = (h, qh, t1v)

                scaled = A["o_free"]

                head_free[h] = (ct.h, pvb_done[-1])
                A["prev_exp_last"] = exp_done[-1]
                A["prev_vhl_last"] = plo_done[-1]
                A["prev_pv_last"] = pvb_done[-1]

                if h % 2 == 1:
                    mv_done = ld_mv.n
                    pair = h // 2

                    @blk.vector
                    def _(vector, mv_done=mv_done, pair=pair, h=h):
                        vector.wait_ge(ld_mv.h, mv_done)
                        if pair >= 1:
                            vector.wait_ge(ct.h, A["apair_free"])
                        q2 = (h // 2) % 2
                        vector.wait_ge(ld_pw[q2].h, ld_pw[q2].n)
                        vector.tensor_copy(pwh[q2][:], pw_sb[q2][:])
                        vector.tensor_copy(aph[:], apair[:])
                        vector.drain()
                        vector.tensor_sub(pwl[q2][:], pw_sb[q2][:], pwh[q2][:])
                        cv.inc(vector.tensor_sub(apl[:], apair[:], aph[:]))
                    ap_split = cv.n
                    pw_split_by_pair[pair] = cv.n

                    pr_mm = []
                    pr_ev = []
                    for idx in range(32):
                        dout, qq = divmod(idx, 4)
                        bank = ps_s[(idx % 4) // 2]
                        slc = (idx % 2) * 512

                        @blk.tensor
                        def _(tensor, idx=idx, dout=dout, qq=qq, bank=bank, slc=slc,
                              ap_split=ap_split, h=h):
                            if idx == 0:
                                tensor.wait_ge(cv.h, ap_split)
                                tensor.wait_ge(cv.h, pw_split_by_pair[h // 2])
                            if idx >= 4:
                                tensor.wait_ge(cv.h, pr_ev[idx - 4])
                            q2 = (h // 2) % 2
                            wsl = slice(dout * 128, (dout + 1) * 128)
                            asl = slice(qq * 512, (qq + 1) * 512)
                            tensor.matmul(bank[:, slc:slc + 512], pwh[q2][:, wsl], aph[:, asl],
                                          start=True, stop=False)
                            tensor.matmul(bank[:, slc:slc + 512], pwh[q2][:, wsl], apl[:, asl],
                                          start=False, stop=False)
                            ct.inc(tensor.matmul(bank[:, slc:slc + 512], pwl[q2][:, wsl],
                                                 aph[:, asl], start=False, stop=True))
                        pr_mm.append(ct.n)

                        @blk.vector
                        def _(vector, idx=idx, dout=dout, qq=qq, bank=bank, slc=slc, pair=pair):
                            vector.wait_ge(ct.h, pr_mm[idx])
                            dst2 = projacc[:, dout * NQ + qq * 512: dout * NQ + (qq + 1) * 512]
                            if pair == 0:
                                cv.inc(vector.tensor_copy(dst2, bank[:, slc:slc + 512]))
                            else:
                                cv.inc(vector.tensor_tensor(out=dst2, in0=bank[:, slc:slc + 512],
                                                            in1=dst2, op=ALU.add))
                        pr_ev.append(cv.n)

                    proj_ev_by_pair[pair] = pr_ev[-1]
                    A["proj_done"] = pr_ev[-1]
                    A["ps_s_free"] = pr_ev[-1]
                    A["apair_free"] = pr_mm[-1]
                    A["apair_split"] = ap_split

            if debug in ("attn", "attn1", "attnonly"):
                final_proj = A["proj_done"]

                @blk.gpsimd
                def _(gpsimd):
                    gpsimd.wait_ge(cv.h, final_proj)
                    for ci in range(DC):
                        gpsimd.dma_start(out=AP_s[ci],
                                         in_=projacc[:, ci * NQ:(ci + 1) * NQ]).then_inc(st.h, 16)
                    st.n += 16 * DC
                    gpsimd.wait_ge(st.h, st.n)

        if debug in ("attn", "attn1", "attnonly"):
            return nc

        with contextlib.ExitStack() as p:
            BFF = mybir.dt.bfloat16
            wsel = p.enter_context(nc.sbuf_tensor("wsel", [128, D], F32))
            w1h = p.enter_context(nc.sbuf_tensor("w1h", [128, DC * D], BFF))
            w1l = p.enter_context(nc.sbuf_tensor("w1l", [128, DC * D], BFF))
            w2h = p.enter_context(nc.sbuf_tensor("w2h", [128, DC * D], BFF))
            w2l = p.enter_context(nc.sbuf_tensor("w2l", [128, DC * D], BFF))
            hbuf = p.enter_context(nc.sbuf_tensor("hbuf", [128, 4 * 512], F32))
            hh = [p.enter_context(nc.sbuf_tensor(f"hh{i}", [128, DC * 512], BFF)) for i in range(2)]
            hl = [p.enter_context(nc.sbuf_tensor(f"hl{i}", [128, DC * 512], BFF)) for i in range(2)]
            wtile = [p.enter_context(nc.sbuf_tensor(f"wtile{i}", [128, D], F32)) for i in range(2)]
            xnh8 = p.enter_context(nc.sbuf_tensor("xnh8", [128, DC * 512], BFF))
            xnl8 = p.enter_context(nc.sbuf_tensor("xnl8", [128, DC * 512], BFF))
            btile = p.enter_context(nc.sbuf_tensor("btile", [128, E], F32))
            b2s = p.enter_context(nc.sbuf_tensor("b2s", [128, 2 * E], F32))
            pb_t = p.enter_context(nc.sbuf_tensor("pb_t", [128, DC], F32))
            ps_f = [p.enter_context(nc.psum_tensor(f"ps_f{i}", [128, 512], F32)) for i in range(DC)]

            blk = p.enter_context(nc.Block("ffn"))

            @blk.sync
            def _(sync):
                sync.dma_start(out=pb_t[:], in_=bproj.rearrange("(j p) -> p j", p=128),
                               allow_slow_non_contiguous=True).then_inc(ld_bt.h, 16)
            ld_bt.n += 16
            pb_ld = ld_bt.n

            for bi, (bsrc, bdst) in enumerate(((b1, b1sel), (b2, b2s))):
                for e in range(E):
                    @blk.sync
                    def _(sync, bsrc=bsrc, e=e):
                        sync.wait_ge(cv.h, cv.n)
                        sync.dma_start(out=btile[:],
                                       in_=bsrc[e].rearrange("(j p) -> p j", p=128),
                                       allow_slow_non_contiguous=True).then_inc(ld_bt.h, 16)
                    ld_bt.n += 16
                    bld = ld_bt.n

                    @blk.vector
                    def _(vector, bdst=bdst, e=e, bld=bld):
                        vector.wait_ge(ld_bt.h, bld)
                        for k, msk in ((0, m1b), (1, m2b)):
                            dst2 = bdst[:, k * E:(k + 1) * E]
                            if e == 0:
                                op = vector.tensor_scalar(out=dst2, in0=btile[:],
                                                          scalar1=msk[:, e:e + 1], scalar2=None,
                                                          op0=ALU.mult)
                            else:
                                op = vector.scalar_tensor_tensor(out=dst2, in0=btile[:],
                                                                 scalar=msk[:, e:e + 1], in1=dst2,
                                                                 op0=ALU.mult, op1=ALU.add)
                        vector.drain()
                        cv.inc(op)

            @blk.vector
            def _(vector, full=None):
                vector.wait_ge(ld_bt.h, ld_bt.n)
                vector.drain()
                vector.tensor_tensor(out=bias_all[:], in0=b2s[:, 0:E], in1=b2s[:, E:2 * E],
                                     op=ALU.add)
                vector.drain()
                cv.inc(vector.scalar_tensor_tensor(out=bias_all[:], in0=bias_all[:], scalar=0.5,
                                                   in1=pb_t[:], op0=ALU.mult, op1=ALU.add))

            wsel_marks = []
            sel_ci = {}
            wfree = {"w1": None, "w2": None}
            R = {"gelu_last": None,
                 "oev_last": None,
                 "omm_par": [None, None],
                 "hbuf_free": None,
                 "xn_free": None}

            def emit_h(k, t, w, my_xld):
                pg = R["gelu_last"]

                @blk.tensor
                def _(tensor, k=k, t=t, w=w, my_xld=my_xld, pg=pg):
                    tensor.wait_ge(ld_xn[0].h, my_xld)
                    if pg is not None:
                        tensor.wait_ge(cs.h, pg)
                    for ci in range(DC):
                        if t == 0 and w == 0:
                            tensor.wait_ge(cg.h, sel_ci[(k, "w1", ci)])
                        xsl = slice(ci * 512, (ci + 1) * 512)
                        for dout in range(4 * w, 4 * w + 4):
                            wsl = slice(ci * D + dout * 128, ci * D + (dout + 1) * 128)
                            tensor.matmul(ps_f[dout % 4][:], w1h[:, wsl], xnh8[:, xsl],
                                          start=(ci == 0), stop=False)
                            tensor.matmul(ps_f[dout % 4][:], w1h[:, wsl], xnl8[:, xsl],
                                          start=False, stop=False)
                            op = tensor.matmul(ps_f[dout % 4][:], w1l[:, wsl], xnh8[:, xsl],
                                               start=False, stop=(ci == DC - 1))
                    ct.inc(op)
                return ct.n

            def emit_gelu(k, t, w, hmmv):
                hhfree = R["omm_par"][t % 2]
                hbfree = R["hbuf_free"]

                @blk.scalar
                def _(scalar, k=k, t=t, w=w, hmmv=hmmv, hhfree=hhfree, hbfree=hbfree):
                    scalar.wait_ge(ct.h, hmmv)
                    if hhfree is not None:
                        scalar.wait_ge(ct.h, hhfree)
                    if hbfree is not None:
                        scalar.wait_ge(cv.h, hbfree)
                    for dj, dout in enumerate(range(4 * w, 4 * w + 4)):
                        b = b1sel[:, k * E + dout:k * E + dout + 1]
                        scalar.activation(out=hh[t % 2][:, dout * 512:(dout + 1) * 512],
                                          in_=ps_f[dout % 4][:], func=AF.Gelu, bias=b)
                        op = scalar.activation(out=hbuf[:, dj * 512:(dj + 1) * 512],
                                               in_=ps_f[dout % 4][:], func=AF.Gelu, bias=b)
                    cs.inc(op)
                R["gelu_last"] = cs.n
                return cs.n

            def emit_sub(k, t, w, gv):
                hlfree = R["omm_par"][t % 2]

                @blk.vector
                def _(vector, t=t, w=w, gv=gv, hlfree=hlfree):
                    vector.wait_ge(cs.h, gv)
                    if hlfree is not None:
                        vector.wait_ge(ct.h, hlfree)
                    for dj, dout in enumerate(range(4 * w, 4 * w + 4)):
                        op = vector.tensor_sub(hl[t % 2][:, dout * 512:(dout + 1) * 512],
                                               hbuf[:, dj * 512:(dj + 1) * 512],
                                               hh[t % 2][:, dout * 512:(dout + 1) * 512])
                    cv.inc(op)
                R["hbuf_free"] = cv.n
                return cv.n

            def emit_o(k, t, w, subv):
                oevfree = R["oev_last"]

                @blk.tensor
                def _(tensor, k=k, t=t, w=w, subv=subv, oevfree=oevfree):
                    tensor.wait_ge(cv.h, subv)
                    if oevfree is not None:
                        tensor.wait_ge(cv.h, oevfree)
                    for ci2 in range(DC):
                        if t == 0 and w == 0:
                            tensor.wait_ge(cg.h, sel_ci[(k, "w2", ci2)])
                        hsl = slice(ci2 * 512, (ci2 + 1) * 512)
                        for dout in range(4 * w, 4 * w + 4):
                            wsl = slice(ci2 * D + dout * 128, ci2 * D + (dout + 1) * 128)
                            tensor.matmul(ps_f[4 + dout % 4][:], w2h[:, wsl],
                                          hh[t % 2][:, hsl], start=(ci2 == 0), stop=False)
                            tensor.matmul(ps_f[4 + dout % 4][:], w2h[:, wsl],
                                          hl[t % 2][:, hsl], start=False, stop=False)
                            op = tensor.matmul(ps_f[4 + dout % 4][:], w2l[:, wsl],
                                               hh[t % 2][:, hsl], start=False,
                                               stop=(ci2 == DC - 1))
                    ct.inc(op)
                R["omm_par"][t % 2] = ct.n
                return ct.n

            def emit_oev(k, t, w, ommv):
                @blk.vector
                def _(vector, t=t, w=w, ommv=ommv):
                    vector.wait_ge(ct.h, ommv)
                    for dout in range(4 * w, 4 * w + 4):
                        dst2 = projacc[:, dout * NQ + t * 512: dout * NQ + (t + 1) * 512]
                        op = vector.scalar_tensor_tensor(out=dst2, in0=ps_f[4 + dout % 4][:],
                                                         scalar=0.5, in1=dst2,
                                                         op0=ALU.mult, op1=ALU.add)
                    cv.inc(op)
                R["oev_last"] = cv.n
                return cv.n

            for k in range(2):
                msk = m1b if k == 0 else m2b
                for wsrc, wdh, wdl, tag in ((W1, w1h, w1l, "w1"), (W2, w2h, w2l, "w2")):
                    for ci in range(DC):
                        for e in range(E):
                            tidx = len(wsel_marks) % 2

                            @blk.sync
                            def _(sync, wsrc=wsrc, e=e, ci=ci, tidx=tidx):
                                if len(wsel_marks) >= 2:
                                    sync.wait_ge(cv.h, wsel_marks[-2])
                                sync.dma_start(out=wtile[tidx][:],
                                               in_=wsrc[e, ci * 128:(ci + 1) * 128, :]).then_inc(ld_wt[tidx].h, 16)
                            ld_wt[tidx].n += 16
                            wld = ld_wt[tidx].n
                            prev_sub = max(sel_ci.values()) if sel_ci else None

                            @blk.vector
                            def _(vector, wdh=wdh, e=e, ci=ci, tidx=tidx, wld=wld,
                                  msk=msk, tag=tag, prev_sub=prev_sub):
                                vector.wait_ge(ld_wt[tidx].h, wld)
                                if e == 0 and prev_sub is not None:
                                    vector.wait_ge(cg.h, prev_sub)
                                vector.drain()
                                if e == 0:
                                    op = vector.tensor_scalar(out=wsel[:], in0=wtile[tidx][:],
                                                              scalar1=msk[:, e:e + 1],
                                                              scalar2=None, op0=ALU.mult)
                                else:
                                    op = vector.scalar_tensor_tensor(out=wsel[:], in0=wtile[tidx][:],
                                                                     scalar=msk[:, e:e + 1],
                                                                     in1=wsel[:], op0=ALU.mult,
                                                                     op1=ALU.add)
                                if e == E - 1:
                                    vector.drain()
                                    if ci == 0 and wfree[tag] is not None:
                                        vector.wait_ge(ct.h, wfree[tag])
                                    op = vector.tensor_copy(wdh[:, ci * D:(ci + 1) * D],
                                                            wsel[:])
                                cv.inc(op)
                            wsel_marks.append(cv.n)

                            if e == E - 1:
                                hi_done = cv.n

                                @blk.gpsimd
                                def _(gpsimd, ci=ci, hi_done=hi_done, wdh=wdh, wdl=wdl):
                                    gpsimd.wait_ge(cv.h, hi_done)
                                    cg.inc(gpsimd.tensor_sub(wdl[:, ci * D:(ci + 1) * D],
                                                             wsel[:],
                                                             wdh[:, ci * D:(ci + 1) * D]))
                                sel_ci[(k, tag, ci)] = cg.n

                hmmd = {}
                subd = {}
                ommd = {}
                for t in range(4):
                    xf = R["xn_free"]

                    @blk.scalar
                    def _(scalar, t=t, xf=xf):
                        if xf is not None:
                            scalar.wait_ge(ct.h, xf)
                        for ci in range(DC):
                            scalar.dma_start(out=xnh8[:, ci * 512:(ci + 1) * 512],
                                             in_=XNH_s[ci, :, t * 512:(t + 1) * 512]).then_inc(ld_xn[0].h, 16)
                            scalar.dma_start(out=xnl8[:, ci * 512:(ci + 1) * 512],
                                             in_=XNL_s[ci, :, t * 512:(t + 1) * 512]).then_inc(ld_xn[0].h, 16)
                    ld_xn[0].n += 256
                    my_xld = ld_xn[0].n

                    hmmd[(t, 0)] = emit_h(k, t, 0, my_xld)
                    if t >= 1:
                        ommd[(t - 1, 0)] = emit_o(k, t - 1, 0, subd[(t - 1, 1)])
                    gv0 = emit_gelu(k, t, 0, hmmd[(t, 0)])
                    subd[(t, 0)] = emit_sub(k, t, 0, gv0)
                    hmmd[(t, 1)] = emit_h(k, t, 1, my_xld)
                    R["xn_free"] = hmmd[(t, 1)]
                    if t >= 1:
                        emit_oev(k, t - 1, 0, ommd[(t - 1, 0)])
                        ommd[(t - 1, 1)] = emit_o(k, t - 1, 1, subd[(t - 1, 1)])
                        emit_oev(k, t - 1, 1, ommd[(t - 1, 1)])
                    gv1 = emit_gelu(k, t, 1, hmmd[(t, 1)])
                    subd[(t, 1)] = emit_sub(k, t, 1, gv1)
                ommd[(3, 0)] = emit_o(k, 3, 0, subd[(3, 1)])
                emit_oev(k, 3, 0, ommd[(3, 0)])
                ommd[(3, 1)] = emit_o(k, 3, 1, subd[(3, 1)])
                emit_oev(k, 3, 1, ommd[(3, 1)])
                wfree["w1"] = hmmd[(3, 1)]
                wfree["w2"] = ommd[(3, 1)]

        with contextlib.ExitStack() as p:
            xqb = [p.enter_context(nc.sbuf_tensor(f"xqb{i}", [128, D], F32)) for i in range(2)]
            outb = [p.enter_context(nc.sbuf_tensor(f"outb{i}", [128, D], F32)) for i in range(2)]
            ps_c = [p.enter_context(nc.psum_tensor(f"ps_c{i}", [128, 512], F32)) for i in range(2)]

            blk = p.enter_context(nc.Block("fin"))

            @blk.vector
            def _(vector):
                for dout in range(DC):
                    op = vector.tensor_scalar(out=projacc[:, dout * NQ:(dout + 1) * NQ],
                                              in0=projacc[:, dout * NQ:(dout + 1) * NQ],
                                              scalar1=bias_all[:, dout:dout + 1], scalar2=None,
                                              op0=ALU.add)
                cv.inc(op)
            bias_done = cv.n

            ev_marks = []
            st_marks = []
            for tc in range(16):
                i = tc % 2

                @blk.sync
                def _(sync, tc=tc, i=i):
                    if tc >= 2:
                        sync.wait_ge(cv.h, ev_marks[tc - 2])
                    sync.dma_start(out=xqb[i][:],
                                   in_=x_q[tc * 128:(tc + 1) * 128, :]).then_inc(ld_xq[i].h, 16)
                ld_xq[i].n += 16
                xld = ld_xq[i].n

                @blk.tensor
                def _(tensor, tc=tc, bias_done=bias_done):
                    if tc == 0:
                        tensor.wait_ge(cv.h, bias_done)
                    if tc >= 1:
                        tensor.wait_ge(cv.h, ev_marks[tc - 1])
                    for half in range(2):
                        for q in range(4):
                            ci = half * 4 + q
                            op = tensor.transpose(ps_c[half][:, q * 128:(q + 1) * 128],
                                                  projacc[:, ci * NQ + tc * 128: ci * NQ + (tc + 1) * 128],
                                                  ident[:])
                    ct.inc(op)
                tdn = ct.n

                @blk.vector
                def _(vector, tc=tc, i=i, xld=xld, tdn=tdn):
                    vector.wait_ge(ct.h, tdn)
                    vector.wait_ge(ld_xq[i].h, xld)
                    if tc >= 2:
                        vector.wait_ge(st_o[i].h, st_marks[tc - 2])
                    for half in range(2):
                        op = vector.tensor_tensor(out=outb[i][:, half * 512:(half + 1) * 512],
                                                  in0=ps_c[half][:],
                                                  in1=xqb[i][:, half * 512:(half + 1) * 512],
                                                  op=ALU.add)
                    cv.inc(op)
                ev_marks.append(cv.n)

                @blk.gpsimd
                def _(gpsimd, tc=tc, i=i, evm=ev_marks[-1]):
                    gpsimd.wait_ge(cv.h, evm)
                    gpsimd.dma_start(out=out[tc * 128:(tc + 1) * 128, :],
                                     in_=outb[i][:]).then_inc(st_o[i].h, 16)
                st_o[i].n += 16
                st_marks.append(st_o[i].n)

            @blk.gpsimd
            def _(gpsimd):
                gpsimd.wait_ge(st_o[0].h, st_o[0].n)
                gpsimd.wait_ge(st_o[1].h, st_o[1].n)

    return nc


_NC_CACHE = {}


def kernel(L0_attn_x, L0_attn_qkv_w, L0_attn_qkv_b, L0_attn_proj_w, L0_attn_proj_b,
           L0_ffn_Wr, L0_ffn_br, L0_ffn_W1, L0_ffn_b1, L0_ffn_W2, L0_ffn_b2):
    x = np.ascontiguousarray(np.asarray(L0_attn_x, dtype=np.float32))
    qkvT = np.ascontiguousarray(np.asarray(L0_attn_qkv_w, dtype=np.float32).T).reshape(DC, 128, QKVD)
    projT = np.ascontiguousarray(np.asarray(L0_attn_proj_w, dtype=np.float32).T).reshape(DC, 128, D)
    wr_t = np.ascontiguousarray(np.asarray(L0_ffn_Wr, dtype=np.float32).reshape(DC, 128, E).transpose(1, 0, 2).reshape(128, DC * E))
    bqkv = np.ascontiguousarray(np.asarray(L0_attn_qkv_b, dtype=np.float32))
    bproj = np.ascontiguousarray(np.asarray(L0_attn_proj_b, dtype=np.float32))
    br = np.ascontiguousarray(np.asarray(L0_ffn_br, dtype=np.float32))
    W1 = np.ascontiguousarray(np.asarray(L0_ffn_W1, dtype=np.float32))
    W2 = np.ascontiguousarray(np.asarray(L0_ffn_W2, dtype=np.float32))
    b1 = np.ascontiguousarray(np.asarray(L0_ffn_b1, dtype=np.float32))
    b2 = np.ascontiguousarray(np.asarray(L0_ffn_b2, dtype=np.float32))

    if "nc" not in _NC_CACHE:
        _NC_CACHE["nc"] = build_nc()
    nc = _NC_CACHE["nc"]

    in_maps = []
    for c in range(8):
        b, qh = divmod(c, 2)
        xp = np.ascontiguousarray(
            np.concatenate([x[b, qh * NQ:(qh + 1) * NQ],
                            x[b, (1 - qh) * NQ:(2 - qh) * NQ]], axis=0))
        in_maps.append({
            "x_kv": xp,
            "x_q": np.ascontiguousarray(x[b, qh * NQ:(qh + 1) * NQ]),
            "rsel": np.array([[1.0 - qh, float(qh)]], dtype=np.float32),
            "wqkv": qkvT, "bqkv": bqkv, "wproj": projT, "bproj": bproj,
            "wr": wr_t, "br": br, "W1": W1, "W2": W2, "b1": b1, "b2": b2,
        })
    res = run_bass_kernel_spmd(nc, in_maps, core_ids=list(range(8)))
    outp = np.empty((B, NTOK, D), dtype=np.float32)
    for c in range(8):
        b, qh = divmod(c, 2)
        outp[b, qh * NQ:(qh + 1) * NQ] = res.results[c]["out"]
    return outp


# version 19
# speedup vs baseline: 1.0384x; 1.0150x over previous
import contextlib
import numpy as np

import concourse.bass as bass
import concourse.mybir as mybir
from concourse.bass_utils import run_bass_kernel_spmd

F32 = mybir.dt.float32
I32 = mybir.dt.int32
AF = mybir.ActivationFunctionType
ALU = mybir.AluOpType
AX = mybir.AxisListType

B, NTOK, NQ, D, H, HD, E = 4, 4096, 2048, 1024, 16, 64, 8
DC = D // 128
QKVD = 3 * D
SHIFT = 6.0


class Ctr:
    def __init__(self, ctx, nc, name):
        self.h = ctx.enter_context(nc.semaphore(name))
        self.n = 0

    def inc(self, op, k=1):
        op.then_inc(self.h, k)
        self.n += k
        return self.n


def build_nc(debug=None):
    nc = bass.Bass("TRN2", target_bir_lowering=False)

    x_kv = nc.dram_tensor("x_kv", [NTOK, D], F32, kind="ExternalInput").ap()
    x_q = nc.dram_tensor("x_q", [NQ, D], F32, kind="ExternalInput").ap()
    wqkv = nc.dram_tensor("wqkv", [DC, 128, QKVD], F32, kind="ExternalInput").ap()
    bqkv = nc.dram_tensor("bqkv", [QKVD], F32, kind="ExternalInput").ap()
    wproj = nc.dram_tensor("wproj", [DC, 128, D], F32, kind="ExternalInput").ap()
    bproj = nc.dram_tensor("bproj", [D], F32, kind="ExternalInput").ap()
    wr = nc.dram_tensor("wr", [128, DC * E], F32, kind="ExternalInput").ap()
    br = nc.dram_tensor("br", [E], F32, kind="ExternalInput").ap()
    W1 = nc.dram_tensor("W1", [E, D, D], F32, kind="ExternalInput").ap()
    W2 = nc.dram_tensor("W2", [E, D, D], F32, kind="ExternalInput").ap()
    b1 = nc.dram_tensor("b1", [E, D], F32, kind="ExternalInput").ap()
    b2 = nc.dram_tensor("b2", [E, D], F32, kind="ExternalInput").ap()
    out = nc.dram_tensor("out", [NQ, D], F32, kind="ExternalOutput").ap()

    sk = "ExternalOutput" if debug else "Internal"
    ski = "ExternalInput" if debug == "attnonly" else sk
    BFD = mybir.dt.bfloat16
    QH_s = nc.dram_tensor("QH_s", [DC, 128, NQ], BFD, kind=ski).ap()
    QL_s = nc.dram_tensor("QL_s", [DC, 128, NQ], BFD, kind=ski).ap()
    KH_s = nc.dram_tensor("KH_s", [DC, 128, NTOK], BFD, kind=ski).ap()
    KL_s = nc.dram_tensor("KL_s", [DC, 128, NTOK], BFD, kind=ski).ap()
    VH_s = nc.dram_tensor("VH_s", [NTOK, D], BFD, kind=ski).ap()
    VL_s = nc.dram_tensor("VL_s", [NTOK, D], BFD, kind=ski).ap()
    rsel = nc.dram_tensor("rsel", [1, 2], F32, kind="ExternalInput").ap()
    XNT_s = nc.dram_tensor("XNT_s", [DC, 128, NQ], F32, kind=ski).ap()
    XNH_s = nc.dram_tensor("XNH_s", [DC, 128, NQ], mybir.dt.bfloat16, kind=ski).ap()
    XNL_s = nc.dram_tensor("XNL_s", [DC, 128, NQ], mybir.dt.bfloat16, kind=ski).ap()
    MB_s = nc.dram_tensor("MB_s", [2, 128, E], F32, kind=sk).ap()
    AP_s = nc.dram_tensor("AP_s", [DC, 128, NQ], F32, kind=sk).ap()

    g = contextlib.ExitStack()
    with g:
        ident = g.enter_context(nc.sbuf_tensor("ident", [128, 128], F32))
        iot = g.enter_context(nc.sbuf_tensor("iot", [128, 128], I32))
        ones = g.enter_context(nc.sbuf_tensor("ones", [128, 128], F32))
        m1b = g.enter_context(nc.sbuf_tensor("m1b", [128, E], F32))
        m2b = g.enter_context(nc.sbuf_tensor("m2b", [128, E], F32))
        qb_sb = g.enter_context(nc.sbuf_tensor("qb_sb", [128, 3 * DC], F32))
        bvb = g.enter_context(nc.sbuf_tensor("bvb", [128, D], F32))
        bias_all = g.enter_context(nc.sbuf_tensor("bias_all", [128, DC], F32))
        b1sel = g.enter_context(nc.sbuf_tensor("b1sel", [128, 2 * DC], F32))

        ld = Ctr(g, nc, "ld")
        st = Ctr(g, nc, "st")
        ld_x = [Ctr(g, nc, f"ld_x{i}") for i in range(2)]
        ld_hd = [Ctr(g, nc, f"ld_hd{i}") for i in range(2)]
        ld_pw = [Ctr(g, nc, f"ld_pw{i}") for i in range(2)]
        ld_bt = Ctr(g, nc, "ld_bt")
        ld_wt = [Ctr(g, nc, f"ld_wt{i}") for i in range(3)]
        ld_xn = [Ctr(g, nc, f"ld_xn{i}") for i in range(2)]
        ld_xq = [Ctr(g, nc, f"ld_xq{i}") for i in range(2)]
        st_o = [Ctr(g, nc, f"st_o{i}") for i in range(2)]
        ld_mv = Ctr(g, nc, "ld_mv")
        cv = Ctr(g, nc, "cv")
        cs = Ctr(g, nc, "cs")
        ct = Ctr(g, nc, "ct")
        cg = Ctr(g, nc, "cg")

        def phase_p0():
            with contextlib.ExitStack() as p:
                BFP = mybir.dt.bfloat16
                wq_st = p.enter_context(nc.sbuf_tensor("wq_st", [128, QKVD], F32))
                wq_hi = p.enter_context(nc.sbuf_tensor("wq_hi", [128, DC * QKVD], BFP))
                wq_lo = p.enter_context(nc.sbuf_tensor("wq_lo", [128, DC * QKVD], BFP))
                xnT_hi = p.enter_context(nc.sbuf_tensor("xnT_hi", [128, DC * 512], BFP))
                xnT_lo = p.enter_context(nc.sbuf_tensor("xnT_lo", [128, DC * 512], BFP))
                wr_hi = p.enter_context(nc.sbuf_tensor("wr_hi", [128, DC * E], BFP))
                wr_lo = p.enter_context(nc.sbuf_tensor("wr_lo", [128, DC * E], BFP))
                xbuf = [p.enter_context(nc.sbuf_tensor(f"xbuf{i}", [128, D], F32)) for i in range(2)]
                x2buf = p.enter_context(nc.sbuf_tensor("x2buf", [128, D], F32))
                xnbuf = [p.enter_context(nc.sbuf_tensor(f"xnbuf{i}", [128, D], F32)) for i in range(2)]
                ev = [p.enter_context(nc.sbuf_tensor(f"ev{i}", [128, 512], F32)) for i in range(2)]
                ehi = [p.enter_context(nc.sbuf_tensor(f"ehi{i}", [128, 512], BFP)) for i in range(2)]
                elo = [p.enter_context(nc.sbuf_tensor(f"elo{i}", [128, 512], BFP)) for i in range(2)]
                vout = [p.enter_context(nc.sbuf_tensor(f"vout{i}", [128, D], F32)) for i in range(2)]
                vhi = [p.enter_context(nc.sbuf_tensor(f"vhi{i}", [128, D], BFP)) for i in range(2)]
                vlo = [p.enter_context(nc.sbuf_tensor(f"vlo{i}", [128, D], BFP)) for i in range(2)]
                stat = p.enter_context(nc.sbuf_tensor("stat", [128, 16], F32))
                wr_sb = p.enter_context(nc.sbuf_tensor("wr_sb", [128, DC * E], F32))
                sc8 = p.enter_context(nc.sbuf_tensor("sc8", [1, E], F32))
                sab = p.enter_context(nc.sbuf_tensor("sab", [1, 2 * E], F32))
                rs_sb = p.enter_context(nc.sbuf_tensor("rs_sb", [1, 2], F32))
                mx = p.enter_context(nc.sbuf_tensor("mx", [1, 2], F32))
                mk12 = p.enter_context(nc.sbuf_tensor("mk12", [1, 2 * E], F32))
                bvrow = p.enter_context(nc.sbuf_tensor("bvrow", [1, D], F32))
                ps_qk = [p.enter_context(nc.psum_tensor(f"ps_qk{i}", [128, 512], F32)) for i in range(2)]
                ps_v = [p.enter_context(nc.psum_tensor(f"ps_v{i}", [128, 512], F32)) for i in range(2)]
                ps_tr = [p.enter_context(nc.psum_tensor(f"ps_tr{i}", [128, 512], F32)) for i in range(2)]
                ps_bc = p.enter_context(nc.psum_tensor("ps_bc", [128, 512], F32))

                blk = p.enter_context(nc.Block("p0"))

                @blk.sync
                def _(sync):
                    sync.dma_start(out=wq_st[:], in_=wqkv[0]).then_inc(ld.h, 16)
                    sync.dma_start(out=wr_sb[:], in_=wr).then_inc(ld.h, 16)
                    sync.dma_start(out=qb_sb[:], in_=bqkv.rearrange("(j p) -> p j", p=128),
                                   allow_slow_non_contiguous=True).then_inc(ld.h, 16)
                    sync.dma_start(out=bvrow[:],
                                   in_=bqkv[2 * D:3 * D].rearrange("(o d) -> o d", o=1)).then_inc(ld.h, 16)
                    sync.dma_start(out=sc8[:], in_=br.rearrange("(o e) -> o e", o=1)).then_inc(ld.h, 16)
                    sync.dma_start(out=rs_sb[:], in_=rsel).then_inc(ld.h, 16)
                ld.n += 16 * 6
                pre_ld = ld.n

                wsplit_marks = []
                for ci in range(DC):
                    if ci > 0:
                        @blk.sync
                        def _(sync, ci=ci):
                            sync.wait_ge(cv.h, wsplit_marks[-1])
                            sync.dma_start(out=wq_st[:], in_=wqkv[ci]).then_inc(ld.h, 16)
                        ld.n += 16
                    wld2 = ld.n

                    @blk.vector
                    def _(vector, ci=ci, wld2=wld2):
                        vector.wait_ge(ld.h, wld2)
                        vector.tensor_copy(wq_hi[:, ci * QKVD:(ci + 1) * QKVD], wq_st[:])
                        vector.drain()
                        cv.inc(vector.tensor_sub(wq_lo[:, ci * QKVD:(ci + 1) * QKVD],
                                                 wq_st[:], wq_hi[:, ci * QKVD:(ci + 1) * QKVD]))
                    wsplit_marks.append(cv.n)

                @blk.vector
                def _(vector):
                    vector.wait_ge(ld.h, pre_ld)
                    vector.tensor_copy(wr_hi[:], wr_sb[:])
                    vector.drain()
                    cv.inc(vector.tensor_sub(wr_lo[:], wr_sb[:], wr_hi[:]))
                wsplit_done = cv.n

                @blk.gpsimd
                def _(gpsimd):
                    gpsimd.iota(iot[:], [[1, 128]], base=0, channel_multiplier=-1)
                    cg.inc(gpsimd.memset(ones[:], 1.0))
                g_pre = cg.n

                @blk.vector
                def _(vector):
                    vector.wait_ge(cg.h, g_pre)
                    cv.inc(vector.tensor_scalar(out=ident[:], in0=iot[:], scalar1=0, scalar2=None,
                                                op0=ALU.is_equal))
                ident_ok = cv.n

                @blk.tensor
                def _(tensor):
                    tensor.wait_ge(ld.h, pre_ld)
                    tensor.wait_ge(cv.h, ident_ok)
                    tensor.matmul(ps_v[0][:], ones[0:1, :], bvrow[0:1, 0:512], start=True, stop=True)
                    ct.inc(tensor.matmul(ps_v[1][:], ones[0:1, :], bvrow[0:1, 512:1024],
                                         start=True, stop=True))
                bv_mm = ct.n

                @blk.vector
                def _(vector):
                    vector.wait_ge(ct.h, bv_mm)
                    vector.tensor_copy(bvb[:, 0:512], ps_v[0][:])
                    cv.inc(vector.tensor_copy(bvb[:, 512:1024], ps_v[1][:]))

                state = {"n_ln": 0, "xbuf_free": [None, None],
                         "tr_evac": [(cv.h, wsplit_done)],
                         "qk_evac": [], "v_evacs": [], "qk_sub": [], "v_sub": []}

                def ln_chunk(src_ap, j):
                    i = state["n_ln"] % 2
                    state["n_ln"] += 1

                    @blk.sync
                    def _(sync, i=i, src_ap=src_ap):
                        if state["xbuf_free"][i] is not None:
                            sem, val = state["xbuf_free"][i]
                            sync.wait_ge(sem, val)
                        sync.dma_start(out=xbuf[i][:], in_=src_ap).then_inc(ld_x[i].h, 16)
                    ld_x[i].n += 16
                    my_ld = ld_x[i].n

                    @blk.scalar
                    def _(scalar, i=i, my_ld=my_ld):
                        scalar.wait_ge(ld_x[i].h, my_ld)
                        cs.inc(scalar.activation(out=x2buf[:], in_=xbuf[i][:], func=AF.Square,
                                                 accum_out=stat[:, 4:5]))
                    sq_done = cs.n

                    @blk.vector
                    def _(vector, i=i, my_ld=my_ld, sq_done=sq_done):
                        vector.wait_ge(ld_x[i].h, my_ld)
                        vector.reduce_sum(stat[:, 0:1], xbuf[i][:], axis=AX.X)
                        vector.wait_ge(cs.h, sq_done)
                        vector.drain()
                        vector.tensor_scalar(out=stat[:, 1:2], in0=stat[:, 0:1], scalar1=1.0 / D,
                                             scalar2=None, op0=ALU.mult)
                        vector.tensor_scalar(out=stat[:, 5:6], in0=stat[:, 4:5], scalar1=1.0 / D,
                                             scalar2=None, op0=ALU.mult)
                        vector.drain()
                        vector.scalar_tensor_tensor(out=stat[:, 2:3], in0=stat[:, 1:2],
                                                    scalar=stat[:, 1:2], in1=stat[:, 5:6],
                                                    op0=ALU.mult, op1=ALU.subtract)
                        vector.drain()
                        cv.inc(vector.tensor_scalar(out=stat[:, 2:3], in0=stat[:, 2:3], scalar1=-1.0,
                                                    scalar2=1e-5, op0=ALU.mult, op1=ALU.add))
                    var_done = cv.n

                    @blk.scalar
                    def _(scalar, var_done=var_done):
                        scalar.wait_ge(cv.h, var_done)
                        cs.inc(scalar.activation(out=stat[:, 3:4], in_=stat[:, 2:3], func=AF.Sqrt))
                    sqrt_done = cs.n

                    @blk.vector
                    def _(vector, sqrt_done=sqrt_done):
                        vector.wait_ge(cs.h, sqrt_done)
                        vector.reciprocal(out=stat[:, 6:7], in_=stat[:, 3:4])
                        vector.drain()
                        cv.inc(vector.scalar_tensor_tensor(out=stat[:, 7:8], in0=stat[:, 1:2],
                                                           scalar=-1.0, in1=stat[:, 6:7],
                                                           op0=ALU.mult, op1=ALU.mult))
                    rb_done = cv.n

                    @blk.scalar
                    def _(scalar, i=i, rb_done=rb_done, te2=(state["tr_evac"][-2]
                                                            if len(state["tr_evac"]) >= 2 else None)):
                        scalar.wait_ge(cv.h, rb_done)
                        if te2 is not None:
                            scalar.wait_ge(te2[0], te2[1])
                        cs.inc(scalar.activation(out=xnbuf[i][:], in_=xbuf[i][:], func=AF.Identity,
                                                 scale=stat[:, 6:7], bias=stat[:, 7:8]))
                    ln_done = cs.n
                    state["xbuf_free"][i] = (cs.h, ln_done)

                    @blk.tensor
                    def _(tensor, i=i, ln_done=ln_done, te1=state["tr_evac"][-1]):
                        tensor.wait_ge(cs.h, ln_done)
                        tensor.wait_ge(te1[0], te1[1])
                        for half in range(2):
                            for q in range(4):
                                ci = half * 4 + q
                                op = tensor.transpose(ps_tr[half][:, q * 128:(q + 1) * 128],
                                                      xnbuf[i][:, ci * 128:(ci + 1) * 128], ident[:])
                        ct.inc(op)
                    tp_done = ct.n

                    @blk.scalar
                    def _(scalar, j=j, tp_done=tp_done):
                        scalar.wait_ge(ct.h, tp_done)
                        for half in range(2):
                            dsth = xnT_hi[:].rearrange("p (c t) -> p c t", c=DC)[:, half * 4:half * 4 + 4,
                                                                                j * 128:(j + 1) * 128]
                            op = scalar.activation(out=dsth,
                                                   in_=ps_tr[half][:].rearrange("p (q t) -> p q t", q=4),
                                                   func=AF.Identity)
                        cs.inc(op)
                    hi_m = cs.n

                    @blk.vector
                    def _(vector, j=j, hi_m=hi_m, tp_done=tp_done):
                        vector.wait_ge(ct.h, tp_done)
                        vector.wait_ge(cs.h, hi_m)
                        for half in range(2):
                            dsth = xnT_hi[:].rearrange("p (c t) -> p c t", c=DC)[:, half * 4:half * 4 + 4,
                                                                                j * 128:(j + 1) * 128]
                            dstl = xnT_lo[:].rearrange("p (c t) -> p c t", c=DC)[:, half * 4:half * 4 + 4,
                                                                                j * 128:(j + 1) * 128]
                            op = vector.tensor_sub(dstl,
                                                   ps_tr[half][:].rearrange("p (q t) -> p q t", q=4),
                                                   dsth)
                        cv.inc(op)
                    state["tr_evac"].append((cv.h, cv.n))

                def qk_matmuls(is_q, base_tok, xnT_ready):
                    woff = 0 if is_q else D
                    boff = 0 if is_q else DC
                    dsth = QH_s if is_q else KH_s
                    dstl = QL_s if is_q else KL_s
                    for dout in range(DC):
                        bank = dout % 2

                        @blk.tensor
                        def _(tensor, dout=dout, bank=bank, xnT_ready=xnT_ready, woff=woff):
                            tensor.wait_ge(xnT_ready[0], xnT_ready[1])
                            if len(state["qk_evac"]) >= 2:
                                tensor.wait_ge(cv.h, state["qk_evac"][-2])
                            base = woff + dout * 128
                            for ci in range(DC):
                                wsl = slice(ci * QKVD + base, ci * QKVD + base + 128)
                                xsl = slice(ci * 512, (ci + 1) * 512)
                                tensor.matmul(ps_qk[bank][:], wq_hi[:, wsl], xnT_hi[:, xsl],
                                              start=(ci == 0), stop=False)
                                tensor.matmul(ps_qk[bank][:], wq_hi[:, wsl], xnT_lo[:, xsl],
                                              start=False, stop=False)
                                op = tensor.matmul(ps_qk[bank][:], wq_lo[:, wsl], xnT_hi[:, xsl],
                                                   start=False, stop=(ci == DC - 1))
                            ct.inc(op)
                        mm_done = ct.n

                        @blk.vector
                        def _(vector, bank=bank, mm_done=mm_done, boff=boff, dout=dout,
                              psub=(state["qk_sub"][-2] if len(state["qk_sub"]) >= 2 else None)):
                            vector.wait_ge(ct.h, mm_done)
                            vector.wait_ge(st.h, st.n)
                            if psub is not None:
                                vector.wait_ge(cg.h, psub)
                            vector.tensor_scalar(out=ev[bank][:], in0=ps_qk[bank][:],
                                                 scalar1=qb_sb[:, boff + dout:boff + dout + 1],
                                                 scalar2=None, op0=ALU.add)
                            vector.drain()
                            cv.inc(vector.tensor_copy(ehi[bank][:], ev[bank][:]))
                        state["qk_evac"].append(cv.n)
                        evd = cv.n

                        @blk.gpsimd
                        def _(gpsimd, bank=bank, evd=evd):
                            gpsimd.wait_ge(cv.h, evd)
                            cg.inc(gpsimd.tensor_sub(elo[bank][:], ev[bank][:], ehi[bank][:]))
                        state["qk_sub"].append(cg.n)
                        subd = cg.n

                        @blk.scalar
                        def _(scalar, dout=dout, bank=bank, subd=subd, base_tok=base_tok,
                              dsth=dsth, dstl=dstl):
                            scalar.wait_ge(cg.h, subd)
                            scalar.dma_start(out=dsth[dout, :, base_tok:base_tok + 512],
                                             in_=ehi[bank][:]).then_inc(st.h, 16)
                            scalar.dma_start(out=dstl[dout, :, base_tok:base_tok + 512],
                                             in_=elo[bank][:]).then_inc(st.h, 16)
                        st.n += 32

                def v_matmuls(base_tok, xnT_ready):
                    for t in range(4):
                        vb = t % 2

                        @blk.tensor
                        def _(tensor, t=t, xnT_ready=xnT_ready):
                            tensor.wait_ge(xnT_ready[0], xnT_ready[1])
                            if state["v_evacs"]:
                                tensor.wait_ge(cv.h, state["v_evacs"][-1])
                            for half in range(2):
                                for ci in range(DC):
                                    xsl = slice(ci * 512 + t * 128, ci * 512 + (t + 1) * 128)
                                    wsl = slice(ci * QKVD + 2 * D + half * 512,
                                                ci * QKVD + 2 * D + (half + 1) * 512)
                                    tensor.matmul(ps_v[half][:], xnT_hi[:, xsl], wq_hi[:, wsl],
                                                  start=(ci == 0), stop=False)
                                    tensor.matmul(ps_v[half][:], xnT_hi[:, xsl], wq_lo[:, wsl],
                                                  start=False, stop=False)
                                    op = tensor.matmul(ps_v[half][:], xnT_lo[:, xsl], wq_hi[:, wsl],
                                                       start=False, stop=(ci == DC - 1))
                            ct.inc(op)
                        vmm = ct.n

                        @blk.vector
                        def _(vector, vb=vb, vmm=vmm,
                              psub=(state["v_sub"][-2] if len(state["v_sub"]) >= 2 else None)):
                            vector.wait_ge(ct.h, vmm)
                            vector.wait_ge(st.h, st.n)
                            if psub is not None:
                                vector.wait_ge(cg.h, psub)
                            vector.tensor_tensor(out=vout[vb][:, 0:512], in0=ps_v[0][:],
                                                 in1=bvb[:, 0:512], op=ALU.add)
                            vector.tensor_tensor(out=vout[vb][:, 512:1024], in0=ps_v[1][:],
                                                 in1=bvb[:, 512:1024], op=ALU.add)
                            vector.drain()
                            cv.inc(vector.tensor_copy(vhi[vb][:], vout[vb][:]))
                        state["v_evacs"].append(cv.n)
                        evd = cv.n

                        @blk.gpsimd
                        def _(gpsimd, vb=vb, evd=evd):
                            gpsimd.wait_ge(cv.h, evd)
                            cg.inc(gpsimd.tensor_sub(vlo[vb][:], vout[vb][:], vhi[vb][:]))
                        state["v_sub"].append(cg.n)
                        vsub = cg.n

                        @blk.sync
                        def _(sync, t=t, vb=vb, vsub=vsub, base_tok=base_tok):
                            sync.wait_ge(cg.h, vsub)
                            sync.dma_start(
                                out=VH_s[base_tok + t * 128: base_tok + (t + 1) * 128, :],
                                in_=vhi[vb][:]).then_inc(st.h, 16)
                            sync.dma_start(
                                out=VL_s[base_tok + t * 128: base_tok + (t + 1) * 128, :],
                                in_=vlo[vb][:]).then_inc(st.h, 16)
                        st.n += 32

                def router_mm(xnT_ready, part):
                    off = part * E

                    @blk.tensor
                    def _(tensor, xnT_ready=xnT_ready, off=off):
                        tensor.wait_ge(xnT_ready[0], xnT_ready[1])
                        for ci in range(DC):
                            xc = slice(ci * 512, ci * 512 + 1)
                            wc = slice(ci * E, (ci + 1) * E)
                            tensor.matmul(ps_bc[0:1, off:off + E], xnT_hi[:, xc], wr_hi[:, wc],
                                          start=(ci == 0), stop=False)
                            tensor.matmul(ps_bc[0:1, off:off + E], xnT_hi[:, xc], wr_lo[:, wc],
                                          start=False, stop=False)
                            op = tensor.matmul(ps_bc[0:1, off:off + E], xnT_lo[:, xc],
                                               wr_hi[:, wc], start=False, stop=(ci == DC - 1))
                        ct.inc(op)
                    rt = ct.n

                    @blk.vector
                    def _(vector, rt=rt, off=off):
                        vector.wait_ge(ct.h, rt)
                        cv.inc(vector.tensor_copy(sab[0:1, off:off + E], ps_bc[0:1, off:off + E]))
                    return cv.n

                def router_combine(r2_done):
                    @blk.vector
                    def _(vector, r2_done=r2_done):
                        vector.wait_ge(cv.h, r2_done)
                        vector.wait_ge(ld.h, pre_ld)
                        vector.drain()
                        vector.scalar_tensor_tensor(out=sc8[:], in0=sab[0:1, 0:E],
                                                    scalar=rs_sb[0:1, 0:1], in1=sc8[:],
                                                    op0=ALU.mult, op1=ALU.add)
                        vector.drain()
                        vector.scalar_tensor_tensor(out=sc8[:], in0=sab[0:1, E:2 * E],
                                                    scalar=rs_sb[0:1, 1:2], in1=sc8[:],
                                                    op0=ALU.mult, op1=ALU.add)
                        vector.drain()
                        vector.reduce_max(mx[0:1, 0:1], sc8[:], axis=AX.X)
                        vector.drain()
                        vector.tensor_scalar(out=mk12[0:1, 0:E], in0=sc8[:], scalar1=mx[0:1, 0:1],
                                             scalar2=None, op0=ALU.is_ge)
                        vector.drain()
                        vector.scalar_tensor_tensor(out=sc8[:], in0=mk12[0:1, 0:E], scalar=-1e30,
                                                    in1=sc8[:], op0=ALU.mult, op1=ALU.add)
                        vector.drain()
                        vector.reduce_max(mx[0:1, 1:2], sc8[:], axis=AX.X)
                        vector.drain()
                        cv.inc(vector.tensor_scalar(out=mk12[0:1, E:2 * E], in0=sc8[:],
                                                    scalar1=mx[0:1, 1:2], scalar2=None, op0=ALU.is_ge))
                    msk = cv.n

                    @blk.tensor
                    def _(tensor, msk=msk):
                        tensor.wait_ge(cv.h, msk)
                        tensor.matmul(ps_bc[:, 256:256 + E], ones[0:1, :], mk12[0:1, 0:E],
                                      start=True, stop=True)
                        ct.inc(tensor.matmul(ps_bc[:, 384:384 + E], ones[0:1, :], mk12[0:1, E:2 * E],
                                             start=True, stop=True))
                    bca = ct.n

                    @blk.vector
                    def _(vector, bca=bca):
                        vector.wait_ge(ct.h, bca)
                        vector.tensor_copy(m1b[:], ps_bc[:, 256:256 + E])
                        cv.inc(vector.tensor_copy(m2b[:], ps_bc[:, 384:384 + E]))
                    mdone = cv.n

                    if debug:
                        @blk.gpsimd
                        def _(gpsimd, mdone=mdone):
                            gpsimd.wait_ge(cv.h, mdone)
                            gpsimd.dma_start(out=MB_s[0], in_=m1b[:]).then_inc(st.h, 16)
                            gpsimd.dma_start(out=MB_s[1], in_=m2b[:]).then_inc(st.h, 16)
                        st.n += 32

                r_parts = {}
                for sc in range(8):
                    base_tok = sc * 512
                    for j in range(4):
                        ln_chunk(x_kv[base_tok + j * 128: base_tok + (j + 1) * 128, :], j)
                    xnT_ready = state["tr_evac"][-1]
                    qk_matmuls(False, base_tok, xnT_ready)
                    v_matmuls(base_tok, xnT_ready)
                    if sc < 4:
                        @blk.sync
                        def _(sync, base_tok=base_tok, xnT_ready=xnT_ready):
                            sync.wait_ge(xnT_ready[0], xnT_ready[1])
                            sync.dma_start(
                                out=XNH_s[:, :, base_tok:base_tok + 512].rearrange("c p t -> p c t"),
                                in_=xnT_hi[:].rearrange("p (c t) -> p c t", c=DC)).then_inc(st.h, 16)
                            sync.dma_start(
                                out=XNL_s[:, :, base_tok:base_tok + 512].rearrange("c p t -> p c t"),
                                in_=xnT_lo[:].rearrange("p (c t) -> p c t", c=DC)).then_inc(st.h, 16)
                        st.n += 32
                        qk_matmuls(True, base_tok, xnT_ready)
                    if sc == 0:
                        r_parts[0] = router_mm(xnT_ready, 0)
                    if sc == 4:
                        r_parts[1] = router_mm(xnT_ready, 1)
                        router_combine(r_parts[1])
                    tdone = ct.n

                    @blk.gpsimd
                    def _(gpsimd, tdone=tdone):
                        gpsimd.wait_ge(ct.h, tdone)
                        gpsimd.wait_ge(st.h, st.n)
                        cg.inc(gpsimd.engine_nop())
                    state["tr_evac"].append((cg.h, cg.n))

                @blk.gpsimd
                def _(gpsimd):
                    gpsimd.wait_ge(st.h, st.n)


        if debug == "attnonly":
            with contextlib.ExitStack() as p:
                blk = p.enter_context(nc.Block("pre"))

                @blk.gpsimd
                def _(gpsimd):
                    cg.inc(gpsimd.memset(ones[:], 1.0))
        else:
            phase_p0()

        if debug == "p0":
            return nc

        projacc = g.enter_context(nc.sbuf_tensor("projacc", [128, DC * NQ], F32))

        with contextlib.ExitStack() as p:
            BF = mybir.dt.bfloat16
            kst = [p.enter_context(nc.sbuf_tensor(f"kst{i}", [128, NTOK], BF)) for i in range(2)]
            qab = [p.enter_context(nc.sbuf_tensor(f"qab{i}", [128, NQ], BF)) for i in range(2)]
            qcd = [p.enter_context(nc.sbuf_tensor(f"qcd{i}", [128, NQ], BF)) for i in range(2)]
            vA = [p.enter_context(nc.sbuf_tensor(f"vA{i}", [128, 32 * (HD + 1)], BF)) for i in range(2)]
            vB = [p.enter_context(nc.sbuf_tensor(f"vB{i}", [128, 32 * HD], BF)) for i in range(2)]
            p32 = [p.enter_context(nc.sbuf_tensor(f"p32_{i}", [128, 1024], F32)) for i in range(2)]
            phi = [p.enter_context(nc.sbuf_tensor(f"phi{i}", [128, 1024], BF)) for i in range(4)]
            plo = [p.enter_context(nc.sbuf_tensor(f"plo{i}", [128, 1024], BF)) for i in range(4)]
            rzb = p.enter_context(nc.sbuf_tensor("rzb", [128, 1024], F32))
            t1 = p.enter_context(nc.sbuf_tensor("t1", [64, 1024], F32))
            aout = p.enter_context(nc.sbuf_tensor("aout", [64, 1024], F32))
            apair = p.enter_context(nc.sbuf_tensor("apair", [128, NQ], F32))
            aph = p.enter_context(nc.sbuf_tensor("aph", [128, NQ], BF))
            apl = p.enter_context(nc.sbuf_tensor("apl", [128, NQ], BF))
            pw_sb = [p.enter_context(nc.sbuf_tensor(f"pw_sb{i}", [128, D], F32)) for i in range(2)]
            pwh = [p.enter_context(nc.sbuf_tensor("pwh0", [128, D], BF))] * 2
            pwl = [p.enter_context(nc.sbuf_tensor("pwl0", [128, D], BF))] * 2
            nshift = p.enter_context(nc.sbuf_tensor("nshift", [128, 1], F32))
            ps_s = [p.enter_context(nc.psum_tensor(f"ps_s{i}", [128, 1024], F32)) for i in range(2)]
            ps_ab = [p.enter_context(nc.psum_tensor(f"ps_ab{i}", [65, 512], F32)) for i in range(2)]
            ps_a2 = [p.enter_context(nc.psum_tensor(f"ps_a2{i}", [128, 512], F32)) for i in range(2)]

            blk = p.enter_context(nc.Block("attn"))

            @blk.gpsimd
            def _(gpsimd):
                for i in range(2):
                    gpsimd.memset(
                        vA[i][:].rearrange("p (kc w) -> p kc w", w=HD + 1)[:, :, HD:HD + 1], 1.0)
                cg.inc(gpsimd.memset(nshift[:], -SHIFT))
            vones = cg.n

            head_ld = {}
            pw_split_by_pair = {}
            conv_done = {}
            head_free = {}
            proj_ev_by_pair = {}
            A = {"o_free": None, "proj_done": None, "prev_exp_last": None,
                 "prev_vhl_last": None, "prev_pv_last": None,
                 "apair_free": None, "apair_split": None, "aout_free": None,
                 "ps_s_free": None, "staging_free": None}

            def load_head(h):
                i = h % 2
                b64 = 64 * (h % 2)

                @blk.sync
                def _(sync, h=h, i=i, b64=b64):
                    if h >= 2:
                        sem, val = head_free[h - 2]
                        sync.wait_ge(sem, val)
                    if h == 0:
                        sync.wait_ge(cg.h, vones)
                    sync.dma_start(out=kst[i][0:64, :],
                                   in_=KH_s[h // 2, b64:b64 + 64, :]).then_inc(ld_hd[i].h, 16)
                    sync.dma_start(out=kst[i][64:128, :],
                                   in_=KL_s[h // 2, b64:b64 + 64, :]).then_inc(ld_hd[i].h, 16)
                    for half in range(2):
                        sync.dma_start(out=qab[i][64 * half:64 * half + 64, :],
                                       in_=QH_s[h // 2, b64:b64 + 64, :]).then_inc(ld_hd[i].h, 16)
                        sync.dma_start(out=qcd[i][64 * half:64 * half + 64, :],
                                       in_=QL_s[h // 2, b64:b64 + 64, :]).then_inc(ld_hd[i].h, 16)
                    for kg in range(4):
                        sync.dma_start(
                            out=vA[i][:].rearrange("p (kc w) -> p kc w", w=HD + 1)[:, kg * 8:(kg + 1) * 8, 0:HD],
                            in_=VH_s.rearrange("(kc p) c -> p kc c", p=128)[:, kg * 8:(kg + 1) * 8,
                                                                           h * HD:(h + 1) * HD]).then_inc(ld_hd[i].h, 16)
                        sync.dma_start(
                            out=vB[i][:].rearrange("p (kc w) -> p kc w", w=HD)[:, kg * 8:(kg + 1) * 8, :],
                            in_=VL_s.rearrange("(kc p) c -> p kc c", p=128)[:, kg * 8:(kg + 1) * 8,
                                                                           h * HD:(h + 1) * HD]).then_inc(ld_hd[i].h, 16)
                    if h % 2 == 0:
                        q = h // 2
                        if q >= 2:
                            sync.wait_ge(cv.h, pw_split_by_pair[q - 2])
                        sync.dma_start(out=pw_sb[q % 2][:], in_=wproj[q]).then_inc(ld_pw[q % 2].h, 16)
                ld_hd[i].n += 14 * 16
                if h % 2 == 0:
                    ld_pw[(h // 2) % 2].n += 16
                head_ld[h] = ld_hd[i].n
                conv_done[h] = None


            load_head(0)

            n_heads = 2 if debug == "attn1" else H

            def emit_pva(h, i, s_, phi_done, pva_done):
                qh, kc = divmod(s_, 32)
                pbk = s_ % 4

                @blk.tensor
                def _(tensor, i=i, s_=s_, kc=kc, pbk=pbk):
                    pd = phi_done[s_]
                    tensor.wait_ge(pd[0], pd[1])
                    if kc == 0 and A["o_free"] is not None:
                        tensor.wait_ge(cv.h, A["o_free"])
                    for qq in range(2):
                        tensor.matmul(ps_ab[qq][:],
                                      vA[i][:, kc * (HD + 1):(kc + 1) * (HD + 1)],
                                      phi[pbk][:, qq * 512:(qq + 1) * 512],
                                      start=(kc == 0), stop=False)
                    for qq in range(2):
                        op = tensor.matmul(ps_a2[qq][0:64, :],
                                           vB[i][:, kc * HD:(kc + 1) * HD],
                                           phi[pbk][:, qq * 512:(qq + 1) * 512],
                                           start=(kc == 0), stop=(kc == 31))
                    ct.inc(op)
                pva_done.append(ct.n)

            def emit_pvb(h, i, s_, plo_done, pvb_done):
                qh, kc = divmod(s_, 32)
                pbk = s_ % 4

                @blk.tensor
                def _(tensor, i=i, s_=s_, kc=kc, pbk=pbk):
                    pl = plo_done[s_]
                    tensor.wait_ge(pl[0], pl[1])
                    for qq in range(2):
                        op = tensor.matmul(ps_ab[qq][:],
                                           vA[i][:, kc * (HD + 1):(kc + 1) * (HD + 1)],
                                           plo[pbk][:, qq * 512:(qq + 1) * 512],
                                           start=False, stop=(kc == 31))
                    ct.inc(op)
                pvb_done.append(ct.n)

            def finalize_a(h, qh, pvb_done):
                pvl = pvb_done[qh * 32 + 31]

                @blk.vector
                def _(vector, pvl=pvl):
                    vector.wait_ge(ct.h, pvl)
                    for qq in range(2):
                        vector.reciprocal(out=rzb[64:65, qq * 512:(qq + 1) * 512],
                                          in_=ps_ab[qq][64:65, :])
                    for qq in range(2):
                        op = vector.tensor_copy(t1[:, qq * 512:(qq + 1) * 512],
                                                ps_ab[qq][0:64, :])
                    cv.inc(op)

                @blk.vector
                def _(vector):
                    vector.drain()
                    for qq in range(2):
                        op = vector.tensor_tensor(out=t1[:, qq * 512:(qq + 1) * 512],
                                                  in0=ps_a2[qq][0:64, :],
                                                  in1=t1[:, qq * 512:(qq + 1) * 512], op=ALU.add)
                    cv.inc(op)
                A["o_free"] = cv.n
                return cv.n

            def finalize_b(h, qh, t1_done):
                qoff = qh * 1024

                @blk.tensor
                def _(tensor, t1_done=t1_done):
                    tensor.wait_ge(cv.h, t1_done)
                    for qq in range(2):
                        op = tensor.matmul(ps_a2[qq][64:128, :],
                                           ones[64:65, 0:64],
                                           rzb[64:65, qq * 512:(qq + 1) * 512],
                                           start=True, stop=True, tile_position=(64, 64),
                                           skip_group_check=True)
                    ct.inc(op)
                zb_done = ct.n

                @blk.vector
                def _(vector, zb_done=zb_done, qoff=qoff, h=h, qh=qh):
                    vector.wait_ge(ct.h, zb_done)
                    if h % 2 == 0 and A["apair_split"] is not None and qh == 0:
                        vector.wait_ge(cv.h, A["apair_split"])
                    if h % 2 == 1 and A["aout_free"] is not None:
                        vector.wait_ge(ld_mv.h, A["aout_free"])
                    for qq in range(2):
                        vector.tensor_copy(rzb[0:64, qq * 512:(qq + 1) * 512],
                                           ps_a2[qq][64:128, :])
                    vector.drain()
                    for qq in range(2):
                        if h % 2 == 0:
                            dsl = apair[0:64, qoff + qq * 512: qoff + (qq + 1) * 512]
                        else:
                            dsl = aout[0:64, qq * 512:(qq + 1) * 512]
                        op = vector.tensor_tensor(
                            out=dsl,
                            in0=t1[:, qq * 512:(qq + 1) * 512],
                            in1=rzb[0:64, qq * 512:(qq + 1) * 512], op=ALU.mult)
                    cv.inc(op)
                if h % 2 == 1:
                    halfdone = cv.n

                    @blk.gpsimd
                    def _(gpsimd, halfdone=halfdone, qoff=qoff):
                        gpsimd.wait_ge(cv.h, halfdone)
                        ld_mv.inc(gpsimd.dma_start(out=apair[64:128, qoff:qoff + 1024],
                                                   in_=aout[0:64, :]), 16)
                    A["aout_free"] = ld_mv.n
                return cv.n

            for h in range(n_heads):
                i = h % 2
                if h + 1 < n_heads:
                    load_head(h + 1)

                exp_done = []
                phi_done = []
                plo_done = []
                pva_done = []
                pvb_done = []
                pv_done = None
                for qh in range(2):
                    qoff = qh * 1024
                    run = cs.n
                    for kc in range(32):
                        run += 2 if (qh * 32 + kc) % 2 == 0 else 1
                        exp_done.append(run)
                    scalar_recs = []
                    for kc in range(32):
                        s_ = qh * 32 + kc
                        pbk = s_ % 2
                        pb4 = s_ % 4

                        @blk.tensor
                        def _(tensor, kc=kc, s_=s_, pbk=pbk, i=i, qoff=qoff, h=h):
                            if s_ == 0:
                                tensor.wait_ge(ld_hd[h % 2].h, head_ld[h])
                                if A["ps_s_free"] is not None:
                                    tensor.wait_ge(cv.h, A["ps_s_free"])
                            if s_ >= 2:
                                tensor.wait_ge(cs.h, exp_done[s_ - 2])
                            elif A["prev_exp_last"] is not None:
                                tensor.wait_ge(cs.h, A["prev_exp_last"])
                            for qq in range(2):
                                tensor.matmul(ps_s[pbk][:, qq * 512:(qq + 1) * 512],
                                              kst[i][:, kc * 128:(kc + 1) * 128],
                                              qab[i][:, qoff + qq * 512: qoff + (qq + 1) * 512],
                                              start=True, stop=False)
                            for qq in range(2):
                                op = tensor.matmul(ps_s[pbk][:, qq * 512:(qq + 1) * 512],
                                                   kst[i][:, kc * 128:(kc + 1) * 128],
                                                   qcd[i][:, qoff + qq * 512: qoff + (qq + 1) * 512],
                                                   start=False, stop=True)
                            ct.inc(op)
                        p32_wait = None
                        if s_ >= 2:
                            p32_wait = plo_done[s_ - 2]
                        elif A["prev_vhl_last"] is not None:
                            p32_wait = A["prev_vhl_last"]
                        phibuf_wait = None
                        if s_ >= 4:
                            phibuf_wait = pva_done[s_ - 4]
                        elif A["prev_pv_last"] is not None:
                            phibuf_wait = A["prev_pv_last"]
                        even = (s_ % 2 == 0)
                        scalar_recs.append((ct.n, p32_wait, pbk, pb4, even, phibuf_wait))

                        if even:
                            phi_done.append((cs.h, exp_done[s_] - 1))

                            @blk.vector
                            def _(vector, s_=s_, pb4=pb4, pbk=pbk):
                                vector.wait_ge(cs.h, exp_done[s_])
                                if s_ >= 4:
                                    vector.wait_ge(ct.h, pvb_done[s_ - 4])
                                cv.inc(vector.tensor_sub(plo[pb4][:], p32[pbk][:],
                                                         phi[pb4][:]))
                            plo_done.append((cv.h, cv.n))
                        else:
                            @blk.vector
                            def _(vector, s_=s_, pb4=pb4, pbk=pbk):
                                vector.wait_ge(cs.h, exp_done[s_])
                                if s_ >= 4:
                                    vector.wait_ge(ct.h, pva_done[s_ - 4])
                                elif A["prev_pv_last"] is not None:
                                    vector.wait_ge(ct.h, A["prev_pv_last"])
                                cv.inc(vector.tensor_copy(phi[pb4][:], p32[pbk][:]))
                            phi_done.append((cv.h, cv.n))

                            @blk.gpsimd
                            def _(gpsimd, s_=s_, pb4=pb4, pbk=pbk, pd=phi_done[-1]):
                                gpsimd.wait_ge(pd[0], pd[1])
                                if s_ >= 4:
                                    gpsimd.wait_ge(ct.h, pvb_done[s_ - 4])
                                cg.inc(gpsimd.tensor_sub(plo[pb4][:], p32[pbk][:],
                                                         phi[pb4][:]))
                            plo_done.append((cg.h, cg.n))

                        if kc >= 1:
                            emit_pva(h, i, s_ - 1, phi_done, pva_done)
                        if kc >= 3:
                            emit_pvb(h, i, s_ - 3, plo_done, pvb_done)
                        if kc == 4 and A.get("pend_fb"):
                            ph, pq, pt1 = A["pend_fb"]
                            A["pend_fb"] = None
                            finalize_b(ph, pq, pt1)

                    emit_pva(h, i, qh * 32 + 31, phi_done, pva_done)
                    for sx in (29, 30, 31):
                        emit_pvb(h, i, qh * 32 + sx, plo_done, pvb_done)

                    @blk.scalar
                    def _(scalar, recs=tuple(scalar_recs)):
                        for (ctv, p32w, pbk2, pb42, ev, phw) in recs:
                            scalar.wait_ge(ct.h, ctv)
                            if ev:
                                if phw is not None:
                                    scalar.wait_ge(ct.h, phw)
                                op = scalar.activation(out=phi[pb42][:], in_=ps_s[pbk2][:],
                                                       func=AF.Exp, scale=0.125,
                                                       bias=nshift[:])
                                op.then_inc(cs.h, 1)
                                cs.n += 1
                            if p32w is not None:
                                scalar.wait_ge(p32w[0], p32w[1])
                            op = scalar.activation(out=p32[pbk2][:], in_=ps_s[pbk2][:],
                                                   func=AF.Exp, scale=0.125, bias=nshift[:])
                            op.then_inc(cs.h, 1)
                            cs.n += 1
                    assert cs.n == exp_done[qh * 32 + 31], (cs.n, exp_done[qh * 32 + 31])
                    t1v = finalize_a(h, qh, pvb_done)
                    if h % 2 == 1 and qh == 1:
                        finalize_b(h, qh, t1v)
                    else:
                        A["pend_fb"] = (h, qh, t1v)

                scaled = A["o_free"]

                head_free[h] = (ct.h, pvb_done[-1])
                A["prev_exp_last"] = exp_done[-1]
                A["prev_vhl_last"] = plo_done[-1]
                A["prev_pv_last"] = pvb_done[-1]

                if h % 2 == 1:
                    mv_done = ld_mv.n
                    pair = h // 2

                    @blk.vector
                    def _(vector, mv_done=mv_done, pair=pair, h=h):
                        vector.wait_ge(ld_mv.h, mv_done)
                        if pair >= 1:
                            vector.wait_ge(ct.h, A["apair_free"])
                        q2 = (h // 2) % 2
                        vector.wait_ge(ld_pw[q2].h, ld_pw[q2].n)
                        vector.tensor_copy(pwh[q2][:], pw_sb[q2][:])
                        vector.tensor_copy(aph[:], apair[:])
                        vector.drain()
                        vector.tensor_sub(pwl[q2][:], pw_sb[q2][:], pwh[q2][:])
                        cv.inc(vector.tensor_sub(apl[:], apair[:], aph[:]))
                    ap_split = cv.n
                    pw_split_by_pair[pair] = cv.n

                    pr_mm = []
                    pr_ev = []
                    for idx in range(32):
                        dout, qq = divmod(idx, 4)
                        bank = ps_s[(idx % 4) // 2]
                        slc = (idx % 2) * 512

                        @blk.tensor
                        def _(tensor, idx=idx, dout=dout, qq=qq, bank=bank, slc=slc,
                              ap_split=ap_split, h=h):
                            if idx == 0:
                                tensor.wait_ge(cv.h, ap_split)
                                tensor.wait_ge(cv.h, pw_split_by_pair[h // 2])
                            if idx >= 4:
                                tensor.wait_ge(cv.h, pr_ev[idx - 4])
                            q2 = (h // 2) % 2
                            wsl = slice(dout * 128, (dout + 1) * 128)
                            asl = slice(qq * 512, (qq + 1) * 512)
                            tensor.matmul(bank[:, slc:slc + 512], pwh[q2][:, wsl], aph[:, asl],
                                          start=True, stop=False)
                            tensor.matmul(bank[:, slc:slc + 512], pwh[q2][:, wsl], apl[:, asl],
                                          start=False, stop=False)
                            ct.inc(tensor.matmul(bank[:, slc:slc + 512], pwl[q2][:, wsl],
                                                 aph[:, asl], start=False, stop=True))
                        pr_mm.append(ct.n)

                        @blk.vector
                        def _(vector, idx=idx, dout=dout, qq=qq, bank=bank, slc=slc, pair=pair):
                            vector.wait_ge(ct.h, pr_mm[idx])
                            dst2 = projacc[:, dout * NQ + qq * 512: dout * NQ + (qq + 1) * 512]
                            if pair == 0:
                                cv.inc(vector.tensor_copy(dst2, bank[:, slc:slc + 512]))
                            else:
                                cv.inc(vector.tensor_tensor(out=dst2, in0=bank[:, slc:slc + 512],
                                                            in1=dst2, op=ALU.add))
                        pr_ev.append(cv.n)

                    proj_ev_by_pair[pair] = pr_ev[-1]
                    A["proj_done"] = pr_ev[-1]
                    A["ps_s_free"] = pr_ev[-1]
                    A["apair_free"] = pr_mm[-1]
                    A["apair_split"] = ap_split

            if debug in ("attn", "attn1", "attnonly"):
                final_proj = A["proj_done"]

                @blk.gpsimd
                def _(gpsimd):
                    gpsimd.wait_ge(cv.h, final_proj)
                    for ci in range(DC):
                        gpsimd.dma_start(out=AP_s[ci],
                                         in_=projacc[:, ci * NQ:(ci + 1) * NQ]).then_inc(st.h, 16)
                    st.n += 16 * DC
                    gpsimd.wait_ge(st.h, st.n)

        if debug in ("attn", "attn1", "attnonly"):
            return nc

        with contextlib.ExitStack() as p:
            BFF = mybir.dt.bfloat16
            wsel = p.enter_context(nc.sbuf_tensor("wsel", [128, D], F32))
            w1h = p.enter_context(nc.sbuf_tensor("w1h", [128, DC * D], BFF))
            w1l = p.enter_context(nc.sbuf_tensor("w1l", [128, DC * D], BFF))
            w2h = p.enter_context(nc.sbuf_tensor("w2h", [128, DC * D], BFF))
            w2l = p.enter_context(nc.sbuf_tensor("w2l", [128, DC * D], BFF))
            hbuf = p.enter_context(nc.sbuf_tensor("hbuf", [128, 4 * 512], F32))
            hh = [p.enter_context(nc.sbuf_tensor(f"hh{i}", [128, DC * 512], BFF)) for i in range(2)]
            hl = [p.enter_context(nc.sbuf_tensor(f"hl{i}", [128, DC * 512], BFF)) for i in range(2)]
            wtile = [p.enter_context(nc.sbuf_tensor(f"wtile{i}", [128, D], F32)) for i in range(2)]
            xnh8 = p.enter_context(nc.sbuf_tensor("xnh8", [128, DC * 512], BFF))
            xnl8 = p.enter_context(nc.sbuf_tensor("xnl8", [128, DC * 512], BFF))
            btile = p.enter_context(nc.sbuf_tensor("btile", [128, E], F32))
            b2s = p.enter_context(nc.sbuf_tensor("b2s", [128, 2 * E], F32))
            pb_t = p.enter_context(nc.sbuf_tensor("pb_t", [128, DC], F32))
            ps_f = [p.enter_context(nc.psum_tensor(f"ps_f{i}", [128, 512], F32)) for i in range(DC)]

            blk = p.enter_context(nc.Block("ffn"))

            @blk.sync
            def _(sync):
                sync.dma_start(out=pb_t[:], in_=bproj.rearrange("(j p) -> p j", p=128),
                               allow_slow_non_contiguous=True).then_inc(ld_bt.h, 16)
            ld_bt.n += 16
            pb_ld = ld_bt.n

            for bi, (bsrc, bdst) in enumerate(((b1, b1sel), (b2, b2s))):
                for e in range(E):
                    @blk.sync
                    def _(sync, bsrc=bsrc, e=e):
                        sync.wait_ge(cv.h, cv.n)
                        sync.dma_start(out=btile[:],
                                       in_=bsrc[e].rearrange("(j p) -> p j", p=128),
                                       allow_slow_non_contiguous=True).then_inc(ld_bt.h, 16)
                    ld_bt.n += 16
                    bld = ld_bt.n

                    @blk.vector
                    def _(vector, bdst=bdst, e=e, bld=bld):
                        vector.wait_ge(ld_bt.h, bld)
                        for k, msk in ((0, m1b), (1, m2b)):
                            dst2 = bdst[:, k * E:(k + 1) * E]
                            if e == 0:
                                op = vector.tensor_scalar(out=dst2, in0=btile[:],
                                                          scalar1=msk[:, e:e + 1], scalar2=None,
                                                          op0=ALU.mult)
                            else:
                                op = vector.scalar_tensor_tensor(out=dst2, in0=btile[:],
                                                                 scalar=msk[:, e:e + 1], in1=dst2,
                                                                 op0=ALU.mult, op1=ALU.add)
                        vector.drain()
                        cv.inc(op)

            @blk.vector
            def _(vector, full=None):
                vector.wait_ge(ld_bt.h, ld_bt.n)
                vector.drain()
                vector.tensor_tensor(out=bias_all[:], in0=b2s[:, 0:E], in1=b2s[:, E:2 * E],
                                     op=ALU.add)
                vector.drain()
                cv.inc(vector.scalar_tensor_tensor(out=bias_all[:], in0=bias_all[:], scalar=0.5,
                                                   in1=pb_t[:], op0=ALU.mult, op1=ALU.add))

            wsel_marks = []
            sel_ci = {}
            wfree = {"w1": None, "w2": None}
            R = {"gelu_last": None,
                 "oev_last": None,
                 "omm_par": [None, None],
                 "hbuf_free": None,
                 "xn_free": None}

            def emit_h(k, t, w, my_xld):
                pg = R["gelu_last"]

                @blk.tensor
                def _(tensor, k=k, t=t, w=w, my_xld=my_xld, pg=pg):
                    tensor.wait_ge(ld_xn[0].h, my_xld)
                    if pg is not None:
                        tensor.wait_ge(cs.h, pg)
                    for ci in range(DC):
                        if t == 0 and w == 0:
                            tensor.wait_ge(cg.h, sel_ci[(k, "w1", ci)])
                        xsl = slice(ci * 512, (ci + 1) * 512)
                        for dout in range(4 * w, 4 * w + 4):
                            wsl = slice(ci * D + dout * 128, ci * D + (dout + 1) * 128)
                            tensor.matmul(ps_f[dout % 4][:], w1h[:, wsl], xnh8[:, xsl],
                                          start=(ci == 0), stop=False)
                            tensor.matmul(ps_f[dout % 4][:], w1h[:, wsl], xnl8[:, xsl],
                                          start=False, stop=False)
                            op = tensor.matmul(ps_f[dout % 4][:], w1l[:, wsl], xnh8[:, xsl],
                                               start=False, stop=(ci == DC - 1))
                    ct.inc(op)
                return ct.n

            def emit_gelu(k, t, w, hmmv):
                hhfree = R["omm_par"][t % 2]
                hbfree = R["hbuf_free"]

                @blk.scalar
                def _(scalar, k=k, t=t, w=w, hmmv=hmmv, hhfree=hhfree, hbfree=hbfree):
                    scalar.wait_ge(ct.h, hmmv)
                    if hhfree is not None:
                        scalar.wait_ge(ct.h, hhfree)
                    if hbfree is not None:
                        scalar.wait_ge(cv.h, hbfree)
                    for dj, dout in enumerate(range(4 * w, 4 * w + 4)):
                        b = b1sel[:, k * E + dout:k * E + dout + 1]
                        scalar.activation(out=hh[t % 2][:, dout * 512:(dout + 1) * 512],
                                          in_=ps_f[dout % 4][:], func=AF.Gelu, bias=b)
                        op = scalar.activation(out=hbuf[:, dj * 512:(dj + 1) * 512],
                                               in_=ps_f[dout % 4][:], func=AF.Gelu, bias=b)
                    cs.inc(op)
                R["gelu_last"] = cs.n
                return cs.n

            def emit_sub(k, t, w, gv):
                hlfree = R["omm_par"][t % 2]

                @blk.vector
                def _(vector, t=t, w=w, gv=gv, hlfree=hlfree):
                    vector.wait_ge(cs.h, gv)
                    if hlfree is not None:
                        vector.wait_ge(ct.h, hlfree)
                    for dj, dout in enumerate(range(4 * w, 4 * w + 4)):
                        op = vector.tensor_sub(hl[t % 2][:, dout * 512:(dout + 1) * 512],
                                               hbuf[:, dj * 512:(dj + 1) * 512],
                                               hh[t % 2][:, dout * 512:(dout + 1) * 512])
                    cv.inc(op)
                R["hbuf_free"] = cv.n
                return cv.n

            def emit_o(k, t, w, subv):
                oevfree = R["oev_last"]

                @blk.tensor
                def _(tensor, k=k, t=t, w=w, subv=subv, oevfree=oevfree):
                    tensor.wait_ge(cv.h, subv)
                    if oevfree is not None:
                        tensor.wait_ge(cv.h, oevfree)
                    for ci2 in range(DC):
                        if t == 0 and w == 0:
                            tensor.wait_ge(cg.h, sel_ci[(k, "w2", ci2)])
                        hsl = slice(ci2 * 512, (ci2 + 1) * 512)
                        for dout in range(4 * w, 4 * w + 4):
                            wsl = slice(ci2 * D + dout * 128, ci2 * D + (dout + 1) * 128)
                            tensor.matmul(ps_f[4 + dout % 4][:], w2h[:, wsl],
                                          hh[t % 2][:, hsl], start=(ci2 == 0), stop=False)
                            tensor.matmul(ps_f[4 + dout % 4][:], w2h[:, wsl],
                                          hl[t % 2][:, hsl], start=False, stop=False)
                            op = tensor.matmul(ps_f[4 + dout % 4][:], w2l[:, wsl],
                                               hh[t % 2][:, hsl], start=False,
                                               stop=(ci2 == DC - 1))
                    ct.inc(op)
                R["omm_par"][t % 2] = ct.n
                return ct.n

            def emit_oev(k, t, w, ommv):
                @blk.vector
                def _(vector, t=t, w=w, ommv=ommv):
                    vector.wait_ge(ct.h, ommv)
                    for dout in range(4 * w, 4 * w + 4):
                        dst2 = projacc[:, dout * NQ + t * 512: dout * NQ + (t + 1) * 512]
                        op = vector.scalar_tensor_tensor(out=dst2, in0=ps_f[4 + dout % 4][:],
                                                         scalar=0.5, in1=dst2,
                                                         op0=ALU.mult, op1=ALU.add)
                    cv.inc(op)
                R["oev_last"] = cv.n
                return cv.n

            for k in range(2):
                msk = m1b if k == 0 else m2b
                for wsrc, wdh, wdl, tag in ((W1, w1h, w1l, "w1"), (W2, w2h, w2l, "w2")):
                    for ci in range(DC):
                        for e in range(E):
                            tidx = len(wsel_marks) % 2

                            @blk.sync
                            def _(sync, wsrc=wsrc, e=e, ci=ci, tidx=tidx):
                                if len(wsel_marks) >= 2:
                                    sync.wait_ge(cv.h, wsel_marks[-2])
                                sync.dma_start(out=wtile[tidx][:],
                                               in_=wsrc[e, ci * 128:(ci + 1) * 128, :]).then_inc(ld_wt[tidx].h, 16)
                            ld_wt[tidx].n += 16
                            wld = ld_wt[tidx].n
                            prev_sub = max(sel_ci.values()) if sel_ci else None

                            @blk.vector
                            def _(vector, wdh=wdh, e=e, ci=ci, tidx=tidx, wld=wld,
                                  msk=msk, tag=tag, prev_sub=prev_sub):
                                vector.wait_ge(ld_wt[tidx].h, wld)
                                if e == 0 and prev_sub is not None:
                                    vector.wait_ge(cg.h, prev_sub)
                                vector.drain()
                                if e == 0:
                                    op = vector.tensor_scalar(out=wsel[:], in0=wtile[tidx][:],
                                                              scalar1=msk[:, e:e + 1],
                                                              scalar2=None, op0=ALU.mult)
                                else:
                                    op = vector.scalar_tensor_tensor(out=wsel[:], in0=wtile[tidx][:],
                                                                     scalar=msk[:, e:e + 1],
                                                                     in1=wsel[:], op0=ALU.mult,
                                                                     op1=ALU.add)
                                if e == E - 1:
                                    vector.drain()
                                    if ci == 0 and wfree[tag] is not None:
                                        vector.wait_ge(ct.h, wfree[tag])
                                    op = vector.tensor_copy(wdh[:, ci * D:(ci + 1) * D],
                                                            wsel[:])
                                cv.inc(op)
                            wsel_marks.append(cv.n)

                            if e == E - 1:
                                hi_done = cv.n

                                @blk.gpsimd
                                def _(gpsimd, ci=ci, hi_done=hi_done, wdh=wdh, wdl=wdl):
                                    gpsimd.wait_ge(cv.h, hi_done)
                                    cg.inc(gpsimd.tensor_sub(wdl[:, ci * D:(ci + 1) * D],
                                                             wsel[:],
                                                             wdh[:, ci * D:(ci + 1) * D]))
                                sel_ci[(k, tag, ci)] = cg.n

                hmmd = {}
                subd = {}
                ommd = {}
                for t in range(4):
                    xf = R["xn_free"]

                    @blk.scalar
                    def _(scalar, t=t, xf=xf):
                        if xf is not None:
                            scalar.wait_ge(ct.h, xf)
                        for ci in range(DC):
                            scalar.dma_start(out=xnh8[:, ci * 512:(ci + 1) * 512],
                                             in_=XNH_s[ci, :, t * 512:(t + 1) * 512]).then_inc(ld_xn[0].h, 16)
                            scalar.dma_start(out=xnl8[:, ci * 512:(ci + 1) * 512],
                                             in_=XNL_s[ci, :, t * 512:(t + 1) * 512]).then_inc(ld_xn[0].h, 16)
                    ld_xn[0].n += 256
                    my_xld = ld_xn[0].n

                    if t >= 2:
                        ommd[(t - 2, 0)] = emit_o(k, t - 2, 0, subd[(t - 2, 1)])
                    hmmd[(t, 0)] = emit_h(k, t, 0, my_xld)
                    if t >= 2:
                        emit_oev(k, t - 2, 0, ommd[(t - 2, 0)])
                        ommd[(t - 2, 1)] = emit_o(k, t - 2, 1, subd[(t - 2, 1)])
                    gv0 = emit_gelu(k, t, 0, hmmd[(t, 0)])
                    subd[(t, 0)] = emit_sub(k, t, 0, gv0)
                    hmmd[(t, 1)] = emit_h(k, t, 1, my_xld)
                    R["xn_free"] = hmmd[(t, 1)]
                    if t >= 2:
                        emit_oev(k, t - 2, 1, ommd[(t - 2, 1)])
                    gv1 = emit_gelu(k, t, 1, hmmd[(t, 1)])
                    subd[(t, 1)] = emit_sub(k, t, 1, gv1)
                for tt in (2, 3):
                    ommd[(tt, 0)] = emit_o(k, tt, 0, subd[(tt, 1)])
                    emit_oev(k, tt, 0, ommd[(tt, 0)])
                    ommd[(tt, 1)] = emit_o(k, tt, 1, subd[(tt, 1)])
                    emit_oev(k, tt, 1, ommd[(tt, 1)])
                wfree["w1"] = hmmd[(3, 1)]
                wfree["w2"] = ommd[(3, 1)]

        with contextlib.ExitStack() as p:
            xqb = [p.enter_context(nc.sbuf_tensor(f"xqb{i}", [128, D], F32)) for i in range(2)]
            outb = [p.enter_context(nc.sbuf_tensor(f"outb{i}", [128, D], F32)) for i in range(2)]
            ps_c = [p.enter_context(nc.psum_tensor(f"ps_c{i}", [128, 512], F32)) for i in range(2)]

            blk = p.enter_context(nc.Block("fin"))

            @blk.vector
            def _(vector):
                for dout in range(DC):
                    op = vector.tensor_scalar(out=projacc[:, dout * NQ:(dout + 1) * NQ],
                                              in0=projacc[:, dout * NQ:(dout + 1) * NQ],
                                              scalar1=bias_all[:, dout:dout + 1], scalar2=None,
                                              op0=ALU.add)
                cv.inc(op)
            bias_done = cv.n

            ev_marks = []
            st_marks = []
            for tc in range(16):
                i = tc % 2

                @blk.sync
                def _(sync, tc=tc, i=i):
                    if tc >= 2:
                        sync.wait_ge(cv.h, ev_marks[tc - 2])
                    sync.dma_start(out=xqb[i][:],
                                   in_=x_q[tc * 128:(tc + 1) * 128, :]).then_inc(ld_xq[i].h, 16)
                ld_xq[i].n += 16
                xld = ld_xq[i].n

                @blk.tensor
                def _(tensor, tc=tc, bias_done=bias_done):
                    if tc == 0:
                        tensor.wait_ge(cv.h, bias_done)
                    if tc >= 1:
                        tensor.wait_ge(cv.h, ev_marks[tc - 1])
                    for half in range(2):
                        for q in range(4):
                            ci = half * 4 + q
                            op = tensor.transpose(ps_c[half][:, q * 128:(q + 1) * 128],
                                                  projacc[:, ci * NQ + tc * 128: ci * NQ + (tc + 1) * 128],
                                                  ident[:])
                    ct.inc(op)
                tdn = ct.n

                @blk.vector
                def _(vector, tc=tc, i=i, xld=xld, tdn=tdn):
                    vector.wait_ge(ct.h, tdn)
                    vector.wait_ge(ld_xq[i].h, xld)
                    if tc >= 2:
                        vector.wait_ge(st_o[i].h, st_marks[tc - 2])
                    for half in range(2):
                        op = vector.tensor_tensor(out=outb[i][:, half * 512:(half + 1) * 512],
                                                  in0=ps_c[half][:],
                                                  in1=xqb[i][:, half * 512:(half + 1) * 512],
                                                  op=ALU.add)
                    cv.inc(op)
                ev_marks.append(cv.n)

                @blk.gpsimd
                def _(gpsimd, tc=tc, i=i, evm=ev_marks[-1]):
                    gpsimd.wait_ge(cv.h, evm)
                    gpsimd.dma_start(out=out[tc * 128:(tc + 1) * 128, :],
                                     in_=outb[i][:]).then_inc(st_o[i].h, 16)
                st_o[i].n += 16
                st_marks.append(st_o[i].n)

            @blk.gpsimd
            def _(gpsimd):
                gpsimd.wait_ge(st_o[0].h, st_o[0].n)
                gpsimd.wait_ge(st_o[1].h, st_o[1].n)

    return nc


_NC_CACHE = {}


def kernel(L0_attn_x, L0_attn_qkv_w, L0_attn_qkv_b, L0_attn_proj_w, L0_attn_proj_b,
           L0_ffn_Wr, L0_ffn_br, L0_ffn_W1, L0_ffn_b1, L0_ffn_W2, L0_ffn_b2):
    x = np.ascontiguousarray(np.asarray(L0_attn_x, dtype=np.float32))
    qkvT = np.ascontiguousarray(np.asarray(L0_attn_qkv_w, dtype=np.float32).T).reshape(DC, 128, QKVD)
    projT = np.ascontiguousarray(np.asarray(L0_attn_proj_w, dtype=np.float32).T).reshape(DC, 128, D)
    wr_t = np.ascontiguousarray(np.asarray(L0_ffn_Wr, dtype=np.float32).reshape(DC, 128, E).transpose(1, 0, 2).reshape(128, DC * E))
    bqkv = np.ascontiguousarray(np.asarray(L0_attn_qkv_b, dtype=np.float32))
    bproj = np.ascontiguousarray(np.asarray(L0_attn_proj_b, dtype=np.float32))
    br = np.ascontiguousarray(np.asarray(L0_ffn_br, dtype=np.float32))
    W1 = np.ascontiguousarray(np.asarray(L0_ffn_W1, dtype=np.float32))
    W2 = np.ascontiguousarray(np.asarray(L0_ffn_W2, dtype=np.float32))
    b1 = np.ascontiguousarray(np.asarray(L0_ffn_b1, dtype=np.float32))
    b2 = np.ascontiguousarray(np.asarray(L0_ffn_b2, dtype=np.float32))

    if "nc" not in _NC_CACHE:
        _NC_CACHE["nc"] = build_nc()
    nc = _NC_CACHE["nc"]

    in_maps = []
    for c in range(8):
        b, qh = divmod(c, 2)
        xp = np.ascontiguousarray(
            np.concatenate([x[b, qh * NQ:(qh + 1) * NQ],
                            x[b, (1 - qh) * NQ:(2 - qh) * NQ]], axis=0))
        in_maps.append({
            "x_kv": xp,
            "x_q": np.ascontiguousarray(x[b, qh * NQ:(qh + 1) * NQ]),
            "rsel": np.array([[1.0 - qh, float(qh)]], dtype=np.float32),
            "wqkv": qkvT, "bqkv": bqkv, "wproj": projT, "bproj": bproj,
            "wr": wr_t, "br": br, "W1": W1, "W2": W2, "b1": b1, "b2": b2,
        })
    res = run_bass_kernel_spmd(nc, in_maps, core_ids=list(range(8)))
    outp = np.empty((B, NTOK, D), dtype=np.float32)
    for c in range(8):
        b, qh = divmod(c, 2)
        outp[b, qh * NQ:(qh + 1) * NQ] = res.results[c]["out"]
    return outp


# version 20 (final)
# speedup vs baseline: 1.1221x; 1.0806x over previous
import contextlib
import numpy as np

import concourse.bass as bass
import concourse.mybir as mybir
from concourse.bass_utils import run_bass_kernel_spmd

F32 = mybir.dt.float32
I32 = mybir.dt.int32
AF = mybir.ActivationFunctionType
ALU = mybir.AluOpType
AX = mybir.AxisListType

B, NTOK, NQ, D, H, HD, E = 4, 4096, 2048, 1024, 16, 64, 8
DC = D // 128
QKVD = 3 * D
SHIFT = 6.0


class Ctr:
    def __init__(self, ctx, nc, name):
        self.h = ctx.enter_context(nc.semaphore(name))
        self.n = 0

    def inc(self, op, k=1):
        op.then_inc(self.h, k)
        self.n += k
        return self.n


def build_nc(debug=None):
    nc = bass.Bass("TRN2", target_bir_lowering=False)

    x_kv = nc.dram_tensor("x_kv", [NTOK, D], F32, kind="ExternalInput").ap()
    x_q = nc.dram_tensor("x_q", [NQ, D], F32, kind="ExternalInput").ap()
    wqkv = nc.dram_tensor("wqkv", [DC, 128, QKVD], F32, kind="ExternalInput").ap()
    bqkv = nc.dram_tensor("bqkv", [QKVD], F32, kind="ExternalInput").ap()
    wproj = nc.dram_tensor("wproj", [DC, 128, D], F32, kind="ExternalInput").ap()
    bproj = nc.dram_tensor("bproj", [D], F32, kind="ExternalInput").ap()
    wr = nc.dram_tensor("wr", [128, DC * E], F32, kind="ExternalInput").ap()
    br = nc.dram_tensor("br", [E], F32, kind="ExternalInput").ap()
    W1 = nc.dram_tensor("W1", [E, D, D], F32, kind="ExternalInput").ap()
    W2 = nc.dram_tensor("W2", [E, D, D], F32, kind="ExternalInput").ap()
    b1 = nc.dram_tensor("b1", [E, D], F32, kind="ExternalInput").ap()
    b2 = nc.dram_tensor("b2", [E, D], F32, kind="ExternalInput").ap()
    out = nc.dram_tensor("out", [NQ, D], F32, kind="ExternalOutput").ap()

    sk = "ExternalOutput" if debug else "Internal"
    ski = "ExternalInput" if debug == "attnonly" else sk
    BFD = mybir.dt.bfloat16
    QH_s = nc.dram_tensor("QH_s", [DC, 128, NQ], BFD, kind=ski).ap()
    QL_s = nc.dram_tensor("QL_s", [DC, 128, NQ], BFD, kind=ski).ap()
    KH_s = nc.dram_tensor("KH_s", [DC, 128, NTOK], BFD, kind=ski).ap()
    KL_s = nc.dram_tensor("KL_s", [DC, 128, NTOK], BFD, kind=ski).ap()
    VH_s = nc.dram_tensor("VH_s", [NTOK, D], BFD, kind=ski).ap()
    VL_s = nc.dram_tensor("VL_s", [NTOK, D], BFD, kind=ski).ap()
    rsel = nc.dram_tensor("rsel", [1, 2], F32, kind="ExternalInput").ap()
    XNT_s = nc.dram_tensor("XNT_s", [DC, 128, NQ], F32, kind=ski).ap()
    XNH_s = nc.dram_tensor("XNH_s", [DC, 128, NQ], mybir.dt.bfloat16, kind=ski).ap()
    XNL_s = nc.dram_tensor("XNL_s", [DC, 128, NQ], mybir.dt.bfloat16, kind=ski).ap()
    MB_s = nc.dram_tensor("MB_s", [2, 128, E], F32, kind=sk).ap()
    AP_s = nc.dram_tensor("AP_s", [DC, 128, NQ], F32, kind=sk).ap()

    idx_info = {}
    g = contextlib.ExitStack()
    with g:
        ident = g.enter_context(nc.sbuf_tensor("ident", [128, 128], F32))
        iot = g.enter_context(nc.sbuf_tensor("iot", [128, 128], I32))
        ones = g.enter_context(nc.sbuf_tensor("ones", [128, 128], F32))
        m1b = g.enter_context(nc.sbuf_tensor("m1b", [128, E], F32))
        m2b = g.enter_context(nc.sbuf_tensor("m2b", [128, E], F32))
        qb_sb = g.enter_context(nc.sbuf_tensor("qb_sb", [128, 3 * DC], F32))
        bvb = g.enter_context(nc.sbuf_tensor("bvb", [128, D], F32))
        bias_all = g.enter_context(nc.sbuf_tensor("bias_all", [128, DC], F32))
        b1sel = g.enter_context(nc.sbuf_tensor("b1sel", [128, 2 * DC], F32))
        idxtab = g.enter_context(nc.sbuf_tensor("idxtab", [128, 2 * E], I32))

        ld = Ctr(g, nc, "ld")
        st = Ctr(g, nc, "st")
        ld_x = [Ctr(g, nc, f"ld_x{i}") for i in range(2)]
        ld_hd = [Ctr(g, nc, f"ld_hd{i}") for i in range(2)]
        ld_pw = [Ctr(g, nc, f"ld_pw{i}") for i in range(2)]
        ld_bt = Ctr(g, nc, "ld_bt")
        ld_wt = [Ctr(g, nc, f"ld_wt{i}") for i in range(3)]
        ld_xn = [Ctr(g, nc, f"ld_xn{i}") for i in range(2)]
        ld_xq = [Ctr(g, nc, f"ld_xq{i}") for i in range(2)]
        st_o = [Ctr(g, nc, f"st_o{i}") for i in range(2)]
        ld_mv = Ctr(g, nc, "ld_mv")
        cv = Ctr(g, nc, "cv")
        cs = Ctr(g, nc, "cs")
        ct = Ctr(g, nc, "ct")
        cg = Ctr(g, nc, "cg")

        def phase_p0():
            with contextlib.ExitStack() as p:
                BFP = mybir.dt.bfloat16
                wq_st = p.enter_context(nc.sbuf_tensor("wq_st", [128, QKVD], F32))
                wq_hi = p.enter_context(nc.sbuf_tensor("wq_hi", [128, DC * QKVD], BFP))
                wq_lo = p.enter_context(nc.sbuf_tensor("wq_lo", [128, DC * QKVD], BFP))
                xnT_hi = p.enter_context(nc.sbuf_tensor("xnT_hi", [128, DC * 512], BFP))
                xnT_lo = p.enter_context(nc.sbuf_tensor("xnT_lo", [128, DC * 512], BFP))
                wr_hi = p.enter_context(nc.sbuf_tensor("wr_hi", [128, DC * E], BFP))
                wr_lo = p.enter_context(nc.sbuf_tensor("wr_lo", [128, DC * E], BFP))
                xbuf = [p.enter_context(nc.sbuf_tensor(f"xbuf{i}", [128, D], F32)) for i in range(2)]
                x2buf = p.enter_context(nc.sbuf_tensor("x2buf", [128, D], F32))
                xnbuf = [p.enter_context(nc.sbuf_tensor(f"xnbuf{i}", [128, D], F32)) for i in range(2)]
                ev = [p.enter_context(nc.sbuf_tensor(f"ev{i}", [128, 512], F32)) for i in range(2)]
                ehi = [p.enter_context(nc.sbuf_tensor(f"ehi{i}", [128, 512], BFP)) for i in range(2)]
                elo = [p.enter_context(nc.sbuf_tensor(f"elo{i}", [128, 512], BFP)) for i in range(2)]
                vout = [p.enter_context(nc.sbuf_tensor(f"vout{i}", [128, D], F32)) for i in range(2)]
                vhi = [p.enter_context(nc.sbuf_tensor(f"vhi{i}", [128, D], BFP)) for i in range(2)]
                vlo = [p.enter_context(nc.sbuf_tensor(f"vlo{i}", [128, D], BFP)) for i in range(2)]
                stat = p.enter_context(nc.sbuf_tensor("stat", [128, 16], F32))
                wr_sb = p.enter_context(nc.sbuf_tensor("wr_sb", [128, DC * E], F32))
                sc8 = p.enter_context(nc.sbuf_tensor("sc8", [1, E], F32))
                sab = p.enter_context(nc.sbuf_tensor("sab", [1, 2 * E], F32))
                iot8 = p.enter_context(nc.sbuf_tensor("iot8", [128, E], I32))
                iot8f = p.enter_context(nc.sbuf_tensor("iot8f", [128, E], F32))
                ew = p.enter_context(nc.sbuf_tensor("ew", [1, E], F32))
                pr2 = p.enter_context(nc.sbuf_tensor("pr2", [1, 2 * E], F32))
                es2 = p.enter_context(nc.sbuf_tensor("es2", [1, 2], F32))
                rs_sb = p.enter_context(nc.sbuf_tensor("rs_sb", [1, 2], F32))
                mx = p.enter_context(nc.sbuf_tensor("mx", [1, 2], F32))
                mk12 = p.enter_context(nc.sbuf_tensor("mk12", [1, 2 * E], F32))
                bvrow = p.enter_context(nc.sbuf_tensor("bvrow", [1, D], F32))
                ps_qk = [p.enter_context(nc.psum_tensor(f"ps_qk{i}", [128, 512], F32)) for i in range(2)]
                ps_v = [p.enter_context(nc.psum_tensor(f"ps_v{i}", [128, 512], F32)) for i in range(2)]
                ps_tr = [p.enter_context(nc.psum_tensor(f"ps_tr{i}", [128, 512], F32)) for i in range(2)]
                ps_bc = p.enter_context(nc.psum_tensor("ps_bc", [128, 512], F32))

                blk = p.enter_context(nc.Block("p0"))

                @blk.sync
                def _(sync):
                    sync.dma_start(out=wq_st[:], in_=wqkv[0]).then_inc(ld.h, 16)
                    sync.dma_start(out=wr_sb[:], in_=wr).then_inc(ld.h, 16)
                    sync.dma_start(out=qb_sb[:], in_=bqkv.rearrange("(j p) -> p j", p=128),
                                   allow_slow_non_contiguous=True).then_inc(ld.h, 16)
                    sync.dma_start(out=bvrow[:],
                                   in_=bqkv[2 * D:3 * D].rearrange("(o d) -> o d", o=1)).then_inc(ld.h, 16)
                    sync.dma_start(out=sc8[:], in_=br.rearrange("(o e) -> o e", o=1)).then_inc(ld.h, 16)
                    sync.dma_start(out=rs_sb[:], in_=rsel).then_inc(ld.h, 16)
                ld.n += 16 * 6
                pre_ld = ld.n

                wsplit_marks = []
                for ci in range(DC):
                    if ci > 0:
                        @blk.sync
                        def _(sync, ci=ci):
                            sync.wait_ge(cv.h, wsplit_marks[-1])
                            sync.dma_start(out=wq_st[:], in_=wqkv[ci]).then_inc(ld.h, 16)
                        ld.n += 16
                    wld2 = ld.n

                    @blk.vector
                    def _(vector, ci=ci, wld2=wld2):
                        vector.wait_ge(ld.h, wld2)
                        vector.tensor_copy(wq_hi[:, ci * QKVD:(ci + 1) * QKVD], wq_st[:])
                        vector.drain()
                        cv.inc(vector.tensor_sub(wq_lo[:, ci * QKVD:(ci + 1) * QKVD],
                                                 wq_st[:], wq_hi[:, ci * QKVD:(ci + 1) * QKVD]))
                    wsplit_marks.append(cv.n)

                @blk.vector
                def _(vector):
                    vector.wait_ge(ld.h, pre_ld)
                    vector.tensor_copy(wr_hi[:], wr_sb[:])
                    vector.drain()
                    cv.inc(vector.tensor_sub(wr_lo[:], wr_sb[:], wr_hi[:]))
                wsplit_done = cv.n

                @blk.gpsimd
                def _(gpsimd):
                    gpsimd.iota(iot[:], [[1, 128]], base=0, channel_multiplier=-1)
                    cg.inc(gpsimd.memset(ones[:], 1.0))
                g_pre = cg.n

                @blk.vector
                def _(vector):
                    vector.wait_ge(cg.h, g_pre)
                    cv.inc(vector.tensor_scalar(out=ident[:], in0=iot[:], scalar1=0, scalar2=None,
                                                op0=ALU.is_equal))
                ident_ok = cv.n

                @blk.tensor
                def _(tensor):
                    tensor.wait_ge(ld.h, pre_ld)
                    tensor.wait_ge(cv.h, ident_ok)
                    tensor.matmul(ps_v[0][:], ones[0:1, :], bvrow[0:1, 0:512], start=True, stop=True)
                    ct.inc(tensor.matmul(ps_v[1][:], ones[0:1, :], bvrow[0:1, 512:1024],
                                         start=True, stop=True))
                bv_mm = ct.n

                @blk.vector
                def _(vector):
                    vector.wait_ge(ct.h, bv_mm)
                    vector.tensor_copy(bvb[:, 0:512], ps_v[0][:])
                    cv.inc(vector.tensor_copy(bvb[:, 512:1024], ps_v[1][:]))

                state = {"n_ln": 0, "xbuf_free": [None, None],
                         "tr_evac": [(cv.h, wsplit_done)],
                         "qk_evac": [], "v_evacs": [], "qk_sub": [], "v_sub": []}

                def ln_chunk(src_ap, j):
                    i = state["n_ln"] % 2
                    state["n_ln"] += 1

                    @blk.sync
                    def _(sync, i=i, src_ap=src_ap):
                        if state["xbuf_free"][i] is not None:
                            sem, val = state["xbuf_free"][i]
                            sync.wait_ge(sem, val)
                        sync.dma_start(out=xbuf[i][:], in_=src_ap).then_inc(ld_x[i].h, 16)
                    ld_x[i].n += 16
                    my_ld = ld_x[i].n

                    @blk.scalar
                    def _(scalar, i=i, my_ld=my_ld):
                        scalar.wait_ge(ld_x[i].h, my_ld)
                        cs.inc(scalar.activation(out=x2buf[:], in_=xbuf[i][:], func=AF.Square,
                                                 accum_out=stat[:, 4:5]))
                    sq_done = cs.n

                    @blk.vector
                    def _(vector, i=i, my_ld=my_ld, sq_done=sq_done):
                        vector.wait_ge(ld_x[i].h, my_ld)
                        vector.reduce_sum(stat[:, 0:1], xbuf[i][:], axis=AX.X)
                        vector.wait_ge(cs.h, sq_done)
                        vector.drain()
                        vector.tensor_scalar(out=stat[:, 1:2], in0=stat[:, 0:1], scalar1=1.0 / D,
                                             scalar2=None, op0=ALU.mult)
                        vector.tensor_scalar(out=stat[:, 5:6], in0=stat[:, 4:5], scalar1=1.0 / D,
                                             scalar2=None, op0=ALU.mult)
                        vector.drain()
                        vector.scalar_tensor_tensor(out=stat[:, 2:3], in0=stat[:, 1:2],
                                                    scalar=stat[:, 1:2], in1=stat[:, 5:6],
                                                    op0=ALU.mult, op1=ALU.subtract)
                        vector.drain()
                        cv.inc(vector.tensor_scalar(out=stat[:, 2:3], in0=stat[:, 2:3], scalar1=-1.0,
                                                    scalar2=1e-5, op0=ALU.mult, op1=ALU.add))
                    var_done = cv.n

                    @blk.scalar
                    def _(scalar, var_done=var_done):
                        scalar.wait_ge(cv.h, var_done)
                        cs.inc(scalar.activation(out=stat[:, 3:4], in_=stat[:, 2:3], func=AF.Sqrt))
                    sqrt_done = cs.n

                    @blk.vector
                    def _(vector, sqrt_done=sqrt_done):
                        vector.wait_ge(cs.h, sqrt_done)
                        vector.reciprocal(out=stat[:, 6:7], in_=stat[:, 3:4])
                        vector.drain()
                        cv.inc(vector.scalar_tensor_tensor(out=stat[:, 7:8], in0=stat[:, 1:2],
                                                           scalar=-1.0, in1=stat[:, 6:7],
                                                           op0=ALU.mult, op1=ALU.mult))
                    rb_done = cv.n

                    @blk.scalar
                    def _(scalar, i=i, rb_done=rb_done, te2=(state["tr_evac"][-2]
                                                            if len(state["tr_evac"]) >= 2 else None)):
                        scalar.wait_ge(cv.h, rb_done)
                        if te2 is not None:
                            scalar.wait_ge(te2[0], te2[1])
                        cs.inc(scalar.activation(out=xnbuf[i][:], in_=xbuf[i][:], func=AF.Identity,
                                                 scale=stat[:, 6:7], bias=stat[:, 7:8]))
                    ln_done = cs.n
                    state["xbuf_free"][i] = (cs.h, ln_done)

                    @blk.tensor
                    def _(tensor, i=i, ln_done=ln_done, te1=state["tr_evac"][-1]):
                        tensor.wait_ge(cs.h, ln_done)
                        tensor.wait_ge(te1[0], te1[1])
                        for half in range(2):
                            for q in range(4):
                                ci = half * 4 + q
                                op = tensor.transpose(ps_tr[half][:, q * 128:(q + 1) * 128],
                                                      xnbuf[i][:, ci * 128:(ci + 1) * 128], ident[:])
                        ct.inc(op)
                    tp_done = ct.n

                    @blk.scalar
                    def _(scalar, j=j, tp_done=tp_done):
                        scalar.wait_ge(ct.h, tp_done)
                        for half in range(2):
                            dsth = xnT_hi[:].rearrange("p (c t) -> p c t", c=DC)[:, half * 4:half * 4 + 4,
                                                                                j * 128:(j + 1) * 128]
                            op = scalar.activation(out=dsth,
                                                   in_=ps_tr[half][:].rearrange("p (q t) -> p q t", q=4),
                                                   func=AF.Identity)
                        cs.inc(op)
                    hi_m = cs.n

                    @blk.vector
                    def _(vector, j=j, hi_m=hi_m, tp_done=tp_done):
                        vector.wait_ge(ct.h, tp_done)
                        vector.wait_ge(cs.h, hi_m)
                        for half in range(2):
                            dsth = xnT_hi[:].rearrange("p (c t) -> p c t", c=DC)[:, half * 4:half * 4 + 4,
                                                                                j * 128:(j + 1) * 128]
                            dstl = xnT_lo[:].rearrange("p (c t) -> p c t", c=DC)[:, half * 4:half * 4 + 4,
                                                                                j * 128:(j + 1) * 128]
                            op = vector.tensor_sub(dstl,
                                                   ps_tr[half][:].rearrange("p (q t) -> p q t", q=4),
                                                   dsth)
                        cv.inc(op)
                    state["tr_evac"].append((cv.h, cv.n))

                def qk_matmuls(is_q, base_tok, xnT_ready):
                    woff = 0 if is_q else D
                    boff = 0 if is_q else DC
                    dsth = QH_s if is_q else KH_s
                    dstl = QL_s if is_q else KL_s
                    for dout in range(DC):
                        bank = dout % 2

                        @blk.tensor
                        def _(tensor, dout=dout, bank=bank, xnT_ready=xnT_ready, woff=woff):
                            tensor.wait_ge(xnT_ready[0], xnT_ready[1])
                            if len(state["qk_evac"]) >= 2:
                                tensor.wait_ge(cv.h, state["qk_evac"][-2])
                            base = woff + dout * 128
                            for ci in range(DC):
                                wsl = slice(ci * QKVD + base, ci * QKVD + base + 128)
                                xsl = slice(ci * 512, (ci + 1) * 512)
                                tensor.matmul(ps_qk[bank][:], wq_hi[:, wsl], xnT_hi[:, xsl],
                                              start=(ci == 0), stop=False)
                                tensor.matmul(ps_qk[bank][:], wq_hi[:, wsl], xnT_lo[:, xsl],
                                              start=False, stop=False)
                                op = tensor.matmul(ps_qk[bank][:], wq_lo[:, wsl], xnT_hi[:, xsl],
                                                   start=False, stop=(ci == DC - 1))
                            ct.inc(op)
                        mm_done = ct.n

                        @blk.vector
                        def _(vector, bank=bank, mm_done=mm_done, boff=boff, dout=dout,
                              psub=(state["qk_sub"][-2] if len(state["qk_sub"]) >= 2 else None)):
                            vector.wait_ge(ct.h, mm_done)
                            vector.wait_ge(st.h, st.n)
                            if psub is not None:
                                vector.wait_ge(cg.h, psub)
                            vector.tensor_scalar(out=ev[bank][:], in0=ps_qk[bank][:],
                                                 scalar1=qb_sb[:, boff + dout:boff + dout + 1],
                                                 scalar2=None, op0=ALU.add)
                            vector.drain()
                            cv.inc(vector.tensor_copy(ehi[bank][:], ev[bank][:]))
                        state["qk_evac"].append(cv.n)
                        evd = cv.n

                        @blk.gpsimd
                        def _(gpsimd, bank=bank, evd=evd):
                            gpsimd.wait_ge(cv.h, evd)
                            cg.inc(gpsimd.tensor_sub(elo[bank][:], ev[bank][:], ehi[bank][:]))
                        state["qk_sub"].append(cg.n)
                        subd = cg.n

                        @blk.scalar
                        def _(scalar, dout=dout, bank=bank, subd=subd, base_tok=base_tok,
                              dsth=dsth, dstl=dstl):
                            scalar.wait_ge(cg.h, subd)
                            scalar.dma_start(out=dsth[dout, :, base_tok:base_tok + 512],
                                             in_=ehi[bank][:]).then_inc(st.h, 16)
                            scalar.dma_start(out=dstl[dout, :, base_tok:base_tok + 512],
                                             in_=elo[bank][:]).then_inc(st.h, 16)
                        st.n += 32

                def v_matmuls(base_tok, xnT_ready):
                    for t in range(4):
                        vb = t % 2

                        @blk.tensor
                        def _(tensor, t=t, xnT_ready=xnT_ready):
                            tensor.wait_ge(xnT_ready[0], xnT_ready[1])
                            if state["v_evacs"]:
                                tensor.wait_ge(cv.h, state["v_evacs"][-1])
                            for half in range(2):
                                for ci in range(DC):
                                    xsl = slice(ci * 512 + t * 128, ci * 512 + (t + 1) * 128)
                                    wsl = slice(ci * QKVD + 2 * D + half * 512,
                                                ci * QKVD + 2 * D + (half + 1) * 512)
                                    tensor.matmul(ps_v[half][:], xnT_hi[:, xsl], wq_hi[:, wsl],
                                                  start=(ci == 0), stop=False)
                                    tensor.matmul(ps_v[half][:], xnT_hi[:, xsl], wq_lo[:, wsl],
                                                  start=False, stop=False)
                                    op = tensor.matmul(ps_v[half][:], xnT_lo[:, xsl], wq_hi[:, wsl],
                                                       start=False, stop=(ci == DC - 1))
                            ct.inc(op)
                        vmm = ct.n

                        @blk.vector
                        def _(vector, vb=vb, vmm=vmm,
                              psub=(state["v_sub"][-2] if len(state["v_sub"]) >= 2 else None)):
                            vector.wait_ge(ct.h, vmm)
                            vector.wait_ge(st.h, st.n)
                            if psub is not None:
                                vector.wait_ge(cg.h, psub)
                            vector.tensor_tensor(out=vout[vb][:, 0:512], in0=ps_v[0][:],
                                                 in1=bvb[:, 0:512], op=ALU.add)
                            vector.tensor_tensor(out=vout[vb][:, 512:1024], in0=ps_v[1][:],
                                                 in1=bvb[:, 512:1024], op=ALU.add)
                            vector.drain()
                            cv.inc(vector.tensor_copy(vhi[vb][:], vout[vb][:]))
                        state["v_evacs"].append(cv.n)
                        evd = cv.n

                        @blk.gpsimd
                        def _(gpsimd, vb=vb, evd=evd):
                            gpsimd.wait_ge(cv.h, evd)
                            cg.inc(gpsimd.tensor_sub(vlo[vb][:], vout[vb][:], vhi[vb][:]))
                        state["v_sub"].append(cg.n)
                        vsub = cg.n

                        @blk.sync
                        def _(sync, t=t, vb=vb, vsub=vsub, base_tok=base_tok):
                            sync.wait_ge(cg.h, vsub)
                            sync.dma_start(
                                out=VH_s[base_tok + t * 128: base_tok + (t + 1) * 128, :],
                                in_=vhi[vb][:]).then_inc(st.h, 16)
                            sync.dma_start(
                                out=VL_s[base_tok + t * 128: base_tok + (t + 1) * 128, :],
                                in_=vlo[vb][:]).then_inc(st.h, 16)
                        st.n += 32

                def router_mm(xnT_ready, part):
                    off = part * E

                    @blk.tensor
                    def _(tensor, xnT_ready=xnT_ready, off=off):
                        tensor.wait_ge(xnT_ready[0], xnT_ready[1])
                        for ci in range(DC):
                            xc = slice(ci * 512, ci * 512 + 1)
                            wc = slice(ci * E, (ci + 1) * E)
                            tensor.matmul(ps_bc[0:1, off:off + E], xnT_hi[:, xc], wr_hi[:, wc],
                                          start=(ci == 0), stop=False)
                            tensor.matmul(ps_bc[0:1, off:off + E], xnT_hi[:, xc], wr_lo[:, wc],
                                          start=False, stop=False)
                            op = tensor.matmul(ps_bc[0:1, off:off + E], xnT_lo[:, xc],
                                               wr_hi[:, wc], start=False, stop=(ci == DC - 1))
                        ct.inc(op)
                    rt = ct.n

                    @blk.vector
                    def _(vector, rt=rt, off=off):
                        vector.wait_ge(ct.h, rt)
                        cv.inc(vector.tensor_copy(sab[0:1, off:off + E], ps_bc[0:1, off:off + E]))
                    return cv.n

                def router_combine(r2_done):
                    @blk.vector
                    def _(vector, r2_done=r2_done):
                        vector.wait_ge(cv.h, r2_done)
                        vector.wait_ge(ld.h, pre_ld)
                        vector.drain()
                        vector.scalar_tensor_tensor(out=sc8[:], in0=sab[0:1, 0:E],
                                                    scalar=rs_sb[0:1, 0:1], in1=sc8[:],
                                                    op0=ALU.mult, op1=ALU.add)
                        vector.drain()
                        vector.scalar_tensor_tensor(out=sc8[:], in0=sab[0:1, E:2 * E],
                                                    scalar=rs_sb[0:1, 1:2], in1=sc8[:],
                                                    op0=ALU.mult, op1=ALU.add)
                        vector.drain()
                        vector.reduce_max(mx[0:1, 0:1], sc8[:], axis=AX.X)
                        vector.drain()
                        vector.tensor_scalar(out=mk12[0:1, 0:E], in0=sc8[:], scalar1=mx[0:1, 0:1],
                                             scalar2=None, op0=ALU.is_ge)
                        vector.drain()
                        vector.scalar_tensor_tensor(out=sc8[:], in0=mk12[0:1, 0:E], scalar=-1e30,
                                                    in1=sc8[:], op0=ALU.mult, op1=ALU.add)
                        vector.drain()
                        vector.reduce_max(mx[0:1, 1:2], sc8[:], axis=AX.X)
                        vector.drain()
                        cv.inc(vector.tensor_scalar(out=mk12[0:1, E:2 * E], in0=sc8[:],
                                                    scalar1=mx[0:1, 1:2], scalar2=None, op0=ALU.is_ge))
                    msk = cv.n

                    @blk.tensor
                    def _(tensor, msk=msk):
                        tensor.wait_ge(cv.h, msk)
                        tensor.matmul(ps_bc[:, 256:256 + E], ones[0:1, :], mk12[0:1, 0:E],
                                      start=True, stop=True)
                        ct.inc(tensor.matmul(ps_bc[:, 384:384 + E], ones[0:1, :], mk12[0:1, E:2 * E],
                                             start=True, stop=True))
                    bca = ct.n

                    @blk.vector
                    def _(vector, bca=bca):
                        vector.wait_ge(ct.h, bca)
                        vector.tensor_copy(m1b[:], ps_bc[:, 256:256 + E])
                        cv.inc(vector.tensor_copy(m2b[:], ps_bc[:, 384:384 + E]))
                    mdone = cv.n

                    @blk.gpsimd
                    def _(gpsimd):
                        cg.inc(gpsimd.iota(iot8[:], [[128, E]], base=0, channel_multiplier=1))
                    iota8_done = cg.n

                    @blk.vector
                    def _(vector, iota8_done=iota8_done):
                        vector.wait_ge(cg.h, iota8_done)
                        vector.drain()
                        vector.tensor_copy(iot8f[:], iot8[:])
                        vector.tensor_scalar(out=ew[:], in0=iot[0:1, 0:E], scalar1=float(D),
                                             scalar2=None, op0=ALU.mult)
                        vector.drain()
                        vector.tensor_tensor(out=pr2[0:1, 0:E], in0=mk12[0:1, 0:E],
                                             in1=ew[:], op=ALU.mult)
                        vector.tensor_tensor(out=pr2[0:1, E:2 * E], in0=mk12[0:1, E:2 * E],
                                             in1=ew[:], op=ALU.mult)
                        vector.drain()
                        vector.reduce_sum(es2[0:1, 0:1], pr2[0:1, 0:E], axis=AX.X)
                        cv.inc(vector.reduce_sum(es2[0:1, 1:2], pr2[0:1, E:2 * E], axis=AX.X))
                    es_done = cv.n

                    @blk.tensor
                    def _(tensor, es_done=es_done):
                        tensor.wait_ge(cv.h, es_done)
                        tensor.matmul(ps_bc[:, 500:501], ones[0:1, :], es2[0:1, 0:1],
                                      start=True, stop=True)
                        ct.inc(tensor.matmul(ps_bc[:, 501:502], ones[0:1, :], es2[0:1, 1:2],
                                             start=True, stop=True))
                    eb_done = ct.n

                    @blk.vector
                    def _(vector, eb_done=eb_done):
                        vector.wait_ge(ct.h, eb_done)
                        for k2 in range(2):
                            op = vector.tensor_scalar(out=idxtab[:, k2 * E:(k2 + 1) * E],
                                                      in0=iot8f[:],
                                                      scalar1=ps_bc[:, 500 + k2:501 + k2],
                                                      scalar2=None, op0=ALU.add)
                        cv.inc(op)
                    idx_info["done"] = cv.n

                    if debug:
                        @blk.gpsimd
                        def _(gpsimd, mdone=mdone):
                            gpsimd.wait_ge(cv.h, mdone)
                            gpsimd.dma_start(out=MB_s[0], in_=m1b[:]).then_inc(st.h, 16)
                            gpsimd.dma_start(out=MB_s[1], in_=m2b[:]).then_inc(st.h, 16)
                        st.n += 32

                r_parts = {}
                for sc in range(8):
                    base_tok = sc * 512
                    for j in range(4):
                        ln_chunk(x_kv[base_tok + j * 128: base_tok + (j + 1) * 128, :], j)
                    xnT_ready = state["tr_evac"][-1]
                    qk_matmuls(False, base_tok, xnT_ready)
                    v_matmuls(base_tok, xnT_ready)
                    if sc < 4:
                        @blk.sync
                        def _(sync, base_tok=base_tok, xnT_ready=xnT_ready):
                            sync.wait_ge(xnT_ready[0], xnT_ready[1])
                            sync.dma_start(
                                out=XNH_s[:, :, base_tok:base_tok + 512].rearrange("c p t -> p c t"),
                                in_=xnT_hi[:].rearrange("p (c t) -> p c t", c=DC)).then_inc(st.h, 16)
                            sync.dma_start(
                                out=XNL_s[:, :, base_tok:base_tok + 512].rearrange("c p t -> p c t"),
                                in_=xnT_lo[:].rearrange("p (c t) -> p c t", c=DC)).then_inc(st.h, 16)
                        st.n += 32
                        qk_matmuls(True, base_tok, xnT_ready)
                    if sc == 0:
                        r_parts[0] = router_mm(xnT_ready, 0)
                    if sc == 4:
                        r_parts[1] = router_mm(xnT_ready, 1)
                        router_combine(r_parts[1])
                    tdone = ct.n

                    @blk.gpsimd
                    def _(gpsimd, tdone=tdone):
                        gpsimd.wait_ge(ct.h, tdone)
                        gpsimd.wait_ge(st.h, st.n)
                        cg.inc(gpsimd.engine_nop())
                    state["tr_evac"].append((cg.h, cg.n))

                @blk.gpsimd
                def _(gpsimd):
                    gpsimd.wait_ge(st.h, st.n)


        if debug == "attnonly":
            with contextlib.ExitStack() as p:
                blk = p.enter_context(nc.Block("pre"))

                @blk.gpsimd
                def _(gpsimd):
                    cg.inc(gpsimd.memset(ones[:], 1.0))
        else:
            phase_p0()

        if debug == "p0":
            return nc

        projacc = g.enter_context(nc.sbuf_tensor("projacc", [128, DC * NQ], F32))

        with contextlib.ExitStack() as p:
            BF = mybir.dt.bfloat16
            kst = [p.enter_context(nc.sbuf_tensor(f"kst{i}", [128, NTOK], BF)) for i in range(2)]
            qab = [p.enter_context(nc.sbuf_tensor(f"qab{i}", [128, NQ], BF)) for i in range(2)]
            qcd = [p.enter_context(nc.sbuf_tensor(f"qcd{i}", [128, NQ], BF)) for i in range(2)]
            vA = [p.enter_context(nc.sbuf_tensor(f"vA{i}", [128, 32 * (HD + 1)], BF)) for i in range(2)]
            vB = [p.enter_context(nc.sbuf_tensor(f"vB{i}", [128, 32 * HD], BF)) for i in range(2)]
            p32 = [p.enter_context(nc.sbuf_tensor(f"p32_{i}", [128, 1024], F32)) for i in range(2)]
            phi = [p.enter_context(nc.sbuf_tensor(f"phi{i}", [128, 1024], BF)) for i in range(4)]
            plo = [p.enter_context(nc.sbuf_tensor(f"plo{i}", [128, 1024], BF)) for i in range(4)]
            rzb = p.enter_context(nc.sbuf_tensor("rzb", [128, 1024], F32))
            t1 = p.enter_context(nc.sbuf_tensor("t1", [64, 1024], F32))
            aout = p.enter_context(nc.sbuf_tensor("aout", [64, 1024], F32))
            apair = p.enter_context(nc.sbuf_tensor("apair", [128, NQ], F32))
            aph = p.enter_context(nc.sbuf_tensor("aph", [128, NQ], BF))
            apl = p.enter_context(nc.sbuf_tensor("apl", [128, NQ], BF))
            pw_sb = [p.enter_context(nc.sbuf_tensor(f"pw_sb{i}", [128, D], F32)) for i in range(2)]
            pwh = [p.enter_context(nc.sbuf_tensor("pwh0", [128, D], BF))] * 2
            pwl = [p.enter_context(nc.sbuf_tensor("pwl0", [128, D], BF))] * 2
            nshift = p.enter_context(nc.sbuf_tensor("nshift", [128, 1], F32))
            ps_s = [p.enter_context(nc.psum_tensor(f"ps_s{i}", [128, 1024], F32)) for i in range(2)]
            ps_ab = [p.enter_context(nc.psum_tensor(f"ps_ab{i}", [65, 512], F32)) for i in range(2)]
            ps_a2 = [p.enter_context(nc.psum_tensor(f"ps_a2{i}", [128, 512], F32)) for i in range(2)]

            blk = p.enter_context(nc.Block("attn"))

            @blk.gpsimd
            def _(gpsimd):
                for i in range(2):
                    gpsimd.memset(
                        vA[i][:].rearrange("p (kc w) -> p kc w", w=HD + 1)[:, :, HD:HD + 1], 1.0)
                cg.inc(gpsimd.memset(nshift[:], -SHIFT))
            vones = cg.n

            head_ld = {}
            pw_split_by_pair = {}
            conv_done = {}
            head_free = {}
            proj_ev_by_pair = {}
            A = {"o_free": None, "proj_done": None, "prev_exp_last": None,
                 "prev_vhl_last": None, "prev_pv_last": None,
                 "apair_free": None, "apair_split": None, "aout_free": None,
                 "ps_s_free": None, "staging_free": None}

            def load_head(h):
                i = h % 2
                b64 = 64 * (h % 2)

                @blk.sync
                def _(sync, h=h, i=i, b64=b64):
                    if h >= 2:
                        sem, val = head_free[h - 2]
                        sync.wait_ge(sem, val)
                    if h == 0:
                        sync.wait_ge(cg.h, vones)
                    sync.dma_start(out=kst[i][0:64, :],
                                   in_=KH_s[h // 2, b64:b64 + 64, :]).then_inc(ld_hd[i].h, 16)
                    sync.dma_start(out=kst[i][64:128, :],
                                   in_=KL_s[h // 2, b64:b64 + 64, :]).then_inc(ld_hd[i].h, 16)
                    for half in range(2):
                        sync.dma_start(out=qab[i][64 * half:64 * half + 64, :],
                                       in_=QH_s[h // 2, b64:b64 + 64, :]).then_inc(ld_hd[i].h, 16)
                        sync.dma_start(out=qcd[i][64 * half:64 * half + 64, :],
                                       in_=QL_s[h // 2, b64:b64 + 64, :]).then_inc(ld_hd[i].h, 16)
                    for kg in range(4):
                        sync.dma_start(
                            out=vA[i][:].rearrange("p (kc w) -> p kc w", w=HD + 1)[:, kg * 8:(kg + 1) * 8, 0:HD],
                            in_=VH_s.rearrange("(kc p) c -> p kc c", p=128)[:, kg * 8:(kg + 1) * 8,
                                                                           h * HD:(h + 1) * HD]).then_inc(ld_hd[i].h, 16)
                        sync.dma_start(
                            out=vB[i][:].rearrange("p (kc w) -> p kc w", w=HD)[:, kg * 8:(kg + 1) * 8, :],
                            in_=VL_s.rearrange("(kc p) c -> p kc c", p=128)[:, kg * 8:(kg + 1) * 8,
                                                                           h * HD:(h + 1) * HD]).then_inc(ld_hd[i].h, 16)
                    if h % 2 == 0:
                        q = h // 2
                        if q >= 2:
                            sync.wait_ge(cv.h, pw_split_by_pair[q - 2])
                        sync.dma_start(out=pw_sb[q % 2][:], in_=wproj[q]).then_inc(ld_pw[q % 2].h, 16)
                ld_hd[i].n += 14 * 16
                if h % 2 == 0:
                    ld_pw[(h // 2) % 2].n += 16
                head_ld[h] = ld_hd[i].n
                conv_done[h] = None


            load_head(0)

            n_heads = 2 if debug == "attn1" else H

            def emit_pva(h, i, s_, phi_done, pva_done):
                qh, kc = divmod(s_, 32)
                pbk = s_ % 4

                @blk.tensor
                def _(tensor, i=i, s_=s_, kc=kc, pbk=pbk):
                    pd = phi_done[s_]
                    tensor.wait_ge(pd[0], pd[1])
                    if kc == 0 and A["o_free"] is not None:
                        tensor.wait_ge(cv.h, A["o_free"])
                    for qq in range(2):
                        tensor.matmul(ps_ab[qq][:],
                                      vA[i][:, kc * (HD + 1):(kc + 1) * (HD + 1)],
                                      phi[pbk][:, qq * 512:(qq + 1) * 512],
                                      start=(kc == 0), stop=False)
                    for qq in range(2):
                        op = tensor.matmul(ps_a2[qq][0:64, :],
                                           vB[i][:, kc * HD:(kc + 1) * HD],
                                           phi[pbk][:, qq * 512:(qq + 1) * 512],
                                           start=(kc == 0), stop=(kc == 31))
                    ct.inc(op)
                pva_done.append(ct.n)

            def emit_pvb(h, i, s_, plo_done, pvb_done):
                qh, kc = divmod(s_, 32)
                pbk = s_ % 4

                @blk.tensor
                def _(tensor, i=i, s_=s_, kc=kc, pbk=pbk):
                    pl = plo_done[s_]
                    tensor.wait_ge(pl[0], pl[1])
                    for qq in range(2):
                        op = tensor.matmul(ps_ab[qq][:],
                                           vA[i][:, kc * (HD + 1):(kc + 1) * (HD + 1)],
                                           plo[pbk][:, qq * 512:(qq + 1) * 512],
                                           start=False, stop=(kc == 31))
                    ct.inc(op)
                pvb_done.append(ct.n)

            def finalize_a(h, qh, pvb_done):
                pvl = pvb_done[qh * 32 + 31]

                @blk.vector
                def _(vector, pvl=pvl):
                    vector.wait_ge(ct.h, pvl)
                    for qq in range(2):
                        vector.reciprocal(out=rzb[64:65, qq * 512:(qq + 1) * 512],
                                          in_=ps_ab[qq][64:65, :])
                    for qq in range(2):
                        op = vector.tensor_copy(t1[:, qq * 512:(qq + 1) * 512],
                                                ps_ab[qq][0:64, :])
                    cv.inc(op)

                @blk.vector
                def _(vector):
                    vector.drain()
                    for qq in range(2):
                        op = vector.tensor_tensor(out=t1[:, qq * 512:(qq + 1) * 512],
                                                  in0=ps_a2[qq][0:64, :],
                                                  in1=t1[:, qq * 512:(qq + 1) * 512], op=ALU.add)
                    cv.inc(op)
                A["o_free"] = cv.n
                return cv.n

            def finalize_b(h, qh, t1_done):
                qoff = qh * 1024

                @blk.tensor
                def _(tensor, t1_done=t1_done):
                    tensor.wait_ge(cv.h, t1_done)
                    for qq in range(2):
                        op = tensor.matmul(ps_a2[qq][64:128, :],
                                           ones[64:65, 0:64],
                                           rzb[64:65, qq * 512:(qq + 1) * 512],
                                           start=True, stop=True, tile_position=(64, 64),
                                           skip_group_check=True)
                    ct.inc(op)
                zb_done = ct.n

                @blk.vector
                def _(vector, zb_done=zb_done, qoff=qoff, h=h, qh=qh):
                    vector.wait_ge(ct.h, zb_done)
                    if h % 2 == 0 and A["apair_split"] is not None and qh == 0:
                        vector.wait_ge(cv.h, A["apair_split"])
                    if h % 2 == 1 and A["aout_free"] is not None:
                        vector.wait_ge(ld_mv.h, A["aout_free"])
                    for qq in range(2):
                        vector.tensor_copy(rzb[0:64, qq * 512:(qq + 1) * 512],
                                           ps_a2[qq][64:128, :])
                    vector.drain()
                    for qq in range(2):
                        if h % 2 == 0:
                            dsl = apair[0:64, qoff + qq * 512: qoff + (qq + 1) * 512]
                        else:
                            dsl = aout[0:64, qq * 512:(qq + 1) * 512]
                        op = vector.tensor_tensor(
                            out=dsl,
                            in0=t1[:, qq * 512:(qq + 1) * 512],
                            in1=rzb[0:64, qq * 512:(qq + 1) * 512], op=ALU.mult)
                    cv.inc(op)
                if h % 2 == 1:
                    halfdone = cv.n

                    @blk.gpsimd
                    def _(gpsimd, halfdone=halfdone, qoff=qoff):
                        gpsimd.wait_ge(cv.h, halfdone)
                        ld_mv.inc(gpsimd.dma_start(out=apair[64:128, qoff:qoff + 1024],
                                                   in_=aout[0:64, :]), 16)
                    A["aout_free"] = ld_mv.n
                return cv.n

            for h in range(n_heads):
                i = h % 2
                if h + 1 < n_heads:
                    load_head(h + 1)

                exp_done = []
                phi_done = []
                plo_done = []
                pva_done = []
                pvb_done = []
                pv_done = None
                for qh in range(2):
                    qoff = qh * 1024
                    run = cs.n
                    for kc in range(32):
                        run += 2 if (qh * 32 + kc) % 2 == 0 else 1
                        exp_done.append(run)
                    scalar_recs = []
                    for kc in range(32):
                        s_ = qh * 32 + kc
                        pbk = s_ % 2
                        pb4 = s_ % 4

                        @blk.tensor
                        def _(tensor, kc=kc, s_=s_, pbk=pbk, i=i, qoff=qoff, h=h):
                            if s_ == 0:
                                tensor.wait_ge(ld_hd[h % 2].h, head_ld[h])
                                if A["ps_s_free"] is not None:
                                    tensor.wait_ge(cv.h, A["ps_s_free"])
                            if s_ >= 2:
                                tensor.wait_ge(cs.h, exp_done[s_ - 2])
                            elif A["prev_exp_last"] is not None:
                                tensor.wait_ge(cs.h, A["prev_exp_last"])
                            for qq in range(2):
                                tensor.matmul(ps_s[pbk][:, qq * 512:(qq + 1) * 512],
                                              kst[i][:, kc * 128:(kc + 1) * 128],
                                              qab[i][:, qoff + qq * 512: qoff + (qq + 1) * 512],
                                              start=True, stop=False)
                            for qq in range(2):
                                op = tensor.matmul(ps_s[pbk][:, qq * 512:(qq + 1) * 512],
                                                   kst[i][:, kc * 128:(kc + 1) * 128],
                                                   qcd[i][:, qoff + qq * 512: qoff + (qq + 1) * 512],
                                                   start=False, stop=True)
                            ct.inc(op)
                        p32_wait = None
                        if s_ >= 2:
                            p32_wait = plo_done[s_ - 2]
                        elif A["prev_vhl_last"] is not None:
                            p32_wait = A["prev_vhl_last"]
                        phibuf_wait = None
                        if s_ >= 4:
                            phibuf_wait = pva_done[s_ - 4]
                        elif A["prev_pv_last"] is not None:
                            phibuf_wait = A["prev_pv_last"]
                        even = (s_ % 2 == 0)
                        scalar_recs.append((ct.n, p32_wait, pbk, pb4, even, phibuf_wait))

                        if even:
                            phi_done.append((cs.h, exp_done[s_] - 1))

                            @blk.vector
                            def _(vector, s_=s_, pb4=pb4, pbk=pbk):
                                vector.wait_ge(cs.h, exp_done[s_])
                                if s_ >= 4:
                                    vector.wait_ge(ct.h, pvb_done[s_ - 4])
                                cv.inc(vector.tensor_sub(plo[pb4][:], p32[pbk][:],
                                                         phi[pb4][:]))
                            plo_done.append((cv.h, cv.n))
                        else:
                            @blk.vector
                            def _(vector, s_=s_, pb4=pb4, pbk=pbk):
                                vector.wait_ge(cs.h, exp_done[s_])
                                if s_ >= 4:
                                    vector.wait_ge(ct.h, pva_done[s_ - 4])
                                elif A["prev_pv_last"] is not None:
                                    vector.wait_ge(ct.h, A["prev_pv_last"])
                                cv.inc(vector.tensor_copy(phi[pb4][:], p32[pbk][:]))
                            phi_done.append((cv.h, cv.n))

                            @blk.gpsimd
                            def _(gpsimd, s_=s_, pb4=pb4, pbk=pbk, pd=phi_done[-1]):
                                gpsimd.wait_ge(pd[0], pd[1])
                                if s_ >= 4:
                                    gpsimd.wait_ge(ct.h, pvb_done[s_ - 4])
                                cg.inc(gpsimd.tensor_sub(plo[pb4][:], p32[pbk][:],
                                                         phi[pb4][:]))
                            plo_done.append((cg.h, cg.n))

                        if kc >= 1:
                            emit_pva(h, i, s_ - 1, phi_done, pva_done)
                        if kc >= 3:
                            emit_pvb(h, i, s_ - 3, plo_done, pvb_done)
                        if kc == 4 and A.get("pend_fb"):
                            ph, pq, pt1 = A["pend_fb"]
                            A["pend_fb"] = None
                            finalize_b(ph, pq, pt1)

                    emit_pva(h, i, qh * 32 + 31, phi_done, pva_done)
                    for sx in (29, 30, 31):
                        emit_pvb(h, i, qh * 32 + sx, plo_done, pvb_done)

                    @blk.scalar
                    def _(scalar, recs=tuple(scalar_recs)):
                        for (ctv, p32w, pbk2, pb42, ev, phw) in recs:
                            scalar.wait_ge(ct.h, ctv)
                            if ev:
                                if phw is not None:
                                    scalar.wait_ge(ct.h, phw)
                                op = scalar.activation(out=phi[pb42][:], in_=ps_s[pbk2][:],
                                                       func=AF.Exp, scale=0.125,
                                                       bias=nshift[:])
                                op.then_inc(cs.h, 1)
                                cs.n += 1
                            if p32w is not None:
                                scalar.wait_ge(p32w[0], p32w[1])
                            op = scalar.activation(out=p32[pbk2][:], in_=ps_s[pbk2][:],
                                                   func=AF.Exp, scale=0.125, bias=nshift[:])
                            op.then_inc(cs.h, 1)
                            cs.n += 1
                    assert cs.n == exp_done[qh * 32 + 31], (cs.n, exp_done[qh * 32 + 31])
                    t1v = finalize_a(h, qh, pvb_done)
                    if h % 2 == 1 and qh == 1:
                        finalize_b(h, qh, t1v)
                    else:
                        A["pend_fb"] = (h, qh, t1v)

                scaled = A["o_free"]

                head_free[h] = (ct.h, pvb_done[-1])
                A["prev_exp_last"] = exp_done[-1]
                A["prev_vhl_last"] = plo_done[-1]
                A["prev_pv_last"] = pvb_done[-1]

                if h % 2 == 1:
                    mv_done = ld_mv.n
                    pair = h // 2

                    @blk.vector
                    def _(vector, mv_done=mv_done, pair=pair, h=h):
                        vector.wait_ge(ld_mv.h, mv_done)
                        if pair >= 1:
                            vector.wait_ge(ct.h, A["apair_free"])
                        q2 = (h // 2) % 2
                        vector.wait_ge(ld_pw[q2].h, ld_pw[q2].n)
                        vector.tensor_copy(pwh[q2][:], pw_sb[q2][:])
                        vector.tensor_copy(aph[:], apair[:])
                        vector.drain()
                        vector.tensor_sub(pwl[q2][:], pw_sb[q2][:], pwh[q2][:])
                        cv.inc(vector.tensor_sub(apl[:], apair[:], aph[:]))
                    ap_split = cv.n
                    pw_split_by_pair[pair] = cv.n

                    pr_mm = []
                    pr_ev = []
                    for idx in range(32):
                        dout, qq = divmod(idx, 4)
                        bank = ps_s[(idx % 4) // 2]
                        slc = (idx % 2) * 512

                        @blk.tensor
                        def _(tensor, idx=idx, dout=dout, qq=qq, bank=bank, slc=slc,
                              ap_split=ap_split, h=h):
                            if idx == 0:
                                tensor.wait_ge(cv.h, ap_split)
                                tensor.wait_ge(cv.h, pw_split_by_pair[h // 2])
                            if idx >= 4:
                                tensor.wait_ge(cv.h, pr_ev[idx - 4])
                            q2 = (h // 2) % 2
                            wsl = slice(dout * 128, (dout + 1) * 128)
                            asl = slice(qq * 512, (qq + 1) * 512)
                            tensor.matmul(bank[:, slc:slc + 512], pwh[q2][:, wsl], aph[:, asl],
                                          start=True, stop=False)
                            tensor.matmul(bank[:, slc:slc + 512], pwh[q2][:, wsl], apl[:, asl],
                                          start=False, stop=False)
                            ct.inc(tensor.matmul(bank[:, slc:slc + 512], pwl[q2][:, wsl],
                                                 aph[:, asl], start=False, stop=True))
                        pr_mm.append(ct.n)

                        @blk.vector
                        def _(vector, idx=idx, dout=dout, qq=qq, bank=bank, slc=slc, pair=pair):
                            vector.wait_ge(ct.h, pr_mm[idx])
                            dst2 = projacc[:, dout * NQ + qq * 512: dout * NQ + (qq + 1) * 512]
                            if pair == 0:
                                cv.inc(vector.tensor_copy(dst2, bank[:, slc:slc + 512]))
                            else:
                                cv.inc(vector.tensor_tensor(out=dst2, in0=bank[:, slc:slc + 512],
                                                            in1=dst2, op=ALU.add))
                        pr_ev.append(cv.n)

                    proj_ev_by_pair[pair] = pr_ev[-1]
                    A["proj_done"] = pr_ev[-1]
                    A["ps_s_free"] = pr_ev[-1]
                    A["apair_free"] = pr_mm[-1]
                    A["apair_split"] = ap_split

            if debug in ("attn", "attn1", "attnonly"):
                final_proj = A["proj_done"]

                @blk.gpsimd
                def _(gpsimd):
                    gpsimd.wait_ge(cv.h, final_proj)
                    for ci in range(DC):
                        gpsimd.dma_start(out=AP_s[ci],
                                         in_=projacc[:, ci * NQ:(ci + 1) * NQ]).then_inc(st.h, 16)
                    st.n += 16 * DC
                    gpsimd.wait_ge(st.h, st.n)

        if debug in ("attn", "attn1", "attnonly"):
            return nc

        with contextlib.ExitStack() as p:
            BFF = mybir.dt.bfloat16
            wsel = p.enter_context(nc.sbuf_tensor("wsel", [128, D], F32))
            w1h = p.enter_context(nc.sbuf_tensor("w1h", [128, DC * D], BFF))
            w1l = p.enter_context(nc.sbuf_tensor("w1l", [128, DC * D], BFF))
            w2h = p.enter_context(nc.sbuf_tensor("w2h", [128, DC * D], BFF))
            w2l = p.enter_context(nc.sbuf_tensor("w2l", [128, DC * D], BFF))
            hbuf = p.enter_context(nc.sbuf_tensor("hbuf", [128, 4 * 512], F32))
            hh = [p.enter_context(nc.sbuf_tensor(f"hh{i}", [128, DC * 512], BFF)) for i in range(2)]
            hl = [p.enter_context(nc.sbuf_tensor(f"hl{i}", [128, DC * 512], BFF)) for i in range(2)]
            wtile = [p.enter_context(nc.sbuf_tensor(f"wtile{i}", [128, D], F32)) for i in range(2)]
            xnh8 = p.enter_context(nc.sbuf_tensor("xnh8", [128, DC * 512], BFF))
            xnl8 = p.enter_context(nc.sbuf_tensor("xnl8", [128, DC * 512], BFF))
            btile = p.enter_context(nc.sbuf_tensor("btile", [128, E], F32))
            b2s = p.enter_context(nc.sbuf_tensor("b2s", [128, 2 * E], F32))
            pb_t = p.enter_context(nc.sbuf_tensor("pb_t", [128, DC], F32))
            ps_f = [p.enter_context(nc.psum_tensor(f"ps_f{i}", [128, 512], F32)) for i in range(DC)]

            blk = p.enter_context(nc.Block("ffn"))

            @blk.sync
            def _(sync):
                sync.dma_start(out=pb_t[:], in_=bproj.rearrange("(j p) -> p j", p=128),
                               allow_slow_non_contiguous=True).then_inc(ld_bt.h, 16)
            ld_bt.n += 16
            pb_ld = ld_bt.n

            for bi, (bsrc, bdst) in enumerate(((b1, b1sel), (b2, b2s))):
                for e in range(E):
                    @blk.sync
                    def _(sync, bsrc=bsrc, e=e):
                        sync.wait_ge(cv.h, cv.n)
                        sync.dma_start(out=btile[:],
                                       in_=bsrc[e].rearrange("(j p) -> p j", p=128),
                                       allow_slow_non_contiguous=True).then_inc(ld_bt.h, 16)
                    ld_bt.n += 16
                    bld = ld_bt.n

                    @blk.vector
                    def _(vector, bdst=bdst, e=e, bld=bld):
                        vector.wait_ge(ld_bt.h, bld)
                        for k, msk in ((0, m1b), (1, m2b)):
                            dst2 = bdst[:, k * E:(k + 1) * E]
                            if e == 0:
                                op = vector.tensor_scalar(out=dst2, in0=btile[:],
                                                          scalar1=msk[:, e:e + 1], scalar2=None,
                                                          op0=ALU.mult)
                            else:
                                op = vector.scalar_tensor_tensor(out=dst2, in0=btile[:],
                                                                 scalar=msk[:, e:e + 1], in1=dst2,
                                                                 op0=ALU.mult, op1=ALU.add)
                        vector.drain()
                        cv.inc(op)

            @blk.vector
            def _(vector, full=None):
                vector.wait_ge(ld_bt.h, ld_bt.n)
                vector.drain()
                vector.tensor_tensor(out=bias_all[:], in0=b2s[:, 0:E], in1=b2s[:, E:2 * E],
                                     op=ALU.add)
                vector.drain()
                cv.inc(vector.scalar_tensor_tensor(out=bias_all[:], in0=bias_all[:], scalar=0.5,
                                                   in1=pb_t[:], op0=ALU.mult, op1=ALU.add))

            wsel_marks = []
            sel_ci = {}
            wfree = {"w1": None, "w2": None}
            R = {"gelu_last": None,
                 "oev_last": None,
                 "omm_par": [None, None],
                 "hbuf_free": None,
                 "xn_free": None}

            def emit_h(k, t, w, my_xld):
                pg = R["gelu_last"]

                @blk.tensor
                def _(tensor, k=k, t=t, w=w, my_xld=my_xld, pg=pg):
                    tensor.wait_ge(ld_xn[0].h, my_xld)
                    if pg is not None:
                        tensor.wait_ge(cs.h, pg)
                    for ci in range(DC):
                        if t == 0 and w == 0:
                            tensor.wait_ge(cg.h, sel_ci[(k, "w1", ci)])
                        xsl = slice(ci * 512, (ci + 1) * 512)
                        for dout in range(4 * w, 4 * w + 4):
                            wsl = slice(ci * D + dout * 128, ci * D + (dout + 1) * 128)
                            tensor.matmul(ps_f[dout % 4][:], w1h[:, wsl], xnh8[:, xsl],
                                          start=(ci == 0), stop=False)
                            tensor.matmul(ps_f[dout % 4][:], w1h[:, wsl], xnl8[:, xsl],
                                          start=False, stop=False)
                            op = tensor.matmul(ps_f[dout % 4][:], w1l[:, wsl], xnh8[:, xsl],
                                               start=False, stop=(ci == DC - 1))
                    ct.inc(op)
                return ct.n

            def emit_gelu(k, t, w, hmmv):
                hhfree = R["omm_par"][t % 2]
                hbfree = R["hbuf_free"]

                @blk.scalar
                def _(scalar, k=k, t=t, w=w, hmmv=hmmv, hhfree=hhfree, hbfree=hbfree):
                    scalar.wait_ge(ct.h, hmmv)
                    if hhfree is not None:
                        scalar.wait_ge(ct.h, hhfree)
                    if hbfree is not None:
                        scalar.wait_ge(cv.h, hbfree)
                    for dj, dout in enumerate(range(4 * w, 4 * w + 4)):
                        b = b1sel[:, k * E + dout:k * E + dout + 1]
                        scalar.activation(out=hh[t % 2][:, dout * 512:(dout + 1) * 512],
                                          in_=ps_f[dout % 4][:], func=AF.Gelu, bias=b)
                        op = scalar.activation(out=hbuf[:, dj * 512:(dj + 1) * 512],
                                               in_=ps_f[dout % 4][:], func=AF.Gelu, bias=b)
                    cs.inc(op)
                R["gelu_last"] = cs.n
                return cs.n

            def emit_sub(k, t, w, gv):
                hlfree = R["omm_par"][t % 2]

                @blk.vector
                def _(vector, t=t, w=w, gv=gv, hlfree=hlfree):
                    vector.wait_ge(cs.h, gv)
                    if hlfree is not None:
                        vector.wait_ge(ct.h, hlfree)
                    for dj, dout in enumerate(range(4 * w, 4 * w + 4)):
                        op = vector.tensor_sub(hl[t % 2][:, dout * 512:(dout + 1) * 512],
                                               hbuf[:, dj * 512:(dj + 1) * 512],
                                               hh[t % 2][:, dout * 512:(dout + 1) * 512])
                    cv.inc(op)
                R["hbuf_free"] = cv.n
                return cv.n

            def emit_o(k, t, w, subv):
                oevfree = R["oev_last"]

                @blk.tensor
                def _(tensor, k=k, t=t, w=w, subv=subv, oevfree=oevfree):
                    tensor.wait_ge(cv.h, subv)
                    if oevfree is not None:
                        tensor.wait_ge(cv.h, oevfree)
                    for ci2 in range(DC):
                        if t == 0 and w == 0:
                            tensor.wait_ge(cg.h, sel_ci[(k, "w2", ci2)])
                        hsl = slice(ci2 * 512, (ci2 + 1) * 512)
                        for dout in range(4 * w, 4 * w + 4):
                            wsl = slice(ci2 * D + dout * 128, ci2 * D + (dout + 1) * 128)
                            tensor.matmul(ps_f[4 + dout % 4][:], w2h[:, wsl],
                                          hh[t % 2][:, hsl], start=(ci2 == 0), stop=False)
                            tensor.matmul(ps_f[4 + dout % 4][:], w2h[:, wsl],
                                          hl[t % 2][:, hsl], start=False, stop=False)
                            op = tensor.matmul(ps_f[4 + dout % 4][:], w2l[:, wsl],
                                               hh[t % 2][:, hsl], start=False,
                                               stop=(ci2 == DC - 1))
                    ct.inc(op)
                R["omm_par"][t % 2] = ct.n
                return ct.n

            def emit_oev(k, t, w, ommv):
                @blk.vector
                def _(vector, t=t, w=w, ommv=ommv):
                    vector.wait_ge(ct.h, ommv)
                    for dout in range(4 * w, 4 * w + 4):
                        dst2 = projacc[:, dout * NQ + t * 512: dout * NQ + (t + 1) * 512]
                        op = vector.scalar_tensor_tensor(out=dst2, in0=ps_f[4 + dout % 4][:],
                                                         scalar=0.5, in1=dst2,
                                                         op0=ALU.mult, op1=ALU.add)
                    cv.inc(op)
                R["oev_last"] = cv.n
                return cv.n

            g_hist = []
            for k in range(2):
                for wsrc, wdh, wdl, tag in ((W1, w1h, w1l, "w1"), (W2, w2h, w2l, "w2")):
                    for ci in range(DC):
                        par2 = len(g_hist) % 2
                        prev2 = g_hist[-2] if len(g_hist) >= 2 else None
                        first = not g_hist

                        @blk.gpsimd
                        def _(gpsimd, wsrc=wsrc, k=k, ci=ci, par2=par2, prev2=prev2,
                              first=first):
                            if first:
                                gpsimd.wait_ge(cv.h, idx_info["done"])
                            if prev2 is not None:
                                gpsimd.wait_ge(cg.h, prev2)
                            gpsimd.indirect_dma_start(
                                out=wtile[par2][:], out_offset=None,
                                in_=wsrc.rearrange("e a b -> (e a) b"),
                                in_offset=bass.IndirectOffsetOnAxis(
                                    idxtab[:, k * E + ci:k * E + ci + 1], 0)).then_inc(ld_wt[par2].h, 16)
                        ld_wt[par2].n += 16
                        gld = ld_wt[par2].n

                        @blk.vector
                        def _(vector, wdh=wdh, ci=ci, par2=par2, gld=gld, tag=tag):
                            vector.wait_ge(ld_wt[par2].h, gld)
                            if ci == 0 and wfree[tag] is not None:
                                vector.wait_ge(ct.h, wfree[tag])
                            cv.inc(vector.tensor_copy(wdh[:, ci * D:(ci + 1) * D],
                                                      wtile[par2][:]))
                        hi_done = cv.n

                        @blk.gpsimd
                        def _(gpsimd, ci=ci, hi_done=hi_done, wdh=wdh, wdl=wdl, par2=par2):
                            gpsimd.wait_ge(cv.h, hi_done)
                            cg.inc(gpsimd.tensor_sub(wdl[:, ci * D:(ci + 1) * D],
                                                     wtile[par2][:],
                                                     wdh[:, ci * D:(ci + 1) * D]))
                        sel_ci[(k, tag, ci)] = cg.n
                        g_hist.append(cg.n)

                hmmd = {}
                subd = {}
                ommd = {}
                for t in range(4):
                    xf = R["xn_free"]

                    @blk.scalar
                    def _(scalar, t=t, xf=xf):
                        if xf is not None:
                            scalar.wait_ge(ct.h, xf)
                        for ci in range(DC):
                            scalar.dma_start(out=xnh8[:, ci * 512:(ci + 1) * 512],
                                             in_=XNH_s[ci, :, t * 512:(t + 1) * 512]).then_inc(ld_xn[0].h, 16)
                            scalar.dma_start(out=xnl8[:, ci * 512:(ci + 1) * 512],
                                             in_=XNL_s[ci, :, t * 512:(t + 1) * 512]).then_inc(ld_xn[0].h, 16)
                    ld_xn[0].n += 256
                    my_xld = ld_xn[0].n

                    if t >= 2:
                        ommd[(t - 2, 0)] = emit_o(k, t - 2, 0, subd[(t - 2, 1)])
                    hmmd[(t, 0)] = emit_h(k, t, 0, my_xld)
                    if t >= 2:
                        emit_oev(k, t - 2, 0, ommd[(t - 2, 0)])
                        ommd[(t - 2, 1)] = emit_o(k, t - 2, 1, subd[(t - 2, 1)])
                    gv0 = emit_gelu(k, t, 0, hmmd[(t, 0)])
                    subd[(t, 0)] = emit_sub(k, t, 0, gv0)
                    hmmd[(t, 1)] = emit_h(k, t, 1, my_xld)
                    R["xn_free"] = hmmd[(t, 1)]
                    if t >= 2:
                        emit_oev(k, t - 2, 1, ommd[(t - 2, 1)])
                    gv1 = emit_gelu(k, t, 1, hmmd[(t, 1)])
                    subd[(t, 1)] = emit_sub(k, t, 1, gv1)
                for tt in (2, 3):
                    ommd[(tt, 0)] = emit_o(k, tt, 0, subd[(tt, 1)])
                    emit_oev(k, tt, 0, ommd[(tt, 0)])
                    ommd[(tt, 1)] = emit_o(k, tt, 1, subd[(tt, 1)])
                    emit_oev(k, tt, 1, ommd[(tt, 1)])
                wfree["w1"] = hmmd[(3, 1)]
                wfree["w2"] = ommd[(3, 1)]

        with contextlib.ExitStack() as p:
            xqb = [p.enter_context(nc.sbuf_tensor(f"xqb{i}", [128, D], F32)) for i in range(2)]
            outb = [p.enter_context(nc.sbuf_tensor(f"outb{i}", [128, D], F32)) for i in range(2)]
            ps_c = [p.enter_context(nc.psum_tensor(f"ps_c{i}", [128, 512], F32)) for i in range(2)]

            blk = p.enter_context(nc.Block("fin"))

            @blk.vector
            def _(vector):
                for dout in range(DC):
                    op = vector.tensor_scalar(out=projacc[:, dout * NQ:(dout + 1) * NQ],
                                              in0=projacc[:, dout * NQ:(dout + 1) * NQ],
                                              scalar1=bias_all[:, dout:dout + 1], scalar2=None,
                                              op0=ALU.add)
                cv.inc(op)
            bias_done = cv.n

            ev_marks = []
            st_marks = []
            for tc in range(16):
                i = tc % 2

                @blk.sync
                def _(sync, tc=tc, i=i):
                    if tc >= 2:
                        sync.wait_ge(cv.h, ev_marks[tc - 2])
                    sync.dma_start(out=xqb[i][:],
                                   in_=x_q[tc * 128:(tc + 1) * 128, :]).then_inc(ld_xq[i].h, 16)
                ld_xq[i].n += 16
                xld = ld_xq[i].n

                @blk.tensor
                def _(tensor, tc=tc, bias_done=bias_done):
                    if tc == 0:
                        tensor.wait_ge(cv.h, bias_done)
                    if tc >= 1:
                        tensor.wait_ge(cv.h, ev_marks[tc - 1])
                    for half in range(2):
                        for q in range(4):
                            ci = half * 4 + q
                            op = tensor.transpose(ps_c[half][:, q * 128:(q + 1) * 128],
                                                  projacc[:, ci * NQ + tc * 128: ci * NQ + (tc + 1) * 128],
                                                  ident[:])
                    ct.inc(op)
                tdn = ct.n

                @blk.vector
                def _(vector, tc=tc, i=i, xld=xld, tdn=tdn):
                    vector.wait_ge(ct.h, tdn)
                    vector.wait_ge(ld_xq[i].h, xld)
                    if tc >= 2:
                        vector.wait_ge(st_o[i].h, st_marks[tc - 2])
                    for half in range(2):
                        op = vector.tensor_tensor(out=outb[i][:, half * 512:(half + 1) * 512],
                                                  in0=ps_c[half][:],
                                                  in1=xqb[i][:, half * 512:(half + 1) * 512],
                                                  op=ALU.add)
                    cv.inc(op)
                ev_marks.append(cv.n)

                @blk.gpsimd
                def _(gpsimd, tc=tc, i=i, evm=ev_marks[-1]):
                    gpsimd.wait_ge(cv.h, evm)
                    gpsimd.dma_start(out=out[tc * 128:(tc + 1) * 128, :],
                                     in_=outb[i][:]).then_inc(st_o[i].h, 16)
                st_o[i].n += 16
                st_marks.append(st_o[i].n)

            @blk.gpsimd
            def _(gpsimd):
                gpsimd.wait_ge(st_o[0].h, st_o[0].n)
                gpsimd.wait_ge(st_o[1].h, st_o[1].n)

    return nc


_NC_CACHE = {}


def kernel(L0_attn_x, L0_attn_qkv_w, L0_attn_qkv_b, L0_attn_proj_w, L0_attn_proj_b,
           L0_ffn_Wr, L0_ffn_br, L0_ffn_W1, L0_ffn_b1, L0_ffn_W2, L0_ffn_b2):
    x = np.ascontiguousarray(np.asarray(L0_attn_x, dtype=np.float32))
    qkvT = np.ascontiguousarray(np.asarray(L0_attn_qkv_w, dtype=np.float32).T).reshape(DC, 128, QKVD)
    projT = np.ascontiguousarray(np.asarray(L0_attn_proj_w, dtype=np.float32).T).reshape(DC, 128, D)
    wr_t = np.ascontiguousarray(np.asarray(L0_ffn_Wr, dtype=np.float32).reshape(DC, 128, E).transpose(1, 0, 2).reshape(128, DC * E))
    bqkv = np.ascontiguousarray(np.asarray(L0_attn_qkv_b, dtype=np.float32))
    bproj = np.ascontiguousarray(np.asarray(L0_attn_proj_b, dtype=np.float32))
    br = np.ascontiguousarray(np.asarray(L0_ffn_br, dtype=np.float32))
    W1 = np.ascontiguousarray(np.asarray(L0_ffn_W1, dtype=np.float32))
    W2 = np.ascontiguousarray(np.asarray(L0_ffn_W2, dtype=np.float32))
    b1 = np.ascontiguousarray(np.asarray(L0_ffn_b1, dtype=np.float32))
    b2 = np.ascontiguousarray(np.asarray(L0_ffn_b2, dtype=np.float32))

    if "nc" not in _NC_CACHE:
        _NC_CACHE["nc"] = build_nc()
    nc = _NC_CACHE["nc"]

    in_maps = []
    for c in range(8):
        b, qh = divmod(c, 2)
        xp = np.ascontiguousarray(
            np.concatenate([x[b, qh * NQ:(qh + 1) * NQ],
                            x[b, (1 - qh) * NQ:(2 - qh) * NQ]], axis=0))
        in_maps.append({
            "x_kv": xp,
            "x_q": np.ascontiguousarray(x[b, qh * NQ:(qh + 1) * NQ]),
            "rsel": np.array([[1.0 - qh, float(qh)]], dtype=np.float32),
            "wqkv": qkvT, "bqkv": bqkv, "wproj": projT, "bproj": bproj,
            "wr": wr_t, "br": br, "W1": W1, "W2": W2, "b1": b1, "b2": b2,
        })
    res = run_bass_kernel_spmd(nc, in_maps, core_ids=list(range(8)))
    outp = np.empty((B, NTOK, D), dtype=np.float32)
    for c in range(8):
        b, qh = divmod(c, 2)
        outp[b, qh * NQ:(qh + 1) * NQ] = res.results[c]["out"]
    return outp


# version 21
# speedup vs baseline: 1.1260x; 1.0035x over previous
import contextlib
import numpy as np

import concourse.bass as bass
import concourse.mybir as mybir
from concourse.bass_utils import run_bass_kernel_spmd

F32 = mybir.dt.float32
I32 = mybir.dt.int32
AF = mybir.ActivationFunctionType
ALU = mybir.AluOpType
AX = mybir.AxisListType

B, NTOK, NQ, D, H, HD, E = 4, 4096, 2048, 1024, 16, 64, 8
DC = D // 128
QKVD = 3 * D
SHIFT = 6.0


class Ctr:
    def __init__(self, ctx, nc, name):
        self.h = ctx.enter_context(nc.semaphore(name))
        self.n = 0

    def inc(self, op, k=1):
        op.then_inc(self.h, k)
        self.n += k
        return self.n


def build_nc(debug=None):
    nc = bass.Bass("TRN2", target_bir_lowering=False)

    x_kv = nc.dram_tensor("x_kv", [NTOK, D], F32, kind="ExternalInput").ap()
    x_q = nc.dram_tensor("x_q", [NQ, D], F32, kind="ExternalInput").ap()
    wqkv = nc.dram_tensor("wqkv", [DC, 128, QKVD], F32, kind="ExternalInput").ap()
    bqkv = nc.dram_tensor("bqkv", [QKVD], F32, kind="ExternalInput").ap()
    wproj = nc.dram_tensor("wproj", [DC, 128, D], F32, kind="ExternalInput").ap()
    bproj = nc.dram_tensor("bproj", [D], F32, kind="ExternalInput").ap()
    wr = nc.dram_tensor("wr", [128, DC * E], F32, kind="ExternalInput").ap()
    br = nc.dram_tensor("br", [E], F32, kind="ExternalInput").ap()
    W1 = nc.dram_tensor("W1", [E, D, D], F32, kind="ExternalInput").ap()
    W2 = nc.dram_tensor("W2", [E, D, D], F32, kind="ExternalInput").ap()
    b1 = nc.dram_tensor("b1", [E, D], F32, kind="ExternalInput").ap()
    b2 = nc.dram_tensor("b2", [E, D], F32, kind="ExternalInput").ap()
    out = nc.dram_tensor("out", [NQ, D], F32, kind="ExternalOutput").ap()

    sk = "ExternalOutput" if debug else "Internal"
    ski = "ExternalInput" if debug == "attnonly" else sk
    BFD = mybir.dt.bfloat16
    QH_s = nc.dram_tensor("QH_s", [DC, 128, NQ], BFD, kind=ski).ap()
    QL_s = nc.dram_tensor("QL_s", [DC, 128, NQ], BFD, kind=ski).ap()
    KH_s = nc.dram_tensor("KH_s", [DC, 128, NTOK], BFD, kind=ski).ap()
    KL_s = nc.dram_tensor("KL_s", [DC, 128, NTOK], BFD, kind=ski).ap()
    VH_s = nc.dram_tensor("VH_s", [NTOK, D], BFD, kind=ski).ap()
    VL_s = nc.dram_tensor("VL_s", [NTOK, D], BFD, kind=ski).ap()
    rsel = nc.dram_tensor("rsel", [1, 2], F32, kind="ExternalInput").ap()
    XNT_s = nc.dram_tensor("XNT_s", [DC, 128, NQ], F32, kind=ski).ap()
    XNH_s = nc.dram_tensor("XNH_s", [DC, 128, NQ], mybir.dt.bfloat16, kind=ski).ap()
    XNL_s = nc.dram_tensor("XNL_s", [DC, 128, NQ], mybir.dt.bfloat16, kind=ski).ap()
    MB_s = nc.dram_tensor("MB_s", [2, 128, E], F32, kind=sk).ap()
    AP_s = nc.dram_tensor("AP_s", [DC, 128, NQ], F32, kind=sk).ap()

    idx_info = {}
    g = contextlib.ExitStack()
    with g:
        ident = g.enter_context(nc.sbuf_tensor("ident", [128, 128], F32))
        iot = g.enter_context(nc.sbuf_tensor("iot", [128, 128], I32))
        ones = g.enter_context(nc.sbuf_tensor("ones", [128, 128], F32))
        m1b = g.enter_context(nc.sbuf_tensor("m1b", [128, E], F32))
        m2b = g.enter_context(nc.sbuf_tensor("m2b", [128, E], F32))
        qb_sb = g.enter_context(nc.sbuf_tensor("qb_sb", [128, 3 * DC], F32))
        bvb = g.enter_context(nc.sbuf_tensor("bvb", [128, D], F32))
        bias_all = g.enter_context(nc.sbuf_tensor("bias_all", [128, DC], F32))
        b1sel = g.enter_context(nc.sbuf_tensor("b1sel", [128, 2 * DC], F32))
        idxtab = g.enter_context(nc.sbuf_tensor("idxtab", [128, 2 * E], I32))

        ld = Ctr(g, nc, "ld")
        st = Ctr(g, nc, "st")
        ld_x = [Ctr(g, nc, f"ld_x{i}") for i in range(2)]
        ld_hd = [Ctr(g, nc, f"ld_hd{i}") for i in range(2)]
        ld_pw = [Ctr(g, nc, f"ld_pw{i}") for i in range(2)]
        ld_bt = Ctr(g, nc, "ld_bt")
        ld_wt = [Ctr(g, nc, f"ld_wt{i}") for i in range(4)]
        ld_xn = [Ctr(g, nc, f"ld_xn{i}") for i in range(2)]
        ld_xq = [Ctr(g, nc, f"ld_xq{i}") for i in range(2)]
        st_o = [Ctr(g, nc, f"st_o{i}") for i in range(2)]
        ld_mv = Ctr(g, nc, "ld_mv")
        cv = Ctr(g, nc, "cv")
        cs = Ctr(g, nc, "cs")
        ct = Ctr(g, nc, "ct")
        cg = Ctr(g, nc, "cg")

        def phase_p0():
            with contextlib.ExitStack() as p:
                BFP = mybir.dt.bfloat16
                wq_st = p.enter_context(nc.sbuf_tensor("wq_st", [128, QKVD], F32))
                wq_hi = p.enter_context(nc.sbuf_tensor("wq_hi", [128, DC * QKVD], BFP))
                wq_lo = p.enter_context(nc.sbuf_tensor("wq_lo", [128, DC * QKVD], BFP))
                xnT_hi = p.enter_context(nc.sbuf_tensor("xnT_hi", [128, DC * 512], BFP))
                xnT_lo = p.enter_context(nc.sbuf_tensor("xnT_lo", [128, DC * 512], BFP))
                wr_hi = p.enter_context(nc.sbuf_tensor("wr_hi", [128, DC * E], BFP))
                wr_lo = p.enter_context(nc.sbuf_tensor("wr_lo", [128, DC * E], BFP))
                xbuf = [p.enter_context(nc.sbuf_tensor(f"xbuf{i}", [128, D], F32)) for i in range(2)]
                x2buf = p.enter_context(nc.sbuf_tensor("x2buf", [128, D], F32))
                xnbuf = [p.enter_context(nc.sbuf_tensor(f"xnbuf{i}", [128, D], F32)) for i in range(2)]
                ev = [p.enter_context(nc.sbuf_tensor(f"ev{i}", [128, 512], F32)) for i in range(2)]
                ehi = [p.enter_context(nc.sbuf_tensor(f"ehi{i}", [128, 512], BFP)) for i in range(2)]
                elo = [p.enter_context(nc.sbuf_tensor(f"elo{i}", [128, 512], BFP)) for i in range(2)]
                vout = [p.enter_context(nc.sbuf_tensor(f"vout{i}", [128, D], F32)) for i in range(2)]
                vhi = [p.enter_context(nc.sbuf_tensor(f"vhi{i}", [128, D], BFP)) for i in range(2)]
                vlo = [p.enter_context(nc.sbuf_tensor(f"vlo{i}", [128, D], BFP)) for i in range(2)]
                stat = p.enter_context(nc.sbuf_tensor("stat", [128, 16], F32))
                wr_sb = p.enter_context(nc.sbuf_tensor("wr_sb", [128, DC * E], F32))
                sc8 = p.enter_context(nc.sbuf_tensor("sc8", [1, E], F32))
                sab = p.enter_context(nc.sbuf_tensor("sab", [1, 2 * E], F32))
                iot8 = p.enter_context(nc.sbuf_tensor("iot8", [128, E], I32))
                iot8f = p.enter_context(nc.sbuf_tensor("iot8f", [128, E], F32))
                ew = p.enter_context(nc.sbuf_tensor("ew", [1, E], F32))
                pr2 = p.enter_context(nc.sbuf_tensor("pr2", [1, 2 * E], F32))
                es2 = p.enter_context(nc.sbuf_tensor("es2", [1, 2], F32))
                rs_sb = p.enter_context(nc.sbuf_tensor("rs_sb", [1, 2], F32))
                mx = p.enter_context(nc.sbuf_tensor("mx", [1, 2], F32))
                mk12 = p.enter_context(nc.sbuf_tensor("mk12", [1, 2 * E], F32))
                bvrow = p.enter_context(nc.sbuf_tensor("bvrow", [1, D], F32))
                ps_qk = [p.enter_context(nc.psum_tensor(f"ps_qk{i}", [128, 512], F32)) for i in range(2)]
                ps_v = [p.enter_context(nc.psum_tensor(f"ps_v{i}", [128, 512], F32)) for i in range(2)]
                ps_tr = [p.enter_context(nc.psum_tensor(f"ps_tr{i}", [128, 512], F32)) for i in range(2)]
                ps_bc = p.enter_context(nc.psum_tensor("ps_bc", [128, 512], F32))

                blk = p.enter_context(nc.Block("p0"))

                @blk.sync
                def _(sync):
                    sync.dma_start(out=wq_st[:], in_=wqkv[0]).then_inc(ld.h, 16)
                    sync.dma_start(out=wr_sb[:], in_=wr).then_inc(ld.h, 16)
                    sync.dma_start(out=qb_sb[:], in_=bqkv.rearrange("(j p) -> p j", p=128),
                                   allow_slow_non_contiguous=True).then_inc(ld.h, 16)
                    sync.dma_start(out=bvrow[:],
                                   in_=bqkv[2 * D:3 * D].rearrange("(o d) -> o d", o=1)).then_inc(ld.h, 16)
                    sync.dma_start(out=sc8[:], in_=br.rearrange("(o e) -> o e", o=1)).then_inc(ld.h, 16)
                    sync.dma_start(out=rs_sb[:], in_=rsel).then_inc(ld.h, 16)
                ld.n += 16 * 6
                pre_ld = ld.n

                wsplit_marks = []
                for ci in range(DC):
                    if ci > 0:
                        @blk.sync
                        def _(sync, ci=ci):
                            sync.wait_ge(cv.h, wsplit_marks[-1])
                            sync.dma_start(out=wq_st[:], in_=wqkv[ci]).then_inc(ld.h, 16)
                        ld.n += 16
                    wld2 = ld.n

                    @blk.vector
                    def _(vector, ci=ci, wld2=wld2):
                        vector.wait_ge(ld.h, wld2)
                        vector.tensor_copy(wq_hi[:, ci * QKVD:(ci + 1) * QKVD], wq_st[:])
                        vector.drain()
                        cv.inc(vector.tensor_sub(wq_lo[:, ci * QKVD:(ci + 1) * QKVD],
                                                 wq_st[:], wq_hi[:, ci * QKVD:(ci + 1) * QKVD]))
                    wsplit_marks.append(cv.n)

                @blk.vector
                def _(vector):
                    vector.wait_ge(ld.h, pre_ld)
                    vector.tensor_copy(wr_hi[:], wr_sb[:])
                    vector.drain()
                    cv.inc(vector.tensor_sub(wr_lo[:], wr_sb[:], wr_hi[:]))
                wsplit_done = cv.n

                @blk.gpsimd
                def _(gpsimd):
                    gpsimd.iota(iot[:], [[1, 128]], base=0, channel_multiplier=-1)
                    cg.inc(gpsimd.memset(ones[:], 1.0))
                g_pre = cg.n

                @blk.vector
                def _(vector):
                    vector.wait_ge(cg.h, g_pre)
                    cv.inc(vector.tensor_scalar(out=ident[:], in0=iot[:], scalar1=0, scalar2=None,
                                                op0=ALU.is_equal))
                ident_ok = cv.n

                @blk.tensor
                def _(tensor):
                    tensor.wait_ge(ld.h, pre_ld)
                    tensor.wait_ge(cv.h, ident_ok)
                    tensor.matmul(ps_v[0][:], ones[0:1, :], bvrow[0:1, 0:512], start=True, stop=True)
                    ct.inc(tensor.matmul(ps_v[1][:], ones[0:1, :], bvrow[0:1, 512:1024],
                                         start=True, stop=True))
                bv_mm = ct.n

                @blk.vector
                def _(vector):
                    vector.wait_ge(ct.h, bv_mm)
                    vector.tensor_copy(bvb[:, 0:512], ps_v[0][:])
                    cv.inc(vector.tensor_copy(bvb[:, 512:1024], ps_v[1][:]))

                state = {"n_ln": 0, "xbuf_free": [None, None],
                         "tr_evac": [(cv.h, wsplit_done)],
                         "qk_evac": [], "v_evacs": [], "qk_sub": [], "v_sub": []}

                def ln_chunk(src_ap, j):
                    i = state["n_ln"] % 2
                    state["n_ln"] += 1

                    @blk.sync
                    def _(sync, i=i, src_ap=src_ap):
                        if state["xbuf_free"][i] is not None:
                            sem, val = state["xbuf_free"][i]
                            sync.wait_ge(sem, val)
                        sync.dma_start(out=xbuf[i][:], in_=src_ap).then_inc(ld_x[i].h, 16)
                    ld_x[i].n += 16
                    my_ld = ld_x[i].n

                    @blk.scalar
                    def _(scalar, i=i, my_ld=my_ld):
                        scalar.wait_ge(ld_x[i].h, my_ld)
                        cs.inc(scalar.activation(out=x2buf[:], in_=xbuf[i][:], func=AF.Square,
                                                 accum_out=stat[:, 4:5]))
                    sq_done = cs.n

                    @blk.vector
                    def _(vector, i=i, my_ld=my_ld, sq_done=sq_done):
                        vector.wait_ge(ld_x[i].h, my_ld)
                        vector.reduce_sum(stat[:, 0:1], xbuf[i][:], axis=AX.X)
                        vector.wait_ge(cs.h, sq_done)
                        vector.drain()
                        vector.tensor_scalar(out=stat[:, 1:2], in0=stat[:, 0:1], scalar1=1.0 / D,
                                             scalar2=None, op0=ALU.mult)
                        vector.tensor_scalar(out=stat[:, 5:6], in0=stat[:, 4:5], scalar1=1.0 / D,
                                             scalar2=None, op0=ALU.mult)
                        vector.drain()
                        vector.scalar_tensor_tensor(out=stat[:, 2:3], in0=stat[:, 1:2],
                                                    scalar=stat[:, 1:2], in1=stat[:, 5:6],
                                                    op0=ALU.mult, op1=ALU.subtract)
                        vector.drain()
                        cv.inc(vector.tensor_scalar(out=stat[:, 2:3], in0=stat[:, 2:3], scalar1=-1.0,
                                                    scalar2=1e-5, op0=ALU.mult, op1=ALU.add))
                    var_done = cv.n

                    @blk.scalar
                    def _(scalar, var_done=var_done):
                        scalar.wait_ge(cv.h, var_done)
                        cs.inc(scalar.activation(out=stat[:, 3:4], in_=stat[:, 2:3], func=AF.Sqrt))
                    sqrt_done = cs.n

                    @blk.vector
                    def _(vector, sqrt_done=sqrt_done):
                        vector.wait_ge(cs.h, sqrt_done)
                        vector.reciprocal(out=stat[:, 6:7], in_=stat[:, 3:4])
                        vector.drain()
                        cv.inc(vector.scalar_tensor_tensor(out=stat[:, 7:8], in0=stat[:, 1:2],
                                                           scalar=-1.0, in1=stat[:, 6:7],
                                                           op0=ALU.mult, op1=ALU.mult))
                    rb_done = cv.n

                    @blk.scalar
                    def _(scalar, i=i, rb_done=rb_done, te2=(state["tr_evac"][-2]
                                                            if len(state["tr_evac"]) >= 2 else None)):
                        scalar.wait_ge(cv.h, rb_done)
                        if te2 is not None:
                            scalar.wait_ge(te2[0], te2[1])
                        cs.inc(scalar.activation(out=xnbuf[i][:], in_=xbuf[i][:], func=AF.Identity,
                                                 scale=stat[:, 6:7], bias=stat[:, 7:8]))
                    ln_done = cs.n
                    state["xbuf_free"][i] = (cs.h, ln_done)

                    @blk.tensor
                    def _(tensor, i=i, ln_done=ln_done, te1=state["tr_evac"][-1]):
                        tensor.wait_ge(cs.h, ln_done)
                        tensor.wait_ge(te1[0], te1[1])
                        for half in range(2):
                            for q in range(4):
                                ci = half * 4 + q
                                op = tensor.transpose(ps_tr[half][:, q * 128:(q + 1) * 128],
                                                      xnbuf[i][:, ci * 128:(ci + 1) * 128], ident[:])
                        ct.inc(op)
                    tp_done = ct.n

                    @blk.scalar
                    def _(scalar, j=j, tp_done=tp_done):
                        scalar.wait_ge(ct.h, tp_done)
                        for half in range(2):
                            dsth = xnT_hi[:].rearrange("p (c t) -> p c t", c=DC)[:, half * 4:half * 4 + 4,
                                                                                j * 128:(j + 1) * 128]
                            op = scalar.activation(out=dsth,
                                                   in_=ps_tr[half][:].rearrange("p (q t) -> p q t", q=4),
                                                   func=AF.Identity)
                        cs.inc(op)
                    hi_m = cs.n

                    @blk.vector
                    def _(vector, j=j, hi_m=hi_m, tp_done=tp_done):
                        vector.wait_ge(ct.h, tp_done)
                        vector.wait_ge(cs.h, hi_m)
                        for half in range(2):
                            dsth = xnT_hi[:].rearrange("p (c t) -> p c t", c=DC)[:, half * 4:half * 4 + 4,
                                                                                j * 128:(j + 1) * 128]
                            dstl = xnT_lo[:].rearrange("p (c t) -> p c t", c=DC)[:, half * 4:half * 4 + 4,
                                                                                j * 128:(j + 1) * 128]
                            op = vector.tensor_sub(dstl,
                                                   ps_tr[half][:].rearrange("p (q t) -> p q t", q=4),
                                                   dsth)
                        cv.inc(op)
                    state["tr_evac"].append((cv.h, cv.n))

                def qk_matmuls(is_q, base_tok, xnT_ready):
                    woff = 0 if is_q else D
                    boff = 0 if is_q else DC
                    dsth = QH_s if is_q else KH_s
                    dstl = QL_s if is_q else KL_s
                    for dout in range(DC):
                        bank = dout % 2

                        @blk.tensor
                        def _(tensor, dout=dout, bank=bank, xnT_ready=xnT_ready, woff=woff):
                            tensor.wait_ge(xnT_ready[0], xnT_ready[1])
                            if len(state["qk_evac"]) >= 2:
                                tensor.wait_ge(cv.h, state["qk_evac"][-2])
                            base = woff + dout * 128
                            for ci in range(DC):
                                wsl = slice(ci * QKVD + base, ci * QKVD + base + 128)
                                xsl = slice(ci * 512, (ci + 1) * 512)
                                tensor.matmul(ps_qk[bank][:], wq_hi[:, wsl], xnT_hi[:, xsl],
                                              start=(ci == 0), stop=False)
                                tensor.matmul(ps_qk[bank][:], wq_hi[:, wsl], xnT_lo[:, xsl],
                                              start=False, stop=False)
                                op = tensor.matmul(ps_qk[bank][:], wq_lo[:, wsl], xnT_hi[:, xsl],
                                                   start=False, stop=(ci == DC - 1))
                            ct.inc(op)
                        mm_done = ct.n

                        @blk.vector
                        def _(vector, bank=bank, mm_done=mm_done, boff=boff, dout=dout,
                              psub=(state["qk_sub"][-2] if len(state["qk_sub"]) >= 2 else None)):
                            vector.wait_ge(ct.h, mm_done)
                            vector.wait_ge(st.h, st.n)
                            if psub is not None:
                                vector.wait_ge(cg.h, psub)
                            vector.tensor_scalar(out=ev[bank][:], in0=ps_qk[bank][:],
                                                 scalar1=qb_sb[:, boff + dout:boff + dout + 1],
                                                 scalar2=None, op0=ALU.add)
                            vector.drain()
                            cv.inc(vector.tensor_copy(ehi[bank][:], ev[bank][:]))
                        state["qk_evac"].append(cv.n)
                        evd = cv.n

                        @blk.gpsimd
                        def _(gpsimd, bank=bank, evd=evd):
                            gpsimd.wait_ge(cv.h, evd)
                            cg.inc(gpsimd.tensor_sub(elo[bank][:], ev[bank][:], ehi[bank][:]))
                        state["qk_sub"].append(cg.n)
                        subd = cg.n

                        @blk.scalar
                        def _(scalar, dout=dout, bank=bank, subd=subd, base_tok=base_tok,
                              dsth=dsth, dstl=dstl):
                            scalar.wait_ge(cg.h, subd)
                            scalar.dma_start(out=dsth[dout, :, base_tok:base_tok + 512],
                                             in_=ehi[bank][:]).then_inc(st.h, 16)
                            scalar.dma_start(out=dstl[dout, :, base_tok:base_tok + 512],
                                             in_=elo[bank][:]).then_inc(st.h, 16)
                        st.n += 32

                def v_matmuls(base_tok, xnT_ready):
                    for t in range(4):
                        vb = t % 2

                        @blk.tensor
                        def _(tensor, t=t, xnT_ready=xnT_ready):
                            tensor.wait_ge(xnT_ready[0], xnT_ready[1])
                            if state["v_evacs"]:
                                tensor.wait_ge(cv.h, state["v_evacs"][-1])
                            for half in range(2):
                                for ci in range(DC):
                                    xsl = slice(ci * 512 + t * 128, ci * 512 + (t + 1) * 128)
                                    wsl = slice(ci * QKVD + 2 * D + half * 512,
                                                ci * QKVD + 2 * D + (half + 1) * 512)
                                    tensor.matmul(ps_v[half][:], xnT_hi[:, xsl], wq_hi[:, wsl],
                                                  start=(ci == 0), stop=False)
                                    tensor.matmul(ps_v[half][:], xnT_hi[:, xsl], wq_lo[:, wsl],
                                                  start=False, stop=False)
                                    op = tensor.matmul(ps_v[half][:], xnT_lo[:, xsl], wq_hi[:, wsl],
                                                       start=False, stop=(ci == DC - 1))
                            ct.inc(op)
                        vmm = ct.n

                        @blk.vector
                        def _(vector, vb=vb, vmm=vmm,
                              psub=(state["v_sub"][-2] if len(state["v_sub"]) >= 2 else None)):
                            vector.wait_ge(ct.h, vmm)
                            vector.wait_ge(st.h, st.n)
                            if psub is not None:
                                vector.wait_ge(cg.h, psub)
                            vector.tensor_tensor(out=vout[vb][:, 0:512], in0=ps_v[0][:],
                                                 in1=bvb[:, 0:512], op=ALU.add)
                            vector.tensor_tensor(out=vout[vb][:, 512:1024], in0=ps_v[1][:],
                                                 in1=bvb[:, 512:1024], op=ALU.add)
                            vector.drain()
                            cv.inc(vector.tensor_copy(vhi[vb][:], vout[vb][:]))
                        state["v_evacs"].append(cv.n)
                        evd = cv.n

                        @blk.gpsimd
                        def _(gpsimd, vb=vb, evd=evd):
                            gpsimd.wait_ge(cv.h, evd)
                            cg.inc(gpsimd.tensor_sub(vlo[vb][:], vout[vb][:], vhi[vb][:]))
                        state["v_sub"].append(cg.n)
                        vsub = cg.n

                        @blk.sync
                        def _(sync, t=t, vb=vb, vsub=vsub, base_tok=base_tok):
                            sync.wait_ge(cg.h, vsub)
                            sync.dma_start(
                                out=VH_s[base_tok + t * 128: base_tok + (t + 1) * 128, :],
                                in_=vhi[vb][:]).then_inc(st.h, 16)
                            sync.dma_start(
                                out=VL_s[base_tok + t * 128: base_tok + (t + 1) * 128, :],
                                in_=vlo[vb][:]).then_inc(st.h, 16)
                        st.n += 32

                def router_mm(xnT_ready, part):
                    off = part * E

                    @blk.tensor
                    def _(tensor, xnT_ready=xnT_ready, off=off):
                        tensor.wait_ge(xnT_ready[0], xnT_ready[1])
                        for ci in range(DC):
                            xc = slice(ci * 512, ci * 512 + 1)
                            wc = slice(ci * E, (ci + 1) * E)
                            tensor.matmul(ps_bc[0:1, off:off + E], xnT_hi[:, xc], wr_hi[:, wc],
                                          start=(ci == 0), stop=False)
                            tensor.matmul(ps_bc[0:1, off:off + E], xnT_hi[:, xc], wr_lo[:, wc],
                                          start=False, stop=False)
                            op = tensor.matmul(ps_bc[0:1, off:off + E], xnT_lo[:, xc],
                                               wr_hi[:, wc], start=False, stop=(ci == DC - 1))
                        ct.inc(op)
                    rt = ct.n

                    @blk.vector
                    def _(vector, rt=rt, off=off):
                        vector.wait_ge(ct.h, rt)
                        cv.inc(vector.tensor_copy(sab[0:1, off:off + E], ps_bc[0:1, off:off + E]))
                    return cv.n

                def router_combine(r2_done):
                    @blk.vector
                    def _(vector, r2_done=r2_done):
                        vector.wait_ge(cv.h, r2_done)
                        vector.wait_ge(ld.h, pre_ld)
                        vector.drain()
                        vector.scalar_tensor_tensor(out=sc8[:], in0=sab[0:1, 0:E],
                                                    scalar=rs_sb[0:1, 0:1], in1=sc8[:],
                                                    op0=ALU.mult, op1=ALU.add)
                        vector.drain()
                        vector.scalar_tensor_tensor(out=sc8[:], in0=sab[0:1, E:2 * E],
                                                    scalar=rs_sb[0:1, 1:2], in1=sc8[:],
                                                    op0=ALU.mult, op1=ALU.add)
                        vector.drain()
                        vector.reduce_max(mx[0:1, 0:1], sc8[:], axis=AX.X)
                        vector.drain()
                        vector.tensor_scalar(out=mk12[0:1, 0:E], in0=sc8[:], scalar1=mx[0:1, 0:1],
                                             scalar2=None, op0=ALU.is_ge)
                        vector.drain()
                        vector.scalar_tensor_tensor(out=sc8[:], in0=mk12[0:1, 0:E], scalar=-1e30,
                                                    in1=sc8[:], op0=ALU.mult, op1=ALU.add)
                        vector.drain()
                        vector.reduce_max(mx[0:1, 1:2], sc8[:], axis=AX.X)
                        vector.drain()
                        cv.inc(vector.tensor_scalar(out=mk12[0:1, E:2 * E], in0=sc8[:],
                                                    scalar1=mx[0:1, 1:2], scalar2=None, op0=ALU.is_ge))
                    msk = cv.n

                    @blk.tensor
                    def _(tensor, msk=msk):
                        tensor.wait_ge(cv.h, msk)
                        tensor.matmul(ps_bc[:, 256:256 + E], ones[0:1, :], mk12[0:1, 0:E],
                                      start=True, stop=True)
                        ct.inc(tensor.matmul(ps_bc[:, 384:384 + E], ones[0:1, :], mk12[0:1, E:2 * E],
                                             start=True, stop=True))
                    bca = ct.n

                    @blk.vector
                    def _(vector, bca=bca):
                        vector.wait_ge(ct.h, bca)
                        vector.tensor_copy(m1b[:], ps_bc[:, 256:256 + E])
                        cv.inc(vector.tensor_copy(m2b[:], ps_bc[:, 384:384 + E]))
                    mdone = cv.n

                    @blk.gpsimd
                    def _(gpsimd):
                        cg.inc(gpsimd.iota(iot8[:], [[128, E]], base=0, channel_multiplier=1))
                    iota8_done = cg.n

                    @blk.vector
                    def _(vector, iota8_done=iota8_done):
                        vector.wait_ge(cg.h, iota8_done)
                        vector.drain()
                        vector.tensor_copy(iot8f[:], iot8[:])
                        vector.tensor_scalar(out=ew[:], in0=iot[0:1, 0:E], scalar1=float(D),
                                             scalar2=None, op0=ALU.mult)
                        vector.drain()
                        vector.tensor_tensor(out=pr2[0:1, 0:E], in0=mk12[0:1, 0:E],
                                             in1=ew[:], op=ALU.mult)
                        vector.tensor_tensor(out=pr2[0:1, E:2 * E], in0=mk12[0:1, E:2 * E],
                                             in1=ew[:], op=ALU.mult)
                        vector.drain()
                        vector.reduce_sum(es2[0:1, 0:1], pr2[0:1, 0:E], axis=AX.X)
                        cv.inc(vector.reduce_sum(es2[0:1, 1:2], pr2[0:1, E:2 * E], axis=AX.X))
                    es_done = cv.n

                    @blk.tensor
                    def _(tensor, es_done=es_done):
                        tensor.wait_ge(cv.h, es_done)
                        tensor.matmul(ps_bc[:, 500:501], ones[0:1, :], es2[0:1, 0:1],
                                      start=True, stop=True)
                        ct.inc(tensor.matmul(ps_bc[:, 501:502], ones[0:1, :], es2[0:1, 1:2],
                                             start=True, stop=True))
                    eb_done = ct.n

                    @blk.vector
                    def _(vector, eb_done=eb_done):
                        vector.wait_ge(ct.h, eb_done)
                        for k2 in range(2):
                            op = vector.tensor_scalar(out=idxtab[:, k2 * E:(k2 + 1) * E],
                                                      in0=iot8f[:],
                                                      scalar1=ps_bc[:, 500 + k2:501 + k2],
                                                      scalar2=None, op0=ALU.add)
                        cv.inc(op)
                    idx_info["done"] = cv.n

                    if debug:
                        @blk.gpsimd
                        def _(gpsimd, mdone=mdone):
                            gpsimd.wait_ge(cv.h, mdone)
                            gpsimd.dma_start(out=MB_s[0], in_=m1b[:]).then_inc(st.h, 16)
                            gpsimd.dma_start(out=MB_s[1], in_=m2b[:]).then_inc(st.h, 16)
                        st.n += 32

                r_parts = {}
                for sc in range(8):
                    base_tok = sc * 512
                    for j in range(4):
                        ln_chunk(x_kv[base_tok + j * 128: base_tok + (j + 1) * 128, :], j)
                    xnT_ready = state["tr_evac"][-1]
                    qk_matmuls(False, base_tok, xnT_ready)
                    v_matmuls(base_tok, xnT_ready)
                    if sc < 4:
                        @blk.sync
                        def _(sync, base_tok=base_tok, xnT_ready=xnT_ready):
                            sync.wait_ge(xnT_ready[0], xnT_ready[1])
                            sync.dma_start(
                                out=XNH_s[:, :, base_tok:base_tok + 512].rearrange("c p t -> p c t"),
                                in_=xnT_hi[:].rearrange("p (c t) -> p c t", c=DC)).then_inc(st.h, 16)
                            sync.dma_start(
                                out=XNL_s[:, :, base_tok:base_tok + 512].rearrange("c p t -> p c t"),
                                in_=xnT_lo[:].rearrange("p (c t) -> p c t", c=DC)).then_inc(st.h, 16)
                        st.n += 32
                        qk_matmuls(True, base_tok, xnT_ready)
                    if sc == 0:
                        r_parts[0] = router_mm(xnT_ready, 0)
                    if sc == 4:
                        r_parts[1] = router_mm(xnT_ready, 1)
                        router_combine(r_parts[1])
                    tdone = ct.n

                    @blk.gpsimd
                    def _(gpsimd, tdone=tdone):
                        gpsimd.wait_ge(ct.h, tdone)
                        gpsimd.wait_ge(st.h, st.n)
                        cg.inc(gpsimd.engine_nop())
                    state["tr_evac"].append((cg.h, cg.n))

                @blk.gpsimd
                def _(gpsimd):
                    gpsimd.wait_ge(st.h, st.n)


        if debug == "attnonly":
            with contextlib.ExitStack() as p:
                blk = p.enter_context(nc.Block("pre"))

                @blk.gpsimd
                def _(gpsimd):
                    cg.inc(gpsimd.memset(ones[:], 1.0))
        else:
            phase_p0()

        if debug == "p0":
            return nc

        projacc = g.enter_context(nc.sbuf_tensor("projacc", [128, DC * NQ], F32))

        with contextlib.ExitStack() as p:
            BF = mybir.dt.bfloat16
            kst = [p.enter_context(nc.sbuf_tensor(f"kst{i}", [128, NTOK], BF)) for i in range(2)]
            qab = [p.enter_context(nc.sbuf_tensor(f"qab{i}", [128, NQ], BF)) for i in range(2)]
            qcd = [p.enter_context(nc.sbuf_tensor(f"qcd{i}", [128, NQ], BF)) for i in range(2)]
            vA = [p.enter_context(nc.sbuf_tensor(f"vA{i}", [128, 32 * (HD + 1)], BF)) for i in range(2)]
            vB = [p.enter_context(nc.sbuf_tensor(f"vB{i}", [128, 32 * HD], BF)) for i in range(2)]
            p32 = [p.enter_context(nc.sbuf_tensor(f"p32_{i}", [128, 1024], F32)) for i in range(2)]
            phi = [p.enter_context(nc.sbuf_tensor(f"phi{i}", [128, 1024], BF)) for i in range(4)]
            plo = [p.enter_context(nc.sbuf_tensor(f"plo{i}", [128, 1024], BF)) for i in range(4)]
            rzb = p.enter_context(nc.sbuf_tensor("rzb", [128, 1024], F32))
            t1 = p.enter_context(nc.sbuf_tensor("t1", [64, 1024], F32))
            aout = p.enter_context(nc.sbuf_tensor("aout", [64, 1024], F32))
            apair = p.enter_context(nc.sbuf_tensor("apair", [128, NQ], F32))
            aph = p.enter_context(nc.sbuf_tensor("aph", [128, NQ], BF))
            apl = p.enter_context(nc.sbuf_tensor("apl", [128, NQ], BF))
            pw_sb = [p.enter_context(nc.sbuf_tensor(f"pw_sb{i}", [128, D], F32)) for i in range(2)]
            pwh = [p.enter_context(nc.sbuf_tensor("pwh0", [128, D], BF))] * 2
            pwl = [p.enter_context(nc.sbuf_tensor("pwl0", [128, D], BF))] * 2
            nshift = p.enter_context(nc.sbuf_tensor("nshift", [128, 1], F32))
            ps_s = [p.enter_context(nc.psum_tensor(f"ps_s{i}", [128, 1024], F32)) for i in range(2)]
            ps_ab = [p.enter_context(nc.psum_tensor(f"ps_ab{i}", [65, 512], F32)) for i in range(2)]
            ps_a2 = [p.enter_context(nc.psum_tensor(f"ps_a2{i}", [128, 512], F32)) for i in range(2)]

            blk = p.enter_context(nc.Block("attn"))

            @blk.gpsimd
            def _(gpsimd):
                for i in range(2):
                    gpsimd.memset(
                        vA[i][:].rearrange("p (kc w) -> p kc w", w=HD + 1)[:, :, HD:HD + 1], 1.0)
                cg.inc(gpsimd.memset(nshift[:], -SHIFT))
            vones = cg.n

            head_ld = {}
            pw_split_by_pair = {}
            conv_done = {}
            head_free = {}
            proj_ev_by_pair = {}
            A = {"o_free": None, "proj_done": None, "prev_exp_last": None,
                 "prev_vhl_last": None, "prev_pv_last": None,
                 "apair_free": None, "apair_split": None, "aout_free": None,
                 "ps_s_free": None, "staging_free": None}

            def load_head(h):
                i = h % 2
                b64 = 64 * (h % 2)

                @blk.sync
                def _(sync, h=h, i=i, b64=b64):
                    if h >= 2:
                        sem, val = head_free[h - 2]
                        sync.wait_ge(sem, val)
                    if h == 0:
                        sync.wait_ge(cg.h, vones)
                    sync.dma_start(out=kst[i][0:64, :],
                                   in_=KH_s[h // 2, b64:b64 + 64, :]).then_inc(ld_hd[i].h, 16)
                    sync.dma_start(out=kst[i][64:128, :],
                                   in_=KL_s[h // 2, b64:b64 + 64, :]).then_inc(ld_hd[i].h, 16)
                    for half in range(2):
                        sync.dma_start(out=qab[i][64 * half:64 * half + 64, :],
                                       in_=QH_s[h // 2, b64:b64 + 64, :]).then_inc(ld_hd[i].h, 16)
                        sync.dma_start(out=qcd[i][64 * half:64 * half + 64, :],
                                       in_=QL_s[h // 2, b64:b64 + 64, :]).then_inc(ld_hd[i].h, 16)
                    for kg in range(4):
                        sync.dma_start(
                            out=vA[i][:].rearrange("p (kc w) -> p kc w", w=HD + 1)[:, kg * 8:(kg + 1) * 8, 0:HD],
                            in_=VH_s.rearrange("(kc p) c -> p kc c", p=128)[:, kg * 8:(kg + 1) * 8,
                                                                           h * HD:(h + 1) * HD]).then_inc(ld_hd[i].h, 16)
                        sync.dma_start(
                            out=vB[i][:].rearrange("p (kc w) -> p kc w", w=HD)[:, kg * 8:(kg + 1) * 8, :],
                            in_=VL_s.rearrange("(kc p) c -> p kc c", p=128)[:, kg * 8:(kg + 1) * 8,
                                                                           h * HD:(h + 1) * HD]).then_inc(ld_hd[i].h, 16)
                    if h % 2 == 0:
                        q = h // 2
                        if q >= 2:
                            sync.wait_ge(cv.h, pw_split_by_pair[q - 2])
                        sync.dma_start(out=pw_sb[q % 2][:], in_=wproj[q]).then_inc(ld_pw[q % 2].h, 16)
                ld_hd[i].n += 14 * 16
                if h % 2 == 0:
                    ld_pw[(h // 2) % 2].n += 16
                head_ld[h] = ld_hd[i].n
                conv_done[h] = None


            load_head(0)

            n_heads = 2 if debug == "attn1" else H

            def emit_pva(h, i, s_, phi_done, pva_done):
                qh, kc = divmod(s_, 32)
                pbk = s_ % 4

                @blk.tensor
                def _(tensor, i=i, s_=s_, kc=kc, pbk=pbk):
                    pd = phi_done[s_]
                    tensor.wait_ge(pd[0], pd[1])
                    if kc == 0 and A["o_free"] is not None:
                        tensor.wait_ge(cv.h, A["o_free"])
                    for qq in range(2):
                        tensor.matmul(ps_ab[qq][:],
                                      vA[i][:, kc * (HD + 1):(kc + 1) * (HD + 1)],
                                      phi[pbk][:, qq * 512:(qq + 1) * 512],
                                      start=(kc == 0), stop=False)
                    for qq in range(2):
                        op = tensor.matmul(ps_a2[qq][0:64, :],
                                           vB[i][:, kc * HD:(kc + 1) * HD],
                                           phi[pbk][:, qq * 512:(qq + 1) * 512],
                                           start=(kc == 0), stop=(kc == 31))
                    ct.inc(op)
                pva_done.append(ct.n)

            def emit_pvb(h, i, s_, plo_done, pvb_done):
                qh, kc = divmod(s_, 32)
                pbk = s_ % 4

                @blk.tensor
                def _(tensor, i=i, s_=s_, kc=kc, pbk=pbk):
                    pl = plo_done[s_]
                    tensor.wait_ge(pl[0], pl[1])
                    for qq in range(2):
                        op = tensor.matmul(ps_ab[qq][:],
                                           vA[i][:, kc * (HD + 1):(kc + 1) * (HD + 1)],
                                           plo[pbk][:, qq * 512:(qq + 1) * 512],
                                           start=False, stop=(kc == 31))
                    ct.inc(op)
                pvb_done.append(ct.n)

            def finalize_a(h, qh, pvb_done):
                pvl = pvb_done[qh * 32 + 31]

                @blk.vector
                def _(vector, pvl=pvl):
                    vector.wait_ge(ct.h, pvl)
                    for qq in range(2):
                        vector.reciprocal(out=rzb[64:65, qq * 512:(qq + 1) * 512],
                                          in_=ps_ab[qq][64:65, :])
                    for qq in range(2):
                        op = vector.tensor_copy(t1[:, qq * 512:(qq + 1) * 512],
                                                ps_ab[qq][0:64, :])
                    cv.inc(op)

                @blk.vector
                def _(vector):
                    vector.drain()
                    for qq in range(2):
                        op = vector.tensor_tensor(out=t1[:, qq * 512:(qq + 1) * 512],
                                                  in0=ps_a2[qq][0:64, :],
                                                  in1=t1[:, qq * 512:(qq + 1) * 512], op=ALU.add)
                    cv.inc(op)
                A["o_free"] = cv.n
                return cv.n

            def finalize_b(h, qh, t1_done):
                qoff = qh * 1024

                @blk.tensor
                def _(tensor, t1_done=t1_done):
                    tensor.wait_ge(cv.h, t1_done)
                    for qq in range(2):
                        op = tensor.matmul(ps_a2[qq][64:128, :],
                                           ones[64:65, 0:64],
                                           rzb[64:65, qq * 512:(qq + 1) * 512],
                                           start=True, stop=True, tile_position=(64, 64),
                                           skip_group_check=True)
                    ct.inc(op)
                zb_done = ct.n

                @blk.vector
                def _(vector, zb_done=zb_done, qoff=qoff, h=h, qh=qh):
                    vector.wait_ge(ct.h, zb_done)
                    if h % 2 == 0 and A["apair_split"] is not None and qh == 0:
                        vector.wait_ge(cv.h, A["apair_split"])
                    if h % 2 == 1 and A["aout_free"] is not None:
                        vector.wait_ge(ld_mv.h, A["aout_free"])
                    for qq in range(2):
                        vector.tensor_copy(rzb[0:64, qq * 512:(qq + 1) * 512],
                                           ps_a2[qq][64:128, :])
                    vector.drain()
                    for qq in range(2):
                        if h % 2 == 0:
                            dsl = apair[0:64, qoff + qq * 512: qoff + (qq + 1) * 512]
                        else:
                            dsl = aout[0:64, qq * 512:(qq + 1) * 512]
                        op = vector.tensor_tensor(
                            out=dsl,
                            in0=t1[:, qq * 512:(qq + 1) * 512],
                            in1=rzb[0:64, qq * 512:(qq + 1) * 512], op=ALU.mult)
                    cv.inc(op)
                if h % 2 == 1:
                    halfdone = cv.n

                    @blk.gpsimd
                    def _(gpsimd, halfdone=halfdone, qoff=qoff):
                        gpsimd.wait_ge(cv.h, halfdone)
                        ld_mv.inc(gpsimd.dma_start(out=apair[64:128, qoff:qoff + 1024],
                                                   in_=aout[0:64, :]), 16)
                    A["aout_free"] = ld_mv.n
                return cv.n

            for h in range(n_heads):
                i = h % 2
                if h + 1 < n_heads:
                    load_head(h + 1)

                exp_done = []
                phi_done = []
                plo_done = []
                pva_done = []
                pvb_done = []
                pv_done = None
                for qh in range(2):
                    qoff = qh * 1024
                    run = cs.n
                    for kc in range(32):
                        run += 2 if (qh * 32 + kc) % 2 == 0 else 1
                        exp_done.append(run)
                    scalar_recs = []
                    for kc in range(32):
                        s_ = qh * 32 + kc
                        pbk = s_ % 2
                        pb4 = s_ % 4

                        @blk.tensor
                        def _(tensor, kc=kc, s_=s_, pbk=pbk, i=i, qoff=qoff, h=h):
                            if s_ == 0:
                                tensor.wait_ge(ld_hd[h % 2].h, head_ld[h])
                                if A["ps_s_free"] is not None:
                                    tensor.wait_ge(cv.h, A["ps_s_free"])
                            if s_ >= 2:
                                tensor.wait_ge(cs.h, exp_done[s_ - 2])
                            elif A["prev_exp_last"] is not None:
                                tensor.wait_ge(cs.h, A["prev_exp_last"])
                            for qq in range(2):
                                tensor.matmul(ps_s[pbk][:, qq * 512:(qq + 1) * 512],
                                              kst[i][:, kc * 128:(kc + 1) * 128],
                                              qab[i][:, qoff + qq * 512: qoff + (qq + 1) * 512],
                                              start=True, stop=False)
                            for qq in range(2):
                                op = tensor.matmul(ps_s[pbk][:, qq * 512:(qq + 1) * 512],
                                                   kst[i][:, kc * 128:(kc + 1) * 128],
                                                   qcd[i][:, qoff + qq * 512: qoff + (qq + 1) * 512],
                                                   start=False, stop=True)
                            ct.inc(op)
                        p32_wait = None
                        if s_ >= 2:
                            p32_wait = plo_done[s_ - 2]
                        elif A["prev_vhl_last"] is not None:
                            p32_wait = A["prev_vhl_last"]
                        phibuf_wait = None
                        if s_ >= 4:
                            phibuf_wait = pva_done[s_ - 4]
                        elif A["prev_pv_last"] is not None:
                            phibuf_wait = A["prev_pv_last"]
                        even = (s_ % 2 == 0)
                        scalar_recs.append((ct.n, p32_wait, pbk, pb4, even, phibuf_wait))

                        if even:
                            phi_done.append((cs.h, exp_done[s_] - 1))

                            @blk.vector
                            def _(vector, s_=s_, pb4=pb4, pbk=pbk):
                                vector.wait_ge(cs.h, exp_done[s_])
                                if s_ >= 4:
                                    vector.wait_ge(ct.h, pvb_done[s_ - 4])
                                cv.inc(vector.tensor_sub(plo[pb4][:], p32[pbk][:],
                                                         phi[pb4][:]))
                            plo_done.append((cv.h, cv.n))
                        else:
                            @blk.vector
                            def _(vector, s_=s_, pb4=pb4, pbk=pbk):
                                vector.wait_ge(cs.h, exp_done[s_])
                                if s_ >= 4:
                                    vector.wait_ge(ct.h, pva_done[s_ - 4])
                                elif A["prev_pv_last"] is not None:
                                    vector.wait_ge(ct.h, A["prev_pv_last"])
                                cv.inc(vector.tensor_copy(phi[pb4][:], p32[pbk][:]))
                            phi_done.append((cv.h, cv.n))

                            @blk.gpsimd
                            def _(gpsimd, s_=s_, pb4=pb4, pbk=pbk, pd=phi_done[-1]):
                                gpsimd.wait_ge(pd[0], pd[1])
                                if s_ >= 4:
                                    gpsimd.wait_ge(ct.h, pvb_done[s_ - 4])
                                cg.inc(gpsimd.tensor_sub(plo[pb4][:], p32[pbk][:],
                                                         phi[pb4][:]))
                            plo_done.append((cg.h, cg.n))

                        if kc >= 1:
                            emit_pva(h, i, s_ - 1, phi_done, pva_done)
                        if kc >= 3:
                            emit_pvb(h, i, s_ - 3, plo_done, pvb_done)
                        if kc == 4 and A.get("pend_fb"):
                            ph, pq, pt1 = A["pend_fb"]
                            A["pend_fb"] = None
                            finalize_b(ph, pq, pt1)

                    emit_pva(h, i, qh * 32 + 31, phi_done, pva_done)
                    for sx in (29, 30, 31):
                        emit_pvb(h, i, qh * 32 + sx, plo_done, pvb_done)

                    @blk.scalar
                    def _(scalar, recs=tuple(scalar_recs)):
                        for (ctv, p32w, pbk2, pb42, ev, phw) in recs:
                            scalar.wait_ge(ct.h, ctv)
                            if ev:
                                if phw is not None:
                                    scalar.wait_ge(ct.h, phw)
                                op = scalar.activation(out=phi[pb42][:], in_=ps_s[pbk2][:],
                                                       func=AF.Exp, scale=0.125,
                                                       bias=nshift[:])
                                op.then_inc(cs.h, 1)
                                cs.n += 1
                            if p32w is not None:
                                scalar.wait_ge(p32w[0], p32w[1])
                            op = scalar.activation(out=p32[pbk2][:], in_=ps_s[pbk2][:],
                                                   func=AF.Exp, scale=0.125, bias=nshift[:])
                            op.then_inc(cs.h, 1)
                            cs.n += 1
                    assert cs.n == exp_done[qh * 32 + 31], (cs.n, exp_done[qh * 32 + 31])
                    t1v = finalize_a(h, qh, pvb_done)
                    if h % 2 == 1 and qh == 1:
                        finalize_b(h, qh, t1v)
                    else:
                        A["pend_fb"] = (h, qh, t1v)

                scaled = A["o_free"]

                head_free[h] = (ct.h, pvb_done[-1])
                A["prev_exp_last"] = exp_done[-1]
                A["prev_vhl_last"] = plo_done[-1]
                A["prev_pv_last"] = pvb_done[-1]

                if h % 2 == 1:
                    mv_done = ld_mv.n
                    pair = h // 2

                    @blk.vector
                    def _(vector, mv_done=mv_done, pair=pair, h=h):
                        vector.wait_ge(ld_mv.h, mv_done)
                        if pair >= 1:
                            vector.wait_ge(ct.h, A["apair_free"])
                        q2 = (h // 2) % 2
                        vector.wait_ge(ld_pw[q2].h, ld_pw[q2].n)
                        vector.tensor_copy(pwh[q2][:], pw_sb[q2][:])
                        vector.tensor_copy(aph[:], apair[:])
                        vector.drain()
                        vector.tensor_sub(pwl[q2][:], pw_sb[q2][:], pwh[q2][:])
                        cv.inc(vector.tensor_sub(apl[:], apair[:], aph[:]))
                    ap_split = cv.n
                    pw_split_by_pair[pair] = cv.n

                    pr_mm = []
                    pr_ev = []
                    for idx in range(32):
                        dout, qq = divmod(idx, 4)
                        bank = ps_s[(idx % 4) // 2]
                        slc = (idx % 2) * 512

                        @blk.tensor
                        def _(tensor, idx=idx, dout=dout, qq=qq, bank=bank, slc=slc,
                              ap_split=ap_split, h=h):
                            if idx == 0:
                                tensor.wait_ge(cv.h, ap_split)
                                tensor.wait_ge(cv.h, pw_split_by_pair[h // 2])
                            if idx >= 4:
                                tensor.wait_ge(cv.h, pr_ev[idx - 4])
                            q2 = (h // 2) % 2
                            wsl = slice(dout * 128, (dout + 1) * 128)
                            asl = slice(qq * 512, (qq + 1) * 512)
                            tensor.matmul(bank[:, slc:slc + 512], pwh[q2][:, wsl], aph[:, asl],
                                          start=True, stop=False)
                            tensor.matmul(bank[:, slc:slc + 512], pwh[q2][:, wsl], apl[:, asl],
                                          start=False, stop=False)
                            ct.inc(tensor.matmul(bank[:, slc:slc + 512], pwl[q2][:, wsl],
                                                 aph[:, asl], start=False, stop=True))
                        pr_mm.append(ct.n)

                        @blk.vector
                        def _(vector, idx=idx, dout=dout, qq=qq, bank=bank, slc=slc, pair=pair):
                            vector.wait_ge(ct.h, pr_mm[idx])
                            dst2 = projacc[:, dout * NQ + qq * 512: dout * NQ + (qq + 1) * 512]
                            if pair == 0:
                                cv.inc(vector.tensor_copy(dst2, bank[:, slc:slc + 512]))
                            else:
                                cv.inc(vector.tensor_tensor(out=dst2, in0=bank[:, slc:slc + 512],
                                                            in1=dst2, op=ALU.add))
                        pr_ev.append(cv.n)

                    proj_ev_by_pair[pair] = pr_ev[-1]
                    A["proj_done"] = pr_ev[-1]
                    A["ps_s_free"] = pr_ev[-1]
                    A["apair_free"] = pr_mm[-1]
                    A["apair_split"] = ap_split

            if debug in ("attn", "attn1", "attnonly"):
                final_proj = A["proj_done"]

                @blk.gpsimd
                def _(gpsimd):
                    gpsimd.wait_ge(cv.h, final_proj)
                    for ci in range(DC):
                        gpsimd.dma_start(out=AP_s[ci],
                                         in_=projacc[:, ci * NQ:(ci + 1) * NQ]).then_inc(st.h, 16)
                    st.n += 16 * DC
                    gpsimd.wait_ge(st.h, st.n)

        if debug in ("attn", "attn1", "attnonly"):
            return nc

        with contextlib.ExitStack() as p:
            BFF = mybir.dt.bfloat16
            wsel = p.enter_context(nc.sbuf_tensor("wsel", [128, D], F32))
            w1h = p.enter_context(nc.sbuf_tensor("w1h", [128, DC * D], BFF))
            w1l = p.enter_context(nc.sbuf_tensor("w1l", [128, DC * D], BFF))
            w2h = p.enter_context(nc.sbuf_tensor("w2h", [128, DC * D], BFF))
            w2l = p.enter_context(nc.sbuf_tensor("w2l", [128, DC * D], BFF))
            hbuf = p.enter_context(nc.sbuf_tensor("hbuf", [128, 4 * 512], F32))
            hh = [p.enter_context(nc.sbuf_tensor(f"hh{i}", [128, DC * 512], BFF)) for i in range(2)]
            hl = [p.enter_context(nc.sbuf_tensor(f"hl{i}", [128, DC * 512], BFF)) for i in range(2)]
            wtile = [p.enter_context(nc.sbuf_tensor(f"wtile{i}", [128, D], F32)) for i in range(3)]
            xnh8 = p.enter_context(nc.sbuf_tensor("xnh8", [128, DC * 512], BFF))
            xnl8 = p.enter_context(nc.sbuf_tensor("xnl8", [128, DC * 512], BFF))
            btile = p.enter_context(nc.sbuf_tensor("btile", [128, E], F32))
            b2s = p.enter_context(nc.sbuf_tensor("b2s", [128, 2 * E], F32))
            pb_t = p.enter_context(nc.sbuf_tensor("pb_t", [128, DC], F32))
            ps_f = [p.enter_context(nc.psum_tensor(f"ps_f{i}", [128, 512], F32)) for i in range(DC)]

            blk = p.enter_context(nc.Block("ffn"))

            @blk.sync
            def _(sync):
                sync.dma_start(out=pb_t[:], in_=bproj.rearrange("(j p) -> p j", p=128),
                               allow_slow_non_contiguous=True).then_inc(ld_bt.h, 16)
            ld_bt.n += 16
            pb_ld = ld_bt.n

            for bi, (bsrc, bdst) in enumerate(((b1, b1sel), (b2, b2s))):
                for e in range(E):
                    @blk.sync
                    def _(sync, bsrc=bsrc, e=e):
                        sync.wait_ge(cv.h, cv.n)
                        sync.dma_start(out=btile[:],
                                       in_=bsrc[e].rearrange("(j p) -> p j", p=128),
                                       allow_slow_non_contiguous=True).then_inc(ld_bt.h, 16)
                    ld_bt.n += 16
                    bld = ld_bt.n

                    @blk.vector
                    def _(vector, bdst=bdst, e=e, bld=bld):
                        vector.wait_ge(ld_bt.h, bld)
                        for k, msk in ((0, m1b), (1, m2b)):
                            dst2 = bdst[:, k * E:(k + 1) * E]
                            if e == 0:
                                op = vector.tensor_scalar(out=dst2, in0=btile[:],
                                                          scalar1=msk[:, e:e + 1], scalar2=None,
                                                          op0=ALU.mult)
                            else:
                                op = vector.scalar_tensor_tensor(out=dst2, in0=btile[:],
                                                                 scalar=msk[:, e:e + 1], in1=dst2,
                                                                 op0=ALU.mult, op1=ALU.add)
                        vector.drain()
                        cv.inc(op)

            @blk.vector
            def _(vector, full=None):
                vector.wait_ge(ld_bt.h, ld_bt.n)
                vector.drain()
                vector.tensor_tensor(out=bias_all[:], in0=b2s[:, 0:E], in1=b2s[:, E:2 * E],
                                     op=ALU.add)
                vector.drain()
                cv.inc(vector.scalar_tensor_tensor(out=bias_all[:], in0=bias_all[:], scalar=0.5,
                                                   in1=pb_t[:], op0=ALU.mult, op1=ALU.add))

            wsel_marks = []
            sel_ci = {}
            wfree = {"w1": None, "w2": None}
            R = {"gelu_last": None,
                 "oev_last": None,
                 "omm_par": [None, None],
                 "hbuf_free": None,
                 "xn_free": None}

            def emit_h(k, t, w, my_xld):
                pg = R["gelu_last"]

                @blk.tensor
                def _(tensor, k=k, t=t, w=w, my_xld=my_xld, pg=pg):
                    tensor.wait_ge(ld_xn[0].h, my_xld)
                    if pg is not None:
                        tensor.wait_ge(cs.h, pg)
                    for ci in range(DC):
                        if t == 0 and w == 0:
                            tensor.wait_ge(cg.h, sel_ci[(k, "w1", ci)])
                        xsl = slice(ci * 512, (ci + 1) * 512)
                        for dout in range(4 * w, 4 * w + 4):
                            wsl = slice(ci * D + dout * 128, ci * D + (dout + 1) * 128)
                            tensor.matmul(ps_f[dout % 4][:], w1h[:, wsl], xnh8[:, xsl],
                                          start=(ci == 0), stop=False)
                            tensor.matmul(ps_f[dout % 4][:], w1h[:, wsl], xnl8[:, xsl],
                                          start=False, stop=False)
                            op = tensor.matmul(ps_f[dout % 4][:], w1l[:, wsl], xnh8[:, xsl],
                                               start=False, stop=(ci == DC - 1))
                    ct.inc(op)
                return ct.n

            def emit_gelu(k, t, w, hmmv):
                hhfree = R["omm_par"][t % 2]
                hbfree = R["hbuf_free"]

                @blk.scalar
                def _(scalar, k=k, t=t, w=w, hmmv=hmmv, hhfree=hhfree, hbfree=hbfree):
                    scalar.wait_ge(ct.h, hmmv)
                    if hhfree is not None:
                        scalar.wait_ge(ct.h, hhfree)
                    if hbfree is not None:
                        scalar.wait_ge(cv.h, hbfree)
                    for dj, dout in enumerate(range(4 * w, 4 * w + 4)):
                        b = b1sel[:, k * E + dout:k * E + dout + 1]
                        scalar.activation(out=hh[t % 2][:, dout * 512:(dout + 1) * 512],
                                          in_=ps_f[dout % 4][:], func=AF.Gelu, bias=b)
                        op = scalar.activation(out=hbuf[:, dj * 512:(dj + 1) * 512],
                                               in_=ps_f[dout % 4][:], func=AF.Gelu, bias=b)
                    cs.inc(op)
                R["gelu_last"] = cs.n
                return cs.n

            def emit_sub(k, t, w, gv):
                hlfree = R["omm_par"][t % 2]

                @blk.vector
                def _(vector, t=t, w=w, gv=gv, hlfree=hlfree):
                    vector.wait_ge(cs.h, gv)
                    if hlfree is not None:
                        vector.wait_ge(ct.h, hlfree)
                    for dj, dout in enumerate(range(4 * w, 4 * w + 4)):
                        op = vector.tensor_sub(hl[t % 2][:, dout * 512:(dout + 1) * 512],
                                               hbuf[:, dj * 512:(dj + 1) * 512],
                                               hh[t % 2][:, dout * 512:(dout + 1) * 512])
                    cv.inc(op)
                R["hbuf_free"] = cv.n
                return cv.n

            def emit_o(k, t, w, subv):
                oevfree = R["oev_last"]

                @blk.tensor
                def _(tensor, k=k, t=t, w=w, subv=subv, oevfree=oevfree):
                    tensor.wait_ge(cv.h, subv)
                    if oevfree is not None:
                        tensor.wait_ge(cv.h, oevfree)
                    for ci2 in range(DC):
                        if t == 0 and w == 0:
                            tensor.wait_ge(cg.h, sel_ci[(k, "w2", ci2)])
                        hsl = slice(ci2 * 512, (ci2 + 1) * 512)
                        for dout in range(4 * w, 4 * w + 4):
                            wsl = slice(ci2 * D + dout * 128, ci2 * D + (dout + 1) * 128)
                            tensor.matmul(ps_f[4 + dout % 4][:], w2h[:, wsl],
                                          hh[t % 2][:, hsl], start=(ci2 == 0), stop=False)
                            tensor.matmul(ps_f[4 + dout % 4][:], w2h[:, wsl],
                                          hl[t % 2][:, hsl], start=False, stop=False)
                            op = tensor.matmul(ps_f[4 + dout % 4][:], w2l[:, wsl],
                                               hh[t % 2][:, hsl], start=False,
                                               stop=(ci2 == DC - 1))
                    ct.inc(op)
                R["omm_par"][t % 2] = ct.n
                return ct.n

            def emit_oev(k, t, w, ommv):
                @blk.vector
                def _(vector, t=t, w=w, ommv=ommv):
                    vector.wait_ge(ct.h, ommv)
                    for dout in range(4 * w, 4 * w + 4):
                        dst2 = projacc[:, dout * NQ + t * 512: dout * NQ + (t + 1) * 512]
                        op = vector.scalar_tensor_tensor(out=dst2, in0=ps_f[4 + dout % 4][:],
                                                         scalar=0.5, in1=dst2,
                                                         op0=ALU.mult, op1=ALU.add)
                    cv.inc(op)
                R["oev_last"] = cv.n
                return cv.n

            g_hist = []
            for k in range(2):
                for wsrc, wdh, wdl, tag in ((W1, w1h, w1l, "w1"), (W2, w2h, w2l, "w2")):
                    for ci in range(DC):
                        par2 = len(g_hist) % 3
                        prev2 = g_hist[-3] if len(g_hist) >= 3 else None
                        first = not g_hist

                        @blk.gpsimd
                        def _(gpsimd, wsrc=wsrc, k=k, ci=ci, par2=par2, prev2=prev2,
                              first=first):
                            if first:
                                gpsimd.wait_ge(cv.h, idx_info["done"])
                            if prev2 is not None:
                                gpsimd.wait_ge(cg.h, prev2)
                            gpsimd.indirect_dma_start(
                                out=wtile[par2][:], out_offset=None,
                                in_=wsrc.rearrange("e a b -> (e a) b"),
                                in_offset=bass.IndirectOffsetOnAxis(
                                    idxtab[:, k * E + ci:k * E + ci + 1], 0)).then_inc(ld_wt[par2].h, 16)
                        ld_wt[par2].n += 16
                        gld = ld_wt[par2].n

                        @blk.vector
                        def _(vector, wdh=wdh, ci=ci, par2=par2, gld=gld, tag=tag):
                            vector.wait_ge(ld_wt[par2].h, gld)
                            if ci == 0 and wfree[tag] is not None:
                                vector.wait_ge(ct.h, wfree[tag])
                            cv.inc(vector.tensor_copy(wdh[:, ci * D:(ci + 1) * D],
                                                      wtile[par2][:]))
                        hi_done = cv.n

                        @blk.gpsimd
                        def _(gpsimd, ci=ci, hi_done=hi_done, wdh=wdh, wdl=wdl, par2=par2):
                            gpsimd.wait_ge(cv.h, hi_done)
                            cg.inc(gpsimd.tensor_sub(wdl[:, ci * D:(ci + 1) * D],
                                                     wtile[par2][:],
                                                     wdh[:, ci * D:(ci + 1) * D]))
                        sel_ci[(k, tag, ci)] = cg.n
                        g_hist.append(cg.n)

                hmmd = {}
                subd = {}
                ommd = {}
                for t in range(4):
                    xf = R["xn_free"]

                    @blk.scalar
                    def _(scalar, t=t, xf=xf):
                        if xf is not None:
                            scalar.wait_ge(ct.h, xf)
                        for ci in range(DC):
                            scalar.dma_start(out=xnh8[:, ci * 512:(ci + 1) * 512],
                                             in_=XNH_s[ci, :, t * 512:(t + 1) * 512]).then_inc(ld_xn[0].h, 16)
                            scalar.dma_start(out=xnl8[:, ci * 512:(ci + 1) * 512],
                                             in_=XNL_s[ci, :, t * 512:(t + 1) * 512]).then_inc(ld_xn[0].h, 16)
                    ld_xn[0].n += 256
                    my_xld = ld_xn[0].n

                    if t >= 2:
                        ommd[(t - 2, 0)] = emit_o(k, t - 2, 0, subd[(t - 2, 1)])
                    hmmd[(t, 0)] = emit_h(k, t, 0, my_xld)
                    if t >= 2:
                        emit_oev(k, t - 2, 0, ommd[(t - 2, 0)])
                        ommd[(t - 2, 1)] = emit_o(k, t - 2, 1, subd[(t - 2, 1)])
                    gv0 = emit_gelu(k, t, 0, hmmd[(t, 0)])
                    subd[(t, 0)] = emit_sub(k, t, 0, gv0)
                    hmmd[(t, 1)] = emit_h(k, t, 1, my_xld)
                    R["xn_free"] = hmmd[(t, 1)]
                    if t >= 2:
                        emit_oev(k, t - 2, 1, ommd[(t - 2, 1)])
                    gv1 = emit_gelu(k, t, 1, hmmd[(t, 1)])
                    subd[(t, 1)] = emit_sub(k, t, 1, gv1)
                for tt in (2, 3):
                    ommd[(tt, 0)] = emit_o(k, tt, 0, subd[(tt, 1)])
                    emit_oev(k, tt, 0, ommd[(tt, 0)])
                    ommd[(tt, 1)] = emit_o(k, tt, 1, subd[(tt, 1)])
                    emit_oev(k, tt, 1, ommd[(tt, 1)])
                wfree["w1"] = hmmd[(3, 1)]
                wfree["w2"] = ommd[(3, 1)]

        with contextlib.ExitStack() as p:
            xqb = [p.enter_context(nc.sbuf_tensor(f"xqb{i}", [128, D], F32)) for i in range(2)]
            outb = [p.enter_context(nc.sbuf_tensor(f"outb{i}", [128, D], F32)) for i in range(2)]
            ps_c = [p.enter_context(nc.psum_tensor(f"ps_c{i}", [128, 512], F32)) for i in range(2)]

            blk = p.enter_context(nc.Block("fin"))

            @blk.vector
            def _(vector):
                for dout in range(DC):
                    op = vector.tensor_scalar(out=projacc[:, dout * NQ:(dout + 1) * NQ],
                                              in0=projacc[:, dout * NQ:(dout + 1) * NQ],
                                              scalar1=bias_all[:, dout:dout + 1], scalar2=None,
                                              op0=ALU.add)
                cv.inc(op)
            bias_done = cv.n

            ev_marks = []
            st_marks = []
            for tc in range(16):
                i = tc % 2

                @blk.sync
                def _(sync, tc=tc, i=i):
                    if tc >= 2:
                        sync.wait_ge(cv.h, ev_marks[tc - 2])
                    sync.dma_start(out=xqb[i][:],
                                   in_=x_q[tc * 128:(tc + 1) * 128, :]).then_inc(ld_xq[i].h, 16)
                ld_xq[i].n += 16
                xld = ld_xq[i].n

                @blk.tensor
                def _(tensor, tc=tc, bias_done=bias_done):
                    if tc == 0:
                        tensor.wait_ge(cv.h, bias_done)
                    if tc >= 1:
                        tensor.wait_ge(cv.h, ev_marks[tc - 1])
                    for half in range(2):
                        for q in range(4):
                            ci = half * 4 + q
                            op = tensor.transpose(ps_c[half][:, q * 128:(q + 1) * 128],
                                                  projacc[:, ci * NQ + tc * 128: ci * NQ + (tc + 1) * 128],
                                                  ident[:])
                    ct.inc(op)
                tdn = ct.n

                @blk.vector
                def _(vector, tc=tc, i=i, xld=xld, tdn=tdn):
                    vector.wait_ge(ct.h, tdn)
                    vector.wait_ge(ld_xq[i].h, xld)
                    if tc >= 2:
                        vector.wait_ge(st_o[i].h, st_marks[tc - 2])
                    for half in range(2):
                        op = vector.tensor_tensor(out=outb[i][:, half * 512:(half + 1) * 512],
                                                  in0=ps_c[half][:],
                                                  in1=xqb[i][:, half * 512:(half + 1) * 512],
                                                  op=ALU.add)
                    cv.inc(op)
                ev_marks.append(cv.n)

                @blk.gpsimd
                def _(gpsimd, tc=tc, i=i, evm=ev_marks[-1]):
                    gpsimd.wait_ge(cv.h, evm)
                    gpsimd.dma_start(out=out[tc * 128:(tc + 1) * 128, :],
                                     in_=outb[i][:]).then_inc(st_o[i].h, 16)
                st_o[i].n += 16
                st_marks.append(st_o[i].n)

            @blk.gpsimd
            def _(gpsimd):
                gpsimd.wait_ge(st_o[0].h, st_o[0].n)
                gpsimd.wait_ge(st_o[1].h, st_o[1].n)

    return nc


_NC_CACHE = {}


def kernel(L0_attn_x, L0_attn_qkv_w, L0_attn_qkv_b, L0_attn_proj_w, L0_attn_proj_b,
           L0_ffn_Wr, L0_ffn_br, L0_ffn_W1, L0_ffn_b1, L0_ffn_W2, L0_ffn_b2):
    x = np.ascontiguousarray(np.asarray(L0_attn_x, dtype=np.float32))
    qkvT = np.ascontiguousarray(np.asarray(L0_attn_qkv_w, dtype=np.float32).T).reshape(DC, 128, QKVD)
    projT = np.ascontiguousarray(np.asarray(L0_attn_proj_w, dtype=np.float32).T).reshape(DC, 128, D)
    wr_t = np.ascontiguousarray(np.asarray(L0_ffn_Wr, dtype=np.float32).reshape(DC, 128, E).transpose(1, 0, 2).reshape(128, DC * E))
    bqkv = np.ascontiguousarray(np.asarray(L0_attn_qkv_b, dtype=np.float32))
    bproj = np.ascontiguousarray(np.asarray(L0_attn_proj_b, dtype=np.float32))
    br = np.ascontiguousarray(np.asarray(L0_ffn_br, dtype=np.float32))
    W1 = np.ascontiguousarray(np.asarray(L0_ffn_W1, dtype=np.float32))
    W2 = np.ascontiguousarray(np.asarray(L0_ffn_W2, dtype=np.float32))
    b1 = np.ascontiguousarray(np.asarray(L0_ffn_b1, dtype=np.float32))
    b2 = np.ascontiguousarray(np.asarray(L0_ffn_b2, dtype=np.float32))

    if "nc" not in _NC_CACHE:
        _NC_CACHE["nc"] = build_nc()
    nc = _NC_CACHE["nc"]

    in_maps = []
    for c in range(8):
        b, qh = divmod(c, 2)
        xp = np.ascontiguousarray(
            np.concatenate([x[b, qh * NQ:(qh + 1) * NQ],
                            x[b, (1 - qh) * NQ:(2 - qh) * NQ]], axis=0))
        in_maps.append({
            "x_kv": xp,
            "x_q": np.ascontiguousarray(x[b, qh * NQ:(qh + 1) * NQ]),
            "rsel": np.array([[1.0 - qh, float(qh)]], dtype=np.float32),
            "wqkv": qkvT, "bqkv": bqkv, "wproj": projT, "bproj": bproj,
            "wr": wr_t, "br": br, "W1": W1, "W2": W2, "b1": b1, "b2": b2,
        })
    res = run_bass_kernel_spmd(nc, in_maps, core_ids=list(range(8)))
    outp = np.empty((B, NTOK, D), dtype=np.float32)
    for c in range(8):
        b, qh = divmod(c, 2)
        outp[b, qh * NQ:(qh + 1) * NQ] = res.results[c]["out"]
    return outp
